# version 12
# baseline (speedup 1.0000x reference)
import numpy as np
from contextlib import ExitStack

import concourse.bass as bass
import concourse.mybir as mybir
from concourse.bass_utils import run_bass_kernel_spmd

F32 = mybir.dt.float32
BF16 = mybir.dt.bfloat16
AX = mybir.AxisListType.X
OP = mybir.AluOpType
AF = mybir.ActivationFunctionType

B, S, D, H, DH = 4, 4096, 1024, 16, 64
NMC = D // 128
NKT = S // 128
NQG = 4
OWN = 2048
EPS = 1e-5
BIGMASK = 1.0e5

ENGS = ("sync", "gpsimd", "tensor", "vector", "scalar")
CENGS = ("tensor", "vector", "scalar")


class Em:

    def __init__(self):
        self.items = {e: [] for e in ENGS}
        self.cnt = {e: 0 for e in CENGS}
        self.kcnt = {}

    def op(self, eng, fn, waits=(), key=None):
        for w in waits:
            if w is not None:
                top = self.cnt.get(w[0], None)
                if top is None:
                    top = self.kcnt[w[0]]
                assert w[1] <= top, (eng, w)
        if eng in CENGS:
            assert key is None
            self.cnt[eng] += 1
            tok = (eng, self.cnt[eng])
            self.items[eng].append((fn, [w for w in waits if w is not None], 1, eng))
        else:
            assert key is not None
            self.kcnt[key] = self.kcnt.get(key, 0) + 16
            tok = (key, self.kcnt[key])
            self.items[eng].append((fn, [w for w in waits if w is not None], 16, key))
        return tok

    def wait_only(self, eng, waits):
        self.items[eng].append((None, [w for w in waits if w is not None], 0, None))

    def all_keys(self):
        return [(k, v) for k, v in self.kcnt.items()]


def emit(em, block, sems):
    def runner(ename):
        def run(engine):
            last = {}
            cnt = 0
            for fn, waits, inc, key in em.items[ename]:
                for (we, wv) in waits:
                    if we == ename and fn is not None:
                        if ename in ("vector", "scalar") and cnt - wv <= 1 \
                                and wv > last.get(we, 0):
                            engine.wait_ge(sems[we], wv)
                            last[we] = wv
                        continue
                    if wv > last.get(we, 0) or fn is None:
                        engine.wait_ge(sems[we], wv)
                        last[we] = max(last.get(we, 0), wv)
                if fn is not None:
                    ins = fn(engine)
                    ins.then_inc(sems[key], inc)
                    if key == ename:
                        cnt += inc
        return run

    block.sync(runner("sync"))
    block.gpsimd(runner("gpsimd"))
    block.tensor(runner("tensor"))
    block.vector(runner("vector"))
    block.scalar(runner("scalar"))


def build_program(stage="all", taps=()):
    nc = bass.Bass("TRN2", target_bir_lowering=False, debug=False)

    dt_in = lambda n, sh, dt=F32: nc.dram_tensor(n, sh, dt, kind="ExternalInput").ap()
    xkv = dt_in("xkv", [S, D])
    wqT = dt_in("wqT", [128, 8192])
    wkT = dt_in("wkT", [128, 8192])
    wv4 = dt_in("wv4", [128, 8192])
    woT = dt_in("woT", [128, 8192])
    bqk = dt_in("bqk", [128, 16])
    bvr = dt_in("bvr", [1, 1024])
    bor = dt_in("bor", [1, 1024])
    wrR = dt_in("wrR", [128, 64])
    rbr = dt_in("rbr", [1, 8])
    wgE = dt_in("wgE", [10, 128, 4096])
    wuE = dt_in("wuE", [10, 128, 4096])
    wdE = dt_in("wdE", [10, 128, 4096])
    kmask = dt_in("kmask", [4, S])
    qind = dt_in("qind", [4, OWN])
    diag = dt_in("diag", [128, 2048])
    ident = dt_in("ident", [128, 128])
    out_hbm = nc.dram_tensor("out", [OWN, D], F32, kind="ExternalOutput").ap()
    tap_hbm = {}
    for tname, tshape, tdt in taps:
        tap_hbm[tname] = nc.dram_tensor("tap_" + tname, tshape, tdt,
                                        kind="ExternalOutput").ap()

    em = Em()
    T = lambda fn, waits=(): em.op("tensor", fn, waits)
    V = lambda fn, waits=(): em.op("vector", fn, waits)
    C = lambda fn, waits=(): em.op("scalar", fn, waits)
    LD = lambda fn, waits=(), key=None: em.op("sync", fn, waits, key=key)
    WL = lambda fn, waits=(), key=None: em.op("gpsimd", fn, waits, key=key)

    ctx = ExitStack()
    sb = lambda n, sh, dt=BF16: ctx.enter_context(nc.sbuf_tensor(n, sh, dt))
    psd = lambda n, sh, dt=F32: ctx.enter_context(nc.psum_tensor(n, sh, dt))

    def make_sems():
        sems = {e: ctx.enter_context(nc.semaphore("sem_" + e)) for e in CENGS}
        for k in em.kcnt:
            sems[k] = ctx.enter_context(nc.semaphore("sem_" + k))
        return sems

    xnT = sb("xnT", [128, NMC * S])
    acc = sb("acc", [128, 16 * D])
    id_bf = sb("id_bf", [128, 128])
    id_f32 = sb("id_f32", [128, 128], F32)
    cones = sb("cones", [9, 256])
    bqk_sb = sb("bqk_sb", [128, 16], F32)
    bv_bf = sb("bv_bf", [9, 1088])
    bo_bf = sb("bo_bf", [9, 1088])
    rb_bf = sb("rb_bf", [9, 72])
    wr_bf = sb("wr_bf", [128, 128])
    epsc = sb("epsc", [128, 1], F32)

    psA = psd("psA", [128, 512])
    psB = psd("psB", [128, 512])
    psB2 = psd("psB2", [128, 512])
    psZ = psd("psZ", [128, 512])
    psV = psd("psV", [128, 512])
    psO = psd("psO", [128, 512])
    psT1 = psd("psT1", [128, 128])
    psT2 = psd("psT2", [128, 512], BF16)

    WL(lambda g: g.dma_start(out=id_bf[:], in_=ident), key="g_const")
    LD(lambda s: s.dma_start(out=id_f32[:], in_=ident), key="s_const")
    LD(lambda s: s.dma_start(out=bqk_sb[:], in_=bqk), key="s_const")
    WL(lambda g: g.dma_start(out=bv_bf[0:1, 0:1024], in_=bvr), key="g_const")
    WL(lambda g: g.dma_start(out=bo_bf[0:1, 0:1024], in_=bor), key="g_const")
    WL(lambda g: g.dma_start(out=rb_bf[0:1, 0:8], in_=rbr), key="g_const")
    WL(lambda g: g.dma_start(out=wr_bf[:, 0:64], in_=wrR), key="g_const")
    onesf = sb("onesf", [128, 1], F32)
    t_ones = V(lambda v: v.memset(cones[0:1, 0:256], 1.0))
    t_eps = V(lambda v: v.memset(epsc[:], EPS))
    t_onesf = V(lambda v: v.memset(onesf[:], 1.0))
    t_gconst = ("g_const", em.kcnt["g_const"])
    t_sconst = ("s_const", em.kcnt["s_const"])
    t_id = t_idf = t_bqk = t_bv = t_bo = t_rb = t_wr = None

    acc_tok = [[None, None] for _ in range(16)]

    with ExitStack() as actx:
        sba = lambda n, sh, dt=BF16: actx.enter_context(nc.sbuf_tensor(n, sh, dt))
        nmean = sba("nmean", [128, 2], F32)
        nmean1 = sba("nmean1", [128, 2], F32)
        varn = sba("varn", [128, 2], F32)
        sq_s = sba("sq_s", [128, 2], F32)
        rstd = sba("rstd", [128, 2], F32)
        diag_bf = sba("diag_bf", [128, 2048])
        wq_sb = sba("wq_sb", [128, 2 * 1024])
        wk_sb = sba("wk_sb", [128, 2 * 1024])
        wv_sb = sba("wv_sb", [128, 2048])
        wo_sb = sba("wo_sb", [128, 2112])
        V4 = sba("V4", [128, NKT * 260])
        KtP = [[sba("Kt%d%d" % (par, hh), [69, S]) for hh in range(2)]
               for par in range(2)]
        QtP = [[sba("Qt%d%d" % (par, hh), [69, OWN]) for hh in range(2)]
               for par in range(2)]
        pstage = sba("pstage", [128, 2 * 512])
        onescol = sba("onescol", [128, 8])
        kmx_c = sba("kmx_c", [128, 1], F32)
        kmax_sb = sba("kmax_sb", [9, 8])
        kb_sb = sba("kb_sb", [128, 2], F32)
        negc = sba("negc", [128, 32], F32)
        zguard = sba("zguard", [128, 1], F32)
        zt_sb = sba("zt_sb", [128, 64])
        rzc = sba("rzc", [128, 1], F32)
        pctx = ExitStack()
        sbp = lambda n, sh, dt=BF16: pctx.enter_context(nc.sbuf_tensor(n, sh, dt))
        xst = sbp("xst", [128, 2 * D], F32)
        lntr = sbp("lntr", [128, D], F32)
        xnt_st = sbp("xnt_st", [128, 2 * D])

        WL(lambda g: g.dma_start(out=diag_bf[:], in_=diag), key="g_mask")

        t_onescol = V(lambda v: v.memset(onescol[:, 0:8], 1.0))
        ktc = []
        for par in range(2):
            for hh in range(2):
                ktc.append(V(lambda v, par=par, hh=hh: v.memset(
                    KtP[par][hh][64:65, :], 1.0)))
                WL(lambda g, par=par, hh=hh: g.dma_start(
                    out=KtP[par][hh][65:69, :], in_=kmask), key="g_mask")
                WL(lambda g, par=par, hh=hh: g.dma_start(
                    out=QtP[par][hh][65:69, :], in_=qind), key="g_mask")
        t_gmask = ("g_mask", em.kcnt["g_mask"])
        qtc = [t_gmask]

        wq_free = [None, None]
        wk_free = [None, None]
        wv_free = [None, None]
        wo_free = [None, None]
        wq_ld = [None] * 8
        wk_ld = [None] * 8
        wv_ld = [None] * 4
        wo_ld = [None] * 4
        kq_free = [None, None]
        v4_free = None
        psS_free = [None, None]
        psA_free = None
        psZ_free = None
        psV_free = None
        psO_free = None
        psT1_free = None
        kmx_free = None
        kmaxsb_free = None
        kb_free = None
        negc_free = None
        zg_free = None
        zrot = 0
        pend_dance = None
        zdma_last = None
        mst_free = []
        ps2b_free = None
        pst_free = [None, None]
        probs_free = [None] * 4
        prep_tok = {}
        zev_tok = {}
        zdma_tok = {}

        zraw_free = None
        zt_free = None
        zall_free_all = None
        zst_free = [None, None]
        vdir_last = [None] * 4

        def load_wqk(p):
            slp = p % 2
            wq_ld[p] = WL(lambda g, p=p, slp=slp: g.dma_start(
                out=wq_sb[:, slp * 1024:(slp + 1) * 1024],
                in_=wqT[:, p * 1024:(p + 1) * 1024]), waits=[wq_free[slp]],
                key="g_wq%d" % slp)
            wk_ld[p] = WL(lambda g, p=p, slp=slp: g.dma_start(
                out=wk_sb[:, slp * 1024:(slp + 1) * 1024],
                in_=wkT[:, p * 1024:(p + 1) * 1024]), waits=[wk_free[slp]],
                key="g_wk%d" % slp)

        def load_wv(grp):
            wv_ld[grp] = WL(lambda g, grp=grp: g.dma_start(
                out=wv_sb[:],
                in_=wv4[:, grp * 2048:(grp + 1) * 2048]), waits=[wv_free[0]],
                key="g_wv")

        def load_wo(grp):
            wo_ld[grp] = WL(lambda g, grp=grp: g.dma_start(
                out=wo_sb[0:128, 0:2048],
                in_=woT[:, grp * 2048:(grp + 1) * 2048]), waits=[wo_free[0]],
                key="g_wo")

        load_wqk(0)
        load_wqk(1)
        load_wv(0)
        load_wo(0)

        vm0 = V(lambda v: v.memset(V4[:], 1.0))
        vdir_evac0 = None

        xst_free = [None, None]
        xnst_free = [None, None]
        ps2_free = None
        prev_sq = [None, None]
        xn_ready = [None] * NKT
        stats_tok = [None] * NKT
        vdir_evac0 = None

        def ln_tail(t):
            nonlocal ps2_free, psV_free, psO_free, vdir_evac0
            sl = t % 2
            x_sl = xst[:, sl * D:(sl + 1) * D]
            xn_sl = xnt_st[:, sl * D:(sl + 1) * D]
            t_sq, t_sqrt = stats_tok[t]
            t_rcp = V(lambda v, sl=sl: v.reciprocal(rstd[:, sl:sl + 1],
                                                    sq_s[:, sl:sl + 1]),
                      waits=[t_sqrt])
            t_app = V(lambda v, sl=sl, x_sl=x_sl, xn_sl=xn_sl: v.tensor_scalar(
                xn_sl, x_sl, nmean1[:, sl:sl + 1], rstd[:, sl:sl + 1],
                OP.add, OP.mult),
                waits=[xnst_free[sl], t_sq, t_rcp])
            xst_free[sl] = t_app
            evl = None
            for half in range(2):
                tps = []
                for b4 in range(4):
                    mc = half * 4 + b4
                    tps.append(T(lambda tt, mc=mc, b4=b4, xn_sl=xn_sl: tt.transpose(
                        psT2[:, b4 * 128:(b4 + 1) * 128],
                        xn_sl[:, mc * 128:(mc + 1) * 128], id_bf[:]),
                        waits=[t_app, t_gconst, ps2_free]))

                def ev(v, half=half, t=t):
                    o = xnT[:].rearrange("p (mc s) -> p mc s", s=S)[
                        :, half * 4:half * 4 + 4, t * 128:(t + 1) * 128]
                    i = psT2[:].rearrange("p (b s) -> p b s", s=128)
                    return v.tensor_copy(o, i)
                evl = V(ev, waits=tps)
                ps2_free = evl
            xnst_free[sl] = evl
            xn_ready[t] = evl
            kt = t
            psX = psV if kt % 2 == 0 else psO
            psX_f = psV_free if kt % 2 == 0 else psO_free
            for mc in range(NMC):
                T(lambda tt, kt=kt, mc=mc, psX=psX: tt.matmul(
                    psX[:, 0:256],
                    xnT[:, mc * S + kt * 128:mc * S + (kt + 1) * 128],
                    wv_sb[:, mc * 256:(mc + 1) * 256],
                    start=(mc == 0), stop=False),
                    waits=[xn_ready[kt], wv_ld[0], psX_f])
            trk = T(lambda tt, psX=psX: tt.matmul(
                psX[:, 0:256], cones[0:1, 0:128], bv_bf[0:1, 0:256],
                start=False, stop=True), waits=[t_gconst, t_ones])
            tev = V(lambda v, kt=kt, psX=psX: v.tensor_copy(
                V4[:, kt * 260:(kt + 1) * 260].rearrange(
                    "p (h d) -> p h d", d=65)[:, :, 0:64],
                psX[:, 0:256].rearrange("p (h d) -> p h d", d=64)),
                waits=[trk, vm0])
            if kt % 2 == 0:
                psV_free = tev
            else:
                psO_free = tev
            vdir_evac0 = tev

        for t in range(NKT):
            sl = t % 2
            x_sl = xst[:, sl * D:(sl + 1) * D]
            t_load = LD(lambda s, t=t, x_sl=x_sl: s.dma_start(
                out=x_sl, in_=xkv[t * 128:(t + 1) * 128, :]), waits=[xst_free[sl]],
                key="s_x%d" % sl)
            t_red = V(lambda v, sl=sl, x_sl=x_sl: v.reduce_sum(
                nmean[:, sl:sl + 1], x_sl, AX), waits=[t_load])
            t_nm = V(lambda v, sl=sl: v.tensor_single_scalar(
                nmean1[:, sl:sl + 1], nmean[:, sl:sl + 1], -1.0 / D, OP.mult),
                waits=[prev_sq[sl], t_red])
            t_sq = C(lambda sc, sl=sl, x_sl=x_sl: sc.activation(
                lntr[:], x_sl, AF.Square, bias=nmean1[:, sl:sl + 1], scale=1.0,
                accum_out=varn[:, sl:sl + 1]), waits=[t_nm])
            prev_sq[sl] = t_sq
            t_sqrt = C(lambda sc, sl=sl: sc.activation(
                sq_s[:, sl:sl + 1], varn[:, sl:sl + 1], AF.Sqrt,
                bias=epsc[:, 0:1], scale=1.0 / D), waits=[t_eps, t_sq])
            stats_tok[t] = (t_sq, t_sqrt)
            if t >= 1:
                ln_tail(t - 1)
        ln_tail(NKT - 1)
        xn_all = xn_ready[NKT - 1]
        vdir_last[0] = vdir_evac0
        ps2b_free = ps2_free
        pctx.close()
        probs = sba("probs", [128, 4 * 512])
        mstage = sba("mstage", [32, 128], F32)
        zraw = sba("zraw", [65, 512], F32)
        zallT = [sba("zallT%d" % i, [128, OWN]) for i in range(2)]
        zst = sba("zst", [69, 2 * 512])
        sqstage = sba("sqstage", [128, 1024])
        t_sqz = V(lambda v: v.memset(sqstage[64:128, :], 0.0), waits=[xn_all])

        def build_prep(p):
            par = p % 2
            st = {}
            tasks = []

            def proj_mm_unit(kind, kc, u):
                def f():
                    last = None
                    for mc in (2 * u, 2 * u + 1):
                        if kind == "k":
                            lhs, rhs_col = wk_sb, kc * 512
                            xw, wld = xn_ready[kc * 4 + 3], wk_ld[p]
                        else:
                            lhs, rhs_col = wq_sb, (2 * kc + 1) * 512
                            xw, wld = xn_all, wq_ld[p]
                        last = T(lambda tt, mc=mc, lhs=lhs, rhs_col=rhs_col:
                                 tt.matmul(
                            psV[:],
                            lhs[:, par * 1024 + mc * 128:par * 1024 + (mc + 1) * 128],
                            xnT[:, mc * S + rhs_col:mc * S + rhs_col + 512],
                            start=(mc == 0), stop=(mc == NMC - 1)),
                            waits=[xw, wld, psV_free])
                    st[("mm", kind, kc)] = last
                return f

            def proj_evac(kind, kc, ci):
                def f():
                    nonlocal psV_free
                    sps = ci % 2
                    cc = 2 * p + (1 if kind == "k" else 0)
                    dA = KtP[par][0] if kind == "k" else QtP[par][0]
                    dB = KtP[par][1] if kind == "k" else QtP[par][1]
                    col = kc * 512
                    mm = st[("mm", kind, kc)]
                    te0 = V(lambda v: v.tensor_scalar(
                        dA[0:64, col:col + 512], psV[0:64, :],
                        bqk_sb[0:64, cc:cc + 1], onesf[0:64, 0:1],
                        OP.add, OP.mult),
                        waits=[mm, t_sconst, t_onesf, t_gmask, kq_free[par]]
                        + ktc)
                    te1 = V(lambda v: v.tensor_scalar(
                        pstage[64:128, sps * 512:(sps + 1) * 512], psV[64:128, :],
                        bqk_sb[64:128, cc:cc + 1], onesf[64:128, 0:1],
                        OP.add, OP.mult),
                        waits=[pst_free[sps]])
                    psV_free = te1
                    tsh = LD(lambda s: s.dma_start(
                        out=dB[0:64, col:col + 512],
                        in_=pstage[64:128, sps * 512:(sps + 1) * 512]),
                        waits=[te1, kq_free[par]], key="s_kq%d" % sps)
                    pst_free[sps] = tsh
                    st["evac_last"] = te1
                return f

            def t_bar(final):
                def f():
                    st["sync"] = [("s_kq0", em.kcnt.get("s_kq0", 0)),
                                  ("s_kq1", em.kcnt.get("s_kq1", 0))]
                    st["scal"] = st["evac_last"]
                    if final:
                        prep_tok[p]["kq_scal"] = st["evac_last"]
                        prep_tok[p]["kq_sync"] = list(st["sync"])
                        if p + 2 < 8:
                            wq_free[par] = st[("mm", "q", 3)]
                            wk_free[par] = st[("mm", "q", 3)]
                            load_wqk(p + 2)
                return f

            def t_square(src, soff):
                def f():
                    st["sq"] = V(lambda v: v.tensor_mul(
                        sqstage[0:64, 0:1024], src[0:64, soff:soff + 1024],
                        src[0:64, soff:soff + 1024]),
                        waits=[t_sqz, st.get("mm_last"), st["scal"]]
                        + st["sync"])
                return f

            def t_normmm(col0, u):
                def f():
                    for c8 in range(4 * u, 4 * u + 4):
                        st["mm_last"] = T(lambda tt, c8=c8: tt.matmul(
                            psA[:, col0 + c8:col0 + c8 + 1],
                            sqstage[0:128, c8 * 128:(c8 + 1) * 128],
                            onescol[0:128, 0:1], start=True, stop=True),
                            waits=[st["sq"], psA_free, t_onescol])
                return f

            def norm_job(kind, hh, half):
                if kind == "k":
                    src, col0 = KtP[par][hh], hh * 32 + half * 8
                else:
                    src, col0 = QtP[par][hh], 64 + hh * 16 + half * 8
                return [t_square(src, half * 1024),
                        t_normmm(col0, 0), t_normmm(col0, 1)]

            def t_kmax():
                nonlocal psT1_free, kmx_free
                for hh in range(2):
                    tr1 = V(lambda v, hh=hh: v.reduce_max(
                        kmx_c[:, 0:1], psA[:, hh * 32:(hh + 1) * 32], AX),
                        waits=[st["mm_last"], kmx_free, kmaxsb_free])
                    ttp = T(lambda tt: tt.transpose(
                        psT1[0:1, 0:128], kmx_c[:, 0:1], id_f32[:]),
                        waits=[tr1, psT1_free, t_sconst])
                    kmx_free = ttp
                    tr2 = V(lambda v, hh=hh: v.reduce_max(
                        kmax_sb[0:1, hh:hh + 1], psT1[0:1, 0:128], AX),
                        waits=[ttp])
                    psT1_free = tr2
                    st["km%d" % hh] = tr2

            def t_bcast():
                nonlocal psT1_free, psA_free, kmaxsb_free, kb_free
                tkb = T(lambda tt: tt.matmul(
                    psT1[0:128, 0:2], cones[0:1, 0:128], kmax_sb[0:1, 0:2],
                    start=True, stop=True),
                    waits=[st["km0"], st["km1"], psT1_free, t_ones])
                kmaxsb_free = tkb
                tkbe = V(lambda v: v.tensor_copy(kb_sb[:, 0:2], psT1[0:128, 0:2]),
                         waits=[tkb, kb_free])
                psT1_free = tkbe
                tpr = V(lambda v: v.tensor_tensor(
                    negc[:].rearrange("p (h q) -> p h q", q=16),
                    psA[:, 64:96].rearrange("p (h q) -> p h q", q=16),
                    kb_sb[:, 0:2].unsqueeze(2).broadcast_to([128, 2, 16]),
                    OP.mult),
                    waits=[tkbe, st["mm_last"], negc_free])
                kb_free = tpr
                psA_free = tpr
                tnc = C(lambda sc: sc.activation(negc[:], negc[:], AF.Sqrt),
                        waits=[tpr])
                st["negc"] = V(lambda v: v.tensor_single_scalar(
                    negc[:], negc[:], -1.0, OP.mult), waits=[tnc])

            def t_mrow():
                nonlocal psT1_free, negc_free, mst_free
                ttr = T(lambda tt: tt.transpose(
                    psT1[0:32, 0:128], negc[:, 0:32], id_f32[:]),
                    waits=[st["negc"], psT1_free, t_sconst])
                negc_free = ttr
                tcp = V(lambda v: v.tensor_copy(mstage[0:32, 0:128],
                                                psT1[0:32, 0:128]),
                        waits=[ttr] + mst_free)
                psT1_free = tcp
                tdmd = {}
                for g in range(NQG):
                    for hh in range(2):
                        QTh = QtP[par][hh]
                        tdmd[(g, hh)] = WL(lambda g_, g=g, hh=hh, QTh=QTh:
                                           g_.dma_start(
                            out=QTh[64:65, g * 512:(g + 1) * 512],
                            in_=mstage[hh * 16 + g * 4:hh * 16 + g * 4 + 4, 0:128]),
                            waits=[tcp, kq_free[par]],
                            key="g_mrow%d" % (g * 2 + hh))
                mst_free = list(tdmd.values())
                prep_tok[p]["tdm"] = tdmd

            def chunk(kind, kc, ci):
                return [proj_mm_unit(kind, kc, u) for u in range(4)] \
                    + [proj_evac(kind, kc, ci)]

            def zipped(projs, norms):
                out, pi, ni = [], 0, 0
                while pi < len(projs) or ni < len(norms):
                    if pi < len(projs):
                        out.append(projs[pi]); pi += 1
                    if ni < len(norms):
                        out.append(norms[ni]); ni += 1
                return out

            pchunks = [chunk("k", kc, kc) for kc in range(8)] \
                + [chunk("q", g, 8 + g) for g in range(NQG)]
            tasks += pchunks[0] + pchunks[1] + [t_bar(False)]
            for half in range(4):
                nj = norm_job("k", 0, half) + norm_job("k", 1, half)
                tasks += zipped(pchunks[2 * half + 2] + pchunks[2 * half + 3], nj)
                tasks.append(t_bar(half == 3 and False))
            nj = norm_job("q", 0, 0) + norm_job("q", 1, 0)
            tasks += zipped(pchunks[10] + pchunks[11], nj)
            tasks.append(t_bar(True))
            tasks += norm_job("q", 0, 1) + norm_job("q", 1, 1)
            tasks.append(t_kmax)
            tasks.append(t_bcast)
            tasks.append(t_mrow)
            prep_tok[p] = {}
            return tasks

        for _f in build_prep(0):
            _f()
        _stop = {"p0": 0, "kv0": 0.2, "pa0": 0.4, "pb0": 0.6, "pb1": 0.7,
                 "pr1": 0.8, "op0": 0.9, "opnb": 0.95,
                 "attn0": 1, "attn": 2, "router": 3, "all": 9}[stage]

        def _finish(local_tensors):
            cw = [(e, em.cnt[e]) for e in CENGS]
            for tname, tshape, tdt in taps:
                if tname in local_tensors:
                    tsrc = local_tensors[tname]
                    LD(lambda s_, tsrc=tsrc, tname=tname: s_.dma_start(
                        out=tap_hbm[tname], in_=tsrc), waits=cw, key="s_tap")
            em.wait_only("sync", em.all_keys() + [(e, em.cnt[e]) for e in CENGS])
            sems = make_sems()
            with nc.Block() as block:
                emit(em, block, sems)
            return nc

        if _stop == 0:
            return _finish({"xnT": xnT[:], "nmean": nmean[:], "nmean1": nmean1[:],
                            "varn": varn[:], "rstd": rstd[:], "xst": xst[:]})


        for grp in range(4):
            if grp == 0:
                v4_all = vdir_last[0]
                vm = vm0
            else:
                vm = V(lambda v: v.memset(V4[:], 1.0), waits=[v4_free])
            for kt in ([] if grp == 0 else range(NKT)):
                psX = psV if kt % 2 == 0 else psO
                psX_f = psV_free if kt % 2 == 0 else psO_free
                last = None
                for mc in range(NMC):
                    last = T(lambda tt, kt=kt, mc=mc, psX=psX: tt.matmul(
                        psX[:, 0:256],
                        xnT[:, mc * S + kt * 128:mc * S + (kt + 1) * 128],
                        wv_sb[:, mc * 256:(mc + 1) * 256],
                        start=(mc == 0), stop=False),
                        waits=[xn_ready[kt], wv_ld[grp], psX_f])
                trk = T(lambda tt, grp=grp, psX=psX: tt.matmul(
                    psX[:, 0:256], cones[0:1, 0:128], bv_bf[0:1, grp * 256:(grp + 1) * 256],
                    start=False, stop=True), waits=[t_gconst, t_ones])
                tev = V(lambda v, kt=kt, psX=psX: v.tensor_copy(
                    V4[:, kt * 260:(kt + 1) * 260].rearrange(
                        "p (h d) -> p h d", d=65)[:, :, 0:64],
                    psX[:, 0:256].rearrange("p (h d) -> p h d", d=64)),
                    waits=[trk, vm])
                if kt % 2 == 0:
                    psV_free = tev
                else:
                    psO_free = tev
                vdir_last[grp] = tev
            v4_all = vdir_last[grp]

            grp_av_last = None
            for pr in (2 * grp, 2 * grp + 1):
                par = pr % 2
                ptk = prep_tok[pr]
                kq_scal = ptk["kq_scal"]
                kq_sync = ptk["kq_sync"]
                tdm_tok = ptk["tdm"]
                tasks = build_prep(pr + 1) if pr + 1 < 8 else []
                n_tasks = len(tasks)
                t_next = 0
                itc = 0

                if _stop == 0.2 and pr == 0:
                    return _finish({"Kt0": KtP[0][0][:], "Kt1": KtP[0][1][:],
                                    "Qt0": QtP[0][0][:], "Qt1": QtP[0][1][:],
                                    "V4": V4[:]})

                if pr % 2 == 1 and grp + 1 < 4:
                    wv_free[0] = vdir_last[grp]
                    load_wv(grp + 1)

                def dance_step(pv, k):
                    nonlocal psT1_free, ps2b_free, zt_free, zraw_free, zg_free
                    nonlocal zdma_last
                    qt, phase = k // 2, k % 2
                    if phase == 0:
                        tt1 = T(lambda tt, qt=qt, pv=pv: tt.transpose(
                            psT1[0:128, 0:65],
                            zraw[0:65, qt * 128:(qt + 1) * 128],
                            id_f32[0:65, 0:65]),
                            waits=[pv["tze"], psT1_free, t_sconst])
                        zraw_free = tt1
                        tg = V(lambda v: v.tensor_single_scalar(
                            zguard[:, 0:1], psT1[:, 64:65], 1e-30, OP.max),
                            waits=[tt1, zg_free])
                        trc = V(lambda v: v.reciprocal(rzc[:], zguard[:, 0:1]),
                                waits=[tg])
                        zg_free = trc
                        tsc = V(lambda v: v.tensor_single_scalar(
                            zt_sb[:], psT1[:, 0:64], rzc[:, 0:1], OP.mult),
                            waits=[trc, zt_free])
                        psT1_free = tsc
                        pv["tsc"] = tsc
                    else:
                        tt2 = T(lambda tt, pv=pv: tt.transpose(
                            psT2[0:64, 0:128], zt_sb[:], id_bf[:]),
                            waits=[pv["tsc"], ps2b_free, t_gconst])
                        zt_free = tt2
                        g, hh, zp, zsl = pv["g"], pv["hh"], pv["zp"], pv["zsl"]
                        if hh == 0:
                            tev = V(lambda v, g=g, qt=qt, zp=zp: v.tensor_copy(
                                zp[0:64, g * 512 + qt * 128:g * 512 + (qt + 1) * 128],
                                psT2[0:64, 0:128]),
                                waits=[tt2, zall_free_all])
                            if qt == 3:
                                zev_tok[(pv["pl"], g)] = tev
                        else:
                            tev = V(lambda v, qt=qt, zsl=zsl: v.tensor_copy(
                                zst[0:64, zsl + qt * 128:zsl + (qt + 1) * 128],
                                psT2[0:64, 0:128]),
                                waits=[tt2, zst_free[g % 2]])
                        ps2b_free = tev
                        if qt == 3 and hh == 1:
                            tzd = LD(lambda s_, g=g, zp=zp, zsl=zsl: s_.dma_start(
                                out=zp[64:128, g * 512:(g + 1) * 512],
                                in_=zst[0:64, zsl:zsl + 512]),
                                waits=[tev, zall_free_all],
                                key="s_zst%d" % (g % 2))
                            zst_free[g % 2] = tzd
                            zdma_tok[(pv["pl"], g)] = tzd

                prev = pend_dance
                for g in range(NQG):
                    nk = 8 * (g + 1)
                    for hh in range(2):
                        KT = KtP[par][hh]
                        QT = QtP[par][hh]
                        h4 = 2 * (pr % 2) + hh
                        zi = zrot % 2
                        zrot += 1
                        psZX = psZ if zi == 0 else psO
                        psZX_f = psZ_free if zi == 0 else psO_free
                        av_last = None
                        pend_av = None
                        for kc in range(nk + 1):
                            if kc < nk:
                                spb = kc % 2
                                psSX = (psB, psB2)[spb]
                                slot = kc % 4
                                tmm = T(lambda tt, kc=kc, psSX=psSX, KT=KT, QT=QT,
                                        g=g: tt.matmul(
                                    psSX[:], KT[0:69, kc * 128:(kc + 1) * 128],
                                    QT[0:69, g * 512:(g + 1) * 512],
                                    start=True, stop=True),
                                    waits=[tdm_tok[(g, hh)], psS_free[spb],
                                           kq_scal] + kq_sync)
                                texp = C(lambda sc, slot=slot, psSX=psSX:
                                         sc.activation(
                                    probs[:, slot * 512:(slot + 1) * 512], psSX[:],
                                    AF.Exp, bias=0.0, scale=0.125),
                                    waits=[tmm, probs_free[slot]])
                                psS_free[spb] = texp
                                prb = texp
                                dg = kc - (8 * g + 4)
                                if 0 <= dg < 4:
                                    prb = V(lambda v, slot=slot, dg=dg:
                                            v.tensor_tensor(
                                        probs[:, slot * 512:(slot + 1) * 512],
                                        probs[:, slot * 512:(slot + 1) * 512],
                                        diag_bf[:, dg * 512:(dg + 1) * 512],
                                        OP.mult), waits=[prb, t_gmask])
                            if pend_av is not None:
                                kcp, slotp, prbp = pend_av
                                tav = T(lambda tt, kc=kcp, slot=slotp, h4=h4,
                                        nk=nk, psZX=psZX: tt.matmul(
                                    psZX[0:65, :],
                                    V4[:, kc * 260 + h4 * 65:kc * 260 + h4 * 65 + 65],
                                    probs[:, slot * 512:(slot + 1) * 512],
                                    start=(kc == 0), stop=(kc == nk - 1)),
                                    waits=[prbp, v4_all, psZX_f])
                                probs_free[slotp] = tav
                                av_last = tav
                            if kc < nk:
                                pend_av = (kc, slot, prb)
                            if prev is not None and kc < 8:
                                dance_step(prev, kc)
                            if kc < nk:
                                itc += 1
                                tgt = n_tasks * itc // 160
                                while t_next < tgt:
                                    tasks[t_next]()
                                    t_next += 1
                        grp_av_last = av_last
                        tze = C(lambda sc, psZX=psZX: sc.activation(
                            zraw[0:65, :], psZX[0:65, :], AF.Identity),
                            waits=[av_last, zraw_free])
                        if zi == 0:
                            psZ_free = tze
                        else:
                            psO_free = tze
                        prev = {"tze": tze, "g": g, "hh": hh, "zp": zallT[pr % 2],
                                "zsl": (g % 2) * 512, "pl": pr % 2}
                while t_next < n_tasks:
                    tasks[t_next]()
                    t_next += 1
                kq_free[par] = grp_av_last
                pend_dance = prev
                if pr % 2 == 1:
                    for k in range(8):
                        dance_step(pend_dance, k)
                    pend_dance = None


            op_last = None
            _nqt = 1 if _stop == 0.9 else 16
            for qt16 in range(_nqt):
                gq = qt16 // 4
                zw = [zev_tok.get((0, gq)), zev_tok.get((1, gq)),
                      zdma_tok.get((0, gq)), zdma_tok.get((1, gq))]
                for mh in range(2):
                    psX = psV if mh == 0 else psO
                    ps_free = psV_free if mh == 0 else psO_free
                    last = None
                    for pl in range(2):
                        last = T(lambda tt, psX=psX, qt16=qt16,
                                 mh=mh, pl=pl, grp=grp: tt.matmul(
                            psX[:],
                            zallT[pl][0:128, qt16 * 128:(qt16 + 1) * 128],
                            wo_sb[0:128,
                                  pl * 1024 + mh * 512:
                                  pl * 1024 + (mh + 1) * 512],
                            start=(pl == 0),
                            stop=(pl == 1 and (grp != 0 or _stop == 0.95))),
                            waits=[wo_ld[grp], ps_free] + zw)
                    if grp == 0 and _stop != 0.95:
                        last = T(lambda tt, mh=mh, psX=psX: tt.matmul(
                            psX[:], cones[0:1, 0:128], bo_bf[0:1, mh * 512:(mh + 1) * 512],
                            start=False, stop=True), waits=[t_gconst, t_ones])
                    aof = qt16 * 1024 + mh * 512
                    if grp == 0:
                        tev = V(lambda v, aof=aof, psX=psX: v.tensor_copy(
                            acc[:, aof:aof + 512], psX[:]), waits=[last])
                    else:
                        tev = V(lambda v, aof=aof, psX=psX: v.tensor_tensor(
                            acc[:, aof:aof + 512], acc[:, aof:aof + 512], psX[:],
                            OP.add), waits=[last, acc_tok[qt16][mh]])
                    acc_tok[qt16][mh] = tev
                    op_last = tev
                    if mh == 0:
                        psV_free = tev
                    else:
                        psO_free = tev
            zall_free_all = op_last
            v4_free = grp_av_last
            if _stop in (0.9, 0.95, 1):
                break
            if grp + 1 < 4:
                wo_free[0] = op_last
                load_wo(grp + 1)

        attn_fin = op_last

    with ExitStack() as fctx:
        sbf = lambda n, sh, dt=BF16: fctx.enter_context(nc.sbuf_tensor(n, sh, dt))
        wg_sb = sbf("wg_sb", [128, 2 * 4096])
        wu_sb = sbf("wu_sb", [128, 2 * 4096])
        wd_sb = sbf("wd_sb", [128, 2 * 4096])
        sil_sb = sbf("sil_sb", [128, 2 * 512])
        hT = sbf("hT", [128, 4 * 512])
        lgs = sbf("lgs", [128, 128], F32)
        v1s = sbf("v1s", [128, 16], F32)
        v2s = sbf("v2s", [128, 16], F32)
        d12 = sbf("d12", [128, 16], F32)
        p1s = sbf("p1s", [128, 16], F32)
        p2s = sbf("p2s", [128, 16], F32)
        eq1 = sbf("eq1", [128, 128], F32)
        eq2 = sbf("eq2", [128, 128], F32)
        rw1 = sbf("rw1", [128, 128], F32)
        rws = sbf("rws", [128, 128], F32)
        cbig = sbf("cbig", [128, 1], F32)
        xm_sb = sbf("xm_sb", [128, 2 * D], F32)
        out_sb = sbf("out_sb", [128, 2 * D], F32)

        wff_ld = [None] * 10
        wff_free = [attn_fin, attn_fin]

        def load_ffn(e):
            sl = e % 2
            t1 = WL(lambda g, e=e, sl=sl: g.dma_start(
                out=wg_sb[:, sl * 4096:(sl + 1) * 4096], in_=wgE[e]),
                waits=[wff_free[sl]], key="g_wg%d" % sl)
            t2 = WL(lambda g, e=e, sl=sl: g.dma_start(
                out=wu_sb[:, sl * 4096:(sl + 1) * 4096], in_=wuE[e]),
                waits=[wff_free[sl]], key="g_wu%d" % sl)
            t3 = WL(lambda g, e=e, sl=sl: g.dma_start(
                out=wd_sb[:, sl * 4096:(sl + 1) * 4096], in_=wdE[e]),
                waits=[wff_free[sl]], key="g_wd%d" % sl)
            wff_ld[e] = (t1, t2, t3)

        load_ffn(0)
        load_ffn(1)

        tmc = V(lambda v: v.memset(cbig[:], -1.0e9))
        psV_f = psV_free
        lg_last = None
        for t16 in range(16):
            g16, tl = t16 // 4, t16 % 4
            qof = (2 * g16 + 1) * 512 + tl * 128
            last = None
            for mc in range(NMC):
                last = T(lambda tt, qof=qof, mc=mc: tt.matmul(
                    psV[:, 0:8],
                    xnT[:, mc * S + qof:mc * S + qof + 128],
                    wr_bf[:, mc * 8:(mc + 1) * 8],
                    start=(mc == 0), stop=False), waits=[t_gconst, psV_f])
            last = T(lambda tt: tt.matmul(
                psV[:, 0:8], cones[0:1, 0:128], rb_bf[0:1, 0:8],
                start=False, stop=True), waits=[t_gconst, t_ones])
            lg_last = V(lambda v, t16=t16: v.tensor_copy(
                lgs[:, t16 * 8:(t16 + 1) * 8], psV[:, 0:8]), waits=[last])
            psV_f = lg_last
        lg3 = lgs[:].rearrange("p (t e) -> p t e", e=8)
        tv1 = V(lambda v: v.tensor_reduce(v1s[:].unsqueeze(2), lg3, AX, OP.max),
                waits=[lg_last])
        teq1 = V(lambda v: v.tensor_tensor(
            eq1[:].rearrange("p (t e) -> p t e", e=8), lg3,
            v1s[:].unsqueeze(2).broadcast_to([128, 16, 8]), OP.is_equal),
            waits=[tv1])
        tl2 = V(lambda v: v.scalar_tensor_tensor(
            rw1[:], eq1[:], cbig[:, 0:1], lgs[:], OP.mult, OP.add),
            waits=[tmc, teq1])
        tv2 = V(lambda v: v.tensor_reduce(
            v2s[:].unsqueeze(2), rw1[:].rearrange("p (t e) -> p t e", e=8), AX,
            OP.max), waits=[tl2])
        teq2 = V(lambda v: v.tensor_tensor(
            eq2[:].rearrange("p (t e) -> p t e", e=8), lg3,
            v2s[:].unsqueeze(2).broadcast_to([128, 16, 8]), OP.is_equal),
            waits=[tv2])
        td = V(lambda v: v.tensor_sub(d12[:], v1s[:], v2s[:]), waits=[tv2])
        tp1 = C(lambda sc: sc.activation(p1s[:], d12[:], AF.Sigmoid), waits=[td])
        tp2 = C(lambda sc: sc.activation(p2s[:], d12[:], AF.Sigmoid, scale=-1.0),
                waits=[td])
        tr1 = V(lambda v: v.tensor_tensor(
            rw1[:].rearrange("p (t e) -> p t e", e=8),
            eq1[:].rearrange("p (t e) -> p t e", e=8),
            p1s[:].unsqueeze(2).broadcast_to([128, 16, 8]), OP.mult), waits=[tp1])
        tr2 = V(lambda v: v.tensor_tensor(
            eq2[:].rearrange("p (t e) -> p t e", e=8),
            eq2[:].rearrange("p (t e) -> p t e", e=8),
            p2s[:].unsqueeze(2).broadcast_to([128, 16, 8]), OP.mult), waits=[tp2])
        trw = V(lambda v: v.tensor_add(rws[:], rw1[:], eq2[:]), waits=[tr1, tr2])

        if _stop == 3:
            return _finish({"rws": rws[:], "lgs": lgs[:], "acc": acc[:]})
        gA_f = [psA_free]
        gU_f = [psS_free[0]]
        dn_banks = [psB2, psZ, psV, psO]
        dn_f = [psS_free[1], psZ_free, psV_free, psO_free]
        sil_free = [None, None]
        hT_free = [None] * 4
        for e in range(10):
            sl = e % 2
            dn_last = None
            for g in range(NQG):
                qof = (2 * g + 1) * 512
                for fs in range(4):
                    fp = 0
                    psG = psA
                    psU = psB
                    gl = None
                    ul = None
                    for mc in range(NMC):
                        gl = T(lambda tt, sl=sl, mc=mc, fs=fs, qof=qof, psG=psG:
                               tt.matmul(
                            psG[:],
                            wg_sb[:, sl * 4096 + (mc * 4 + fs) * 128:
                                  sl * 4096 + (mc * 4 + fs) * 128 + 128],
                            xnT[:, mc * S + qof:mc * S + qof + 512],
                            start=(mc == 0), stop=(mc == NMC - 1)),
                            waits=[wff_ld[e][0], gA_f[fp], xn_all])
                    for mc in range(NMC):
                        ul = T(lambda tt, sl=sl, mc=mc, fs=fs, qof=qof, psU=psU:
                               tt.matmul(
                            psU[:],
                            wu_sb[:, sl * 4096 + (mc * 4 + fs) * 128:
                                  sl * 4096 + (mc * 4 + fs) * 128 + 128],
                            xnT[:, mc * S + qof:mc * S + qof + 512],
                            start=(mc == 0), stop=(mc == NMC - 1)),
                            waits=[wff_ld[e][1], gU_f[fp]])
                    ssl = fs % 2
                    tsil = C(lambda sc, ssl=ssl, psG=psG: sc.activation(
                        sil_sb[:, ssl * 512:(ssl + 1) * 512], psG[:], AF.Silu),
                        waits=[gl, sil_free[ssl]])
                    gA_f[fp] = tsil
                    th = V(lambda v, fs=fs, ssl=ssl, psU=psU: v.tensor_tensor(
                        hT[:, fs * 512:(fs + 1) * 512],
                        sil_sb[:, ssl * 512:(ssl + 1) * 512], psU[:], OP.mult),
                        waits=[tsil, ul, hT_free[fs]])
                    gU_f[fp] = th
                    sil_free[ssl] = th
                    hT_free[fs] = th
                for tl4 in range(4):
                    t16 = g * 4 + tl4
                    for mh in range(2):
                        dbi = (tl4 * 2 + mh) % 4
                        psX = dn_banks[dbi]
                        psX_f = dn_f[dbi]
                        last = None
                        for fc in range(4):
                            last = T(lambda tt, sl=sl, fc=fc, tl4=tl4, mh=mh,
                                     psX=psX: tt.matmul(
                                psX[:],
                                hT[:, fc * 512 + tl4 * 128:fc * 512 + tl4 * 128 + 128],
                                wd_sb[:, sl * 4096 + fc * 1024 + mh * 512:
                                      sl * 4096 + fc * 1024 + (mh + 1) * 512],
                                start=(fc == 0), stop=(fc == 3)),
                                waits=[wff_ld[e][2], psX_f, hT_free[fc]])
                        aof = t16 * 1024 + mh * 512
                        if e < 8:
                            tadd = V(lambda v, aof=aof, psX=psX, t16=t16, e=e:
                                     v.scalar_tensor_tensor(
                                acc[:, aof:aof + 512], psX[:],
                                rws[:, t16 * 8 + e:t16 * 8 + e + 1],
                                acc[:, aof:aof + 512], OP.mult, OP.add),
                                waits=[last, trw, acc_tok[t16][mh]])
                        else:
                            tadd = V(lambda v, aof=aof, psX=psX: v.tensor_tensor(
                                acc[:, aof:aof + 512], acc[:, aof:aof + 512],
                                psX[:], OP.add), waits=[last, acc_tok[t16][mh]])
                        acc_tok[t16][mh] = tadd
                        dn_last = tadd
                        dn_f[dbi] = tadd
                for fc in range(4):
                    hT_free[fc] = dn_last
            if e + 2 < 10:
                wff_free[sl] = dn_last
                load_ffn(e + 2)

        xm_free = [None, None]
        os_free = [None, None]
        for t16 in range(16):
            sl = t16 % 2
            g16, tl4 = t16 // 4, t16 % 4
            row0 = (2 * g16 + 1) * 512 + tl4 * 128
            tx = LD(lambda s, sl=sl, row0=row0: s.dma_start(
                out=xm_sb[:, sl * D:(sl + 1) * D], in_=xkv[row0:row0 + 128, :]),
                waits=[xm_free[sl]], key="s_xm%d" % sl)
            tme = V(lambda v, sl=sl, t16=t16: v.tensor_tensor(
                out_sb[:, sl * D:(sl + 1) * D], xm_sb[:, sl * D:(sl + 1) * D],
                acc[:, t16 * D:(t16 + 1) * D], OP.add),
                waits=[tx, acc_tok[t16][0], acc_tok[t16][1], os_free[sl]])
            xm_free[sl] = tme
            tst = LD(lambda s, sl=sl, t16=t16: s.dma_start(
                out=out_hbm[t16 * 128:(t16 + 1) * 128, :],
                in_=out_sb[:, sl * D:(sl + 1) * D]), waits=[tme],
                key="s_out%d" % sl)
            os_free[sl] = tst

        em.wait_only("sync", em.all_keys() + [(e, em.cnt[e]) for e in CENGS])

    sems = make_sems()
    with nc.Block() as block:
        emit(em, block, sems)
    return nc


_PROG = None


def _get_prog():
    global _PROG
    if _PROG is None:
        _PROG = build_program()
    return _PROG


def _host_layouts(inp):
    f = np.float32
    WQ = np.asarray(inp["L0_attn_W_Q"], f)
    WK = np.asarray(inp["L0_attn_W_K"], f)
    WV = np.asarray(inp["L0_attn_W_V"], f)
    WO = np.asarray(inp["L0_attn_W_O"], f)
    bQ = np.asarray(inp["L0_attn_b_Q"], f)
    bK = np.asarray(inp["L0_attn_b_K"], f)
    bV = np.asarray(inp["L0_attn_b_V"], f)
    bO = np.asarray(inp["L0_attn_b_O"], f)
    Wr = np.asarray(inp["L0_ffn_Wr"], f)
    Wg = np.asarray(inp["L0_ffn_Wg"], f)
    Wu = np.asarray(inp["L0_ffn_Wu"], f)
    Wd = np.asarray(inp["L0_ffn_Wd"], f)
    Sg = np.asarray(inp["L0_ffn_Sg"], f)
    Su = np.asarray(inp["L0_ffn_Su"], f)
    Sd = np.asarray(inp["L0_ffn_Sd"], f)
    eb = np.asarray(inp["L0_ffn_expert_bias"], f)

    d = {}
    d["wqT"] = np.ascontiguousarray(
        WQ.reshape(8, 2, 8, 128, 64).transpose(3, 0, 2, 1, 4).reshape(128, 8192))
    d["wkT"] = np.ascontiguousarray(
        WK.reshape(8, 2, 8, 128, 64).transpose(3, 0, 2, 1, 4).reshape(128, 8192))
    d["wv4"] = np.ascontiguousarray(
        WV.reshape(4, 4, 8, 128, 64).transpose(3, 0, 2, 1, 4).reshape(128, 8192))
    d["woT"] = np.ascontiguousarray(
        WO.reshape(4, 2, 2, 64, 1024).transpose(2, 3, 0, 1, 4).reshape(128, 8192))
    d["bqk"] = np.ascontiguousarray(
        np.stack([bQ.reshape(8, 2 * 64), bK.reshape(8, 2 * 64)], axis=2)
        .transpose(1, 0, 2).reshape(128, 16))
    d["bvr"] = bV.reshape(1, 1024).copy()
    d["bor"] = bO.reshape(1, 1024).copy()
    d["wrR"] = np.ascontiguousarray(
        Wr.reshape(8, 128, 8).transpose(1, 0, 2).reshape(128, 64))
    d["rbr"] = eb.reshape(1, 8).copy()

    def gate_layout(w):
        return np.ascontiguousarray(
            w.reshape(-1, 8, 128, 4, 128).transpose(0, 2, 1, 3, 4).reshape(-1, 128, 4096))

    def down_layout(w):
        return np.ascontiguousarray(
            w.reshape(-1, 4, 128, 1024).transpose(0, 2, 1, 3).reshape(-1, 128, 4096))

    g_sh = np.stack([Sg[:, 0:512], Sg[:, 512:1024]], 0)
    u_sh = np.stack([Su[:, 0:512], Su[:, 512:1024]], 0)
    dd_sh = np.stack([Sd[0:512, :], Sd[512:1024, :]], 0)
    d["wgE"] = np.concatenate([gate_layout(Wg), gate_layout(g_sh)], 0)
    d["wuE"] = np.concatenate([gate_layout(Wu), gate_layout(u_sh)], 0)
    d["wdE"] = np.concatenate([down_layout(Wd), down_layout(dd_sh)], 0)

    dg = np.zeros((128, 2048), np.float32)
    for c in range(4):
        for p in range(128):
            dg[p, c * 512 + c * 128 + p:(c + 1) * 512] = 1.0
    d["diag"] = dg
    d["ident"] = np.eye(128, dtype=np.float32)
    return d


_PERM_B = [1, 0, 3, 2, 5, 4, 7, 6]


def kernel(**inputs):
    x = np.asarray(inputs["L0_attn_x"], np.float32)
    shared = _host_layouts(inputs)
    nc = _get_prog()

    qi = np.zeros((4, OWN), np.float32)
    for g in range(4):
        qi[g, g * 512:(g + 1) * 512] = -1.0

    km_A = np.zeros((4, S), np.float32)
    km_B = np.zeros((4, S), np.float32)
    for g in range(4):
        km_B[g, 2 * g * 512:(2 * g + 1) * 512] = BIGMASK

    in_maps = []
    for core in range(8):
        b = core // 2
        role = core % 2
        if role == 0:
            xc = np.ascontiguousarray(x[b])
            km = km_A
        else:
            xc = np.ascontiguousarray(x[b].reshape(8, 512, D)[_PERM_B].reshape(S, D))
            km = km_B
        m = dict(shared)
        m["xkv"] = xc
        m["kmask"] = km
        m["qind"] = qi
        in_maps.append(m)

    res = run_bass_kernel_spmd(nc, in_maps, core_ids=list(range(8)))
    y = np.empty((B, S, D), np.float32)
    for core in range(8):
        b = core // 2
        role = core % 2
        o = np.asarray(res.results[core]["out"], np.float32)
        for g in range(4):
            blk = 2 * g + 1 if role == 0 else 2 * g
            y[b, blk * 512:(blk + 1) * 512, :] = o[g * 512:(g + 1) * 512, :]
    return y


# version 13
# speedup vs baseline: 1.0120x; 1.0120x over previous
import numpy as np
from contextlib import ExitStack

import concourse.bass as bass
import concourse.mybir as mybir
from concourse.bass_utils import run_bass_kernel_spmd

F32 = mybir.dt.float32
BF16 = mybir.dt.bfloat16
AX = mybir.AxisListType.X
OP = mybir.AluOpType
AF = mybir.ActivationFunctionType

B, S, D, H, DH = 4, 4096, 1024, 16, 64
NMC = D // 128
NKT = S // 128
NQG = 4
OWN = 2048
EPS = 1e-5
BIGMASK = 1.0e5

ENGS = ("sync", "gpsimd", "tensor", "vector", "scalar")
CENGS = ("tensor", "vector", "scalar")


class Em:

    def __init__(self):
        self.items = {e: [] for e in ENGS}
        self.cnt = {e: 0 for e in CENGS}
        self.kcnt = {}

    def op(self, eng, fn, waits=(), key=None):
        for w in waits:
            if w is not None:
                top = self.cnt.get(w[0], None)
                if top is None:
                    top = self.kcnt[w[0]]
                assert w[1] <= top, (eng, w)
        if eng in CENGS:
            assert key is None
            self.cnt[eng] += 1
            tok = (eng, self.cnt[eng])
            self.items[eng].append((fn, [w for w in waits if w is not None], 1, eng))
        else:
            assert key is not None
            self.kcnt[key] = self.kcnt.get(key, 0) + 16
            tok = (key, self.kcnt[key])
            self.items[eng].append((fn, [w for w in waits if w is not None], 16, key))
        return tok

    def wait_only(self, eng, waits):
        self.items[eng].append((None, [w for w in waits if w is not None], 0, None))

    def all_keys(self):
        return [(k, v) for k, v in self.kcnt.items()]


def emit(em, block, sems):
    def runner(ename):
        def run(engine):
            last = {}
            cnt = 0
            for fn, waits, inc, key in em.items[ename]:
                for (we, wv) in waits:
                    if we == ename and fn is not None:
                        if ename in ("vector", "scalar") and cnt - wv <= 1 \
                                and wv > last.get(we, 0):
                            engine.wait_ge(sems[we], wv)
                            last[we] = wv
                        continue
                    if wv > last.get(we, 0) or fn is None:
                        engine.wait_ge(sems[we], wv)
                        last[we] = max(last.get(we, 0), wv)
                if fn is not None:
                    ins = fn(engine)
                    ins.then_inc(sems[key], inc)
                    if key == ename:
                        cnt += inc
        return run

    block.sync(runner("sync"))
    block.gpsimd(runner("gpsimd"))
    block.tensor(runner("tensor"))
    block.vector(runner("vector"))
    block.scalar(runner("scalar"))


def build_program(stage="all", taps=()):
    nc = bass.Bass("TRN2", target_bir_lowering=False, debug=False)

    dt_in = lambda n, sh, dt=F32: nc.dram_tensor(n, sh, dt, kind="ExternalInput").ap()
    xkv = dt_in("xkv", [S, D])
    wqT = dt_in("wqT", [128, 8192])
    wkT = dt_in("wkT", [128, 8192])
    wv4 = dt_in("wv4", [128, 8192])
    woT = dt_in("woT", [128, 8192])
    bqk = dt_in("bqk", [128, 16])
    bvr = dt_in("bvr", [1, 1024])
    bor = dt_in("bor", [1, 1024])
    wrR = dt_in("wrR", [128, 64])
    rbr = dt_in("rbr", [1, 8])
    wgE = dt_in("wgE", [10, 128, 4096])
    wuE = dt_in("wuE", [10, 128, 4096])
    wdE = dt_in("wdE", [10, 128, 4096])
    kmask = dt_in("kmask", [4, S])
    qind = dt_in("qind", [4, OWN])
    diag = dt_in("diag", [128, 2048])
    ident = dt_in("ident", [128, 128])
    out_hbm = nc.dram_tensor("out", [OWN, D], F32, kind="ExternalOutput").ap()
    tap_hbm = {}
    for tname, tshape, tdt in taps:
        tap_hbm[tname] = nc.dram_tensor("tap_" + tname, tshape, tdt,
                                        kind="ExternalOutput").ap()

    em = Em()
    T = lambda fn, waits=(): em.op("tensor", fn, waits)
    V = lambda fn, waits=(): em.op("vector", fn, waits)
    C = lambda fn, waits=(): em.op("scalar", fn, waits)
    LD = lambda fn, waits=(), key=None: em.op("sync", fn, waits, key=key)
    WL = lambda fn, waits=(), key=None: em.op("gpsimd", fn, waits, key=key)

    ctx = ExitStack()
    sb = lambda n, sh, dt=BF16: ctx.enter_context(nc.sbuf_tensor(n, sh, dt))
    psd = lambda n, sh, dt=F32: ctx.enter_context(nc.psum_tensor(n, sh, dt))

    def make_sems():
        sems = {e: ctx.enter_context(nc.semaphore("sem_" + e)) for e in CENGS}
        for k in em.kcnt:
            sems[k] = ctx.enter_context(nc.semaphore("sem_" + k))
        return sems

    xnT = sb("xnT", [128, NMC * S])
    acc = sb("acc", [128, 16 * D])
    id_bf = sb("id_bf", [128, 128])
    id_f32 = sb("id_f32", [128, 128], F32)
    cones = sb("cones", [9, 256])
    bqk_sb = sb("bqk_sb", [128, 16], F32)
    bv_bf = sb("bv_bf", [9, 1088])
    bo_bf = sb("bo_bf", [9, 1088])
    rb_bf = sb("rb_bf", [9, 72])
    wr_bf = sb("wr_bf", [128, 128])
    epsc = sb("epsc", [128, 1], F32)

    psA = psd("psA", [128, 512])
    psB = psd("psB", [128, 512])
    psB2 = psd("psB2", [128, 512])
    psZ = psd("psZ", [128, 512])
    psV = psd("psV", [128, 512])
    psO = psd("psO", [128, 512])
    psT1 = psd("psT1", [128, 128])
    psT2 = psd("psT2", [128, 512], BF16)

    WL(lambda g: g.dma_start(out=id_bf[:], in_=ident), key="g_const")
    LD(lambda s: s.dma_start(out=id_f32[:], in_=ident), key="s_const")
    LD(lambda s: s.dma_start(out=bqk_sb[:], in_=bqk), key="s_const")
    WL(lambda g: g.dma_start(out=bv_bf[0:1, 0:1024], in_=bvr), key="g_const")
    WL(lambda g: g.dma_start(out=bo_bf[0:1, 0:1024], in_=bor), key="g_const")
    WL(lambda g: g.dma_start(out=rb_bf[0:1, 0:8], in_=rbr), key="g_const")
    WL(lambda g: g.dma_start(out=wr_bf[:, 0:64], in_=wrR), key="g_const")
    onesf = sb("onesf", [128, 1], F32)
    t_ones = V(lambda v: v.memset(cones[0:1, 0:256], 1.0))
    t_eps = V(lambda v: v.memset(epsc[:], EPS))
    t_onesf = V(lambda v: v.memset(onesf[:], 1.0))
    t_gconst = ("g_const", em.kcnt["g_const"])
    t_sconst = ("s_const", em.kcnt["s_const"])
    t_id = t_idf = t_bqk = t_bv = t_bo = t_rb = t_wr = None

    acc_tok = [[None, None] for _ in range(16)]

    with ExitStack() as actx:
        sba = lambda n, sh, dt=BF16: actx.enter_context(nc.sbuf_tensor(n, sh, dt))
        nmean = sba("nmean", [128, 2], F32)
        nmean1 = sba("nmean1", [128, 2], F32)
        varn = sba("varn", [128, 2], F32)
        sq_s = sba("sq_s", [128, 2], F32)
        rstd = sba("rstd", [128, 2], F32)
        diag_bf = sba("diag_bf", [128, 2048])
        wq_sb = sba("wq_sb", [128, 2 * 1024])
        wk_sb = sba("wk_sb", [128, 2 * 1024])
        wv_sb = sba("wv_sb", [128, 2048])
        wo_sb = sba("wo_sb", [128, 2112])
        V4 = sba("V4", [128, NKT * 260])
        KtP = [[sba("Kt%d%d" % (par, hh), [69, S]) for hh in range(2)]
               for par in range(2)]
        QtP = [[sba("Qt%d%d" % (par, hh), [69, OWN]) for hh in range(2)]
               for par in range(2)]
        pstage = sba("pstage", [128, 2 * 512])
        onescol = sba("onescol", [128, 8])
        kmx_c = sba("kmx_c", [128, 1], F32)
        kmax_sb = sba("kmax_sb", [9, 8])
        kb_sb = sba("kb_sb", [128, 2], F32)
        negc = sba("negc", [128, 32], F32)
        zguard = sba("zguard", [128, 1], F32)
        zt_sb = sba("zt_sb", [128, 64])
        rzc = sba("rzc", [128, 1], F32)
        pctx = ExitStack()
        sbp = lambda n, sh, dt=BF16: pctx.enter_context(nc.sbuf_tensor(n, sh, dt))
        xst = sbp("xst", [128, 2 * D], F32)
        lntr = sbp("lntr", [128, D], F32)
        xnt_st = sbp("xnt_st", [128, 2 * D])

        WL(lambda g: g.dma_start(out=diag_bf[:], in_=diag), key="g_mask")

        t_onescol = V(lambda v: v.memset(onescol[:, 0:8], 1.0))
        ktc = []
        for par in range(2):
            for hh in range(2):
                ktc.append(V(lambda v, par=par, hh=hh: v.memset(
                    KtP[par][hh][64:65, :], 1.0)))
                WL(lambda g, par=par, hh=hh: g.dma_start(
                    out=KtP[par][hh][65:69, :], in_=kmask), key="g_mask")
                WL(lambda g, par=par, hh=hh: g.dma_start(
                    out=QtP[par][hh][65:69, :], in_=qind), key="g_mask")
        t_gmask = ("g_mask", em.kcnt["g_mask"])
        qtc = [t_gmask]

        wq_free = [None, None]
        wk_free = [None, None]
        wv_free = [None, None]
        wo_free = [None, None]
        wq_ld = [None] * 8
        wk_ld = [None] * 8
        wv_ld = [None] * 4
        wo_ld = [None] * 4
        kq_free = [None, None]
        v4_free = None
        psS_free = [None, None]
        psA_free = None
        psZ_free = None
        psV_free = None
        psO_free = None
        psT1_free = None
        kmx_free = None
        kmaxsb_free = None
        kb_free = None
        negc_free = None
        zg_free = None
        zrot = 0
        pend_dance = None
        zdma_last = None
        mst_free = []
        ps2b_free = None
        pst_free = [None, None]
        probs_free = [None] * 4
        prep_tok = {}
        zev_tok = {}
        zdma_tok = {}

        zraw_free = None
        zt_free = None
        zall_free_all = None
        zst_free = [None, None]
        vdir_last = [None] * 4

        def load_wqk(p):
            slp = p % 2
            wq_ld[p] = WL(lambda g, p=p, slp=slp: g.dma_start(
                out=wq_sb[:, slp * 1024:(slp + 1) * 1024],
                in_=wqT[:, p * 1024:(p + 1) * 1024]), waits=[wq_free[slp]],
                key="g_wq%d" % slp)
            wk_ld[p] = WL(lambda g, p=p, slp=slp: g.dma_start(
                out=wk_sb[:, slp * 1024:(slp + 1) * 1024],
                in_=wkT[:, p * 1024:(p + 1) * 1024]), waits=[wk_free[slp]],
                key="g_wk%d" % slp)

        def load_wv(grp):
            wv_ld[grp] = WL(lambda g, grp=grp: g.dma_start(
                out=wv_sb[:],
                in_=wv4[:, grp * 2048:(grp + 1) * 2048]), waits=[wv_free[0]],
                key="g_wv")

        def load_wo(grp):
            wo_ld[grp] = WL(lambda g, grp=grp: g.dma_start(
                out=wo_sb[0:128, 0:2048],
                in_=woT[:, grp * 2048:(grp + 1) * 2048]), waits=[wo_free[0]],
                key="g_wo")

        load_wqk(0)
        load_wqk(1)
        load_wv(0)
        load_wo(0)

        vm0 = V(lambda v: v.memset(V4[:], 1.0))
        vdir_evac0 = None

        xst_free = [None, None]
        xnst_free = [None, None]
        ps2_free = None
        prev_sq = [None, None]
        xn_ready = [None] * NKT
        stats_tok = [None] * NKT
        vdir_evac0 = None

        def ln_tail(t):
            nonlocal ps2_free, psV_free, psO_free, vdir_evac0
            sl = t % 2
            x_sl = xst[:, sl * D:(sl + 1) * D]
            xn_sl = xnt_st[:, sl * D:(sl + 1) * D]
            t_sq, t_sqrt = stats_tok[t]
            t_rcp = V(lambda v, sl=sl: v.reciprocal(rstd[:, sl:sl + 1],
                                                    sq_s[:, sl:sl + 1]),
                      waits=[t_sqrt])
            t_app = V(lambda v, sl=sl, x_sl=x_sl, xn_sl=xn_sl: v.tensor_scalar(
                xn_sl, x_sl, nmean1[:, sl:sl + 1], rstd[:, sl:sl + 1],
                OP.add, OP.mult),
                waits=[xnst_free[sl], t_sq, t_rcp])
            xst_free[sl] = t_app
            evl = None
            for half in range(2):
                tps = []
                for b4 in range(4):
                    mc = half * 4 + b4
                    tps.append(T(lambda tt, mc=mc, b4=b4, xn_sl=xn_sl: tt.transpose(
                        psT2[:, b4 * 128:(b4 + 1) * 128],
                        xn_sl[:, mc * 128:(mc + 1) * 128], id_bf[:]),
                        waits=[t_app, t_gconst, ps2_free]))

                def ev(v, half=half, t=t):
                    o = xnT[:].rearrange("p (mc s) -> p mc s", s=S)[
                        :, half * 4:half * 4 + 4, t * 128:(t + 1) * 128]
                    i = psT2[:].rearrange("p (b s) -> p b s", s=128)
                    return v.tensor_copy(o, i)
                evl = V(ev, waits=tps)
                ps2_free = evl
            xnst_free[sl] = evl
            xn_ready[t] = evl
            kt = t
            psX = psV if kt % 2 == 0 else psO
            psX_f = psV_free if kt % 2 == 0 else psO_free
            for mc in range(NMC):
                T(lambda tt, kt=kt, mc=mc, psX=psX: tt.matmul(
                    psX[:, 0:256],
                    xnT[:, mc * S + kt * 128:mc * S + (kt + 1) * 128],
                    wv_sb[:, mc * 256:(mc + 1) * 256],
                    start=(mc == 0), stop=False),
                    waits=[xn_ready[kt], wv_ld[0], psX_f])
            trk = T(lambda tt, psX=psX: tt.matmul(
                psX[:, 0:256], cones[0:1, 0:128], bv_bf[0:1, 0:256],
                start=False, stop=True), waits=[t_gconst, t_ones])
            tev = V(lambda v, kt=kt, psX=psX: v.tensor_copy(
                V4[:, kt * 260:(kt + 1) * 260].rearrange(
                    "p (h d) -> p h d", d=65)[:, :, 0:64],
                psX[:, 0:256].rearrange("p (h d) -> p h d", d=64)),
                waits=[trk, vm0])
            if kt % 2 == 0:
                psV_free = tev
            else:
                psO_free = tev
            vdir_evac0 = tev

        for t in range(NKT):
            sl = t % 2
            x_sl = xst[:, sl * D:(sl + 1) * D]
            t_load = LD(lambda s, t=t, x_sl=x_sl: s.dma_start(
                out=x_sl, in_=xkv[t * 128:(t + 1) * 128, :]), waits=[xst_free[sl]],
                key="s_x%d" % sl)
            t_red = V(lambda v, sl=sl, x_sl=x_sl: v.reduce_sum(
                nmean[:, sl:sl + 1], x_sl, AX), waits=[t_load])
            t_nm = V(lambda v, sl=sl: v.tensor_single_scalar(
                nmean1[:, sl:sl + 1], nmean[:, sl:sl + 1], -1.0 / D, OP.mult),
                waits=[prev_sq[sl], t_red])
            t_sq = C(lambda sc, sl=sl, x_sl=x_sl: sc.activation(
                lntr[:], x_sl, AF.Square, bias=nmean1[:, sl:sl + 1], scale=1.0,
                accum_out=varn[:, sl:sl + 1]), waits=[t_nm])
            prev_sq[sl] = t_sq
            t_sqrt = C(lambda sc, sl=sl: sc.activation(
                sq_s[:, sl:sl + 1], varn[:, sl:sl + 1], AF.Sqrt,
                bias=epsc[:, 0:1], scale=1.0 / D), waits=[t_eps, t_sq])
            stats_tok[t] = (t_sq, t_sqrt)
            if t >= 1:
                ln_tail(t - 1)
        ln_tail(NKT - 1)
        xn_all = xn_ready[NKT - 1]
        vdir_last[0] = vdir_evac0
        ps2b_free = ps2_free
        pctx.close()
        probs = sba("probs", [128, 4 * 512])
        mstage = sba("mstage", [32, 128], F32)
        zraw = sba("zraw", [65, 512], F32)
        zallT = [sba("zallT%d" % i, [128, OWN]) for i in range(2)]
        zst = sba("zst", [69, 2 * 512])
        sqstage = sba("sqstage", [128, 1024])
        t_sqz = V(lambda v: v.memset(sqstage[64:128, :], 0.0), waits=[xn_all])

        def build_prep(p):
            par = p % 2
            st = {}
            tasks = []

            def proj_mm_unit(kind, kc, u):
                def f():
                    last = None
                    for mc in (2 * u, 2 * u + 1):
                        if kind == "k":
                            lhs, rhs_col = wk_sb, kc * 512
                            xw, wld = xn_ready[kc * 4 + 3], wk_ld[p]
                        else:
                            lhs, rhs_col = wq_sb, (2 * kc + 1) * 512
                            xw, wld = xn_all, wq_ld[p]
                        last = T(lambda tt, mc=mc, lhs=lhs, rhs_col=rhs_col:
                                 tt.matmul(
                            psV[:],
                            lhs[:, par * 1024 + mc * 128:par * 1024 + (mc + 1) * 128],
                            xnT[:, mc * S + rhs_col:mc * S + rhs_col + 512],
                            start=(mc == 0), stop=(mc == NMC - 1)),
                            waits=[xw, wld, psV_free])
                    st[("mm", kind, kc)] = last
                return f

            def proj_evac(kind, kc, ci):
                def f():
                    nonlocal psV_free
                    sps = ci % 2
                    cc = 2 * p + (1 if kind == "k" else 0)
                    dA = KtP[par][0] if kind == "k" else QtP[par][0]
                    dB = KtP[par][1] if kind == "k" else QtP[par][1]
                    col = kc * 512
                    mm = st[("mm", kind, kc)]
                    te0 = V(lambda v: v.tensor_scalar(
                        dA[0:64, col:col + 512], psV[0:64, :],
                        bqk_sb[0:64, cc:cc + 1], onesf[0:64, 0:1],
                        OP.add, OP.mult),
                        waits=[mm, t_sconst, t_onesf, t_gmask, kq_free[par]]
                        + ktc)
                    te1 = V(lambda v: v.tensor_scalar(
                        pstage[64:128, sps * 512:(sps + 1) * 512], psV[64:128, :],
                        bqk_sb[64:128, cc:cc + 1], onesf[64:128, 0:1],
                        OP.add, OP.mult),
                        waits=[pst_free[sps]])
                    psV_free = te1
                    tsh = LD(lambda s: s.dma_start(
                        out=dB[0:64, col:col + 512],
                        in_=pstage[64:128, sps * 512:(sps + 1) * 512]),
                        waits=[te1, kq_free[par]], key="s_kq%d" % sps)
                    pst_free[sps] = tsh
                    st["evac_last"] = te1
                return f

            def t_bar(final):
                def f():
                    st["sync"] = [("s_kq0", em.kcnt.get("s_kq0", 0)),
                                  ("s_kq1", em.kcnt.get("s_kq1", 0))]
                    st["scal"] = st["evac_last"]
                    if final:
                        prep_tok[p]["kq_scal"] = st["evac_last"]
                        prep_tok[p]["kq_sync"] = list(st["sync"])
                        if p + 2 < 8:
                            wq_free[par] = st[("mm", "q", 3)]
                            wk_free[par] = st[("mm", "q", 3)]
                            load_wqk(p + 2)
                return f

            def t_square(src, soff):
                def f():
                    st["sq"] = V(lambda v: v.tensor_mul(
                        sqstage[0:64, 0:1024], src[0:64, soff:soff + 1024],
                        src[0:64, soff:soff + 1024]),
                        waits=[t_sqz, st.get("mm_last"), st["scal"]]
                        + st["sync"])
                return f

            def t_normmm(col0, u):
                def f():
                    for c8 in range(4 * u, 4 * u + 4):
                        st["mm_last"] = T(lambda tt, c8=c8: tt.matmul(
                            psA[:, col0 + c8:col0 + c8 + 1],
                            sqstage[0:128, c8 * 128:(c8 + 1) * 128],
                            onescol[0:128, 0:1], start=True, stop=True),
                            waits=[st["sq"], psA_free, t_onescol])
                return f

            def norm_job(kind, hh, half):
                if kind == "k":
                    src, col0 = KtP[par][hh], hh * 32 + half * 8
                else:
                    src, col0 = QtP[par][hh], 64 + hh * 16 + half * 8
                return [t_square(src, half * 1024),
                        t_normmm(col0, 0), t_normmm(col0, 1)]

            def t_kmax():
                nonlocal psT1_free, kmx_free
                for hh in range(2):
                    tr1 = V(lambda v, hh=hh: v.reduce_max(
                        kmx_c[:, 0:1], psA[:, hh * 32:(hh + 1) * 32], AX),
                        waits=[st["mm_last"], kmx_free, kmaxsb_free])
                    ttp = T(lambda tt: tt.transpose(
                        psT1[0:1, 0:128], kmx_c[:, 0:1], id_f32[:]),
                        waits=[tr1, psT1_free, t_sconst])
                    kmx_free = ttp
                    tr2 = V(lambda v, hh=hh: v.reduce_max(
                        kmax_sb[0:1, hh:hh + 1], psT1[0:1, 0:128], AX),
                        waits=[ttp])
                    psT1_free = tr2
                    st["km%d" % hh] = tr2

            def t_bcast():
                nonlocal psT1_free, psA_free, kmaxsb_free, kb_free
                tkb = T(lambda tt: tt.matmul(
                    psT1[0:128, 0:2], cones[0:1, 0:128], kmax_sb[0:1, 0:2],
                    start=True, stop=True),
                    waits=[st["km0"], st["km1"], psT1_free, t_ones])
                kmaxsb_free = tkb
                tkbe = V(lambda v: v.tensor_copy(kb_sb[:, 0:2], psT1[0:128, 0:2]),
                         waits=[tkb, kb_free])
                psT1_free = tkbe
                tpr = V(lambda v: v.tensor_tensor(
                    negc[:].rearrange("p (h q) -> p h q", q=16),
                    psA[:, 64:96].rearrange("p (h q) -> p h q", q=16),
                    kb_sb[:, 0:2].unsqueeze(2).broadcast_to([128, 2, 16]),
                    OP.mult),
                    waits=[tkbe, st["mm_last"], negc_free])
                kb_free = tpr
                psA_free = tpr
                tnc = C(lambda sc: sc.activation(negc[:], negc[:], AF.Sqrt),
                        waits=[tpr])
                st["negc"] = V(lambda v: v.tensor_single_scalar(
                    negc[:], negc[:], -1.0, OP.mult), waits=[tnc])

            def t_mrow():
                nonlocal psT1_free, negc_free, mst_free
                ttr = T(lambda tt: tt.transpose(
                    psT1[0:32, 0:128], negc[:, 0:32], id_f32[:]),
                    waits=[st["negc"], psT1_free, t_sconst])
                negc_free = ttr
                tcp = V(lambda v: v.tensor_copy(mstage[0:32, 0:128],
                                                psT1[0:32, 0:128]),
                        waits=[ttr] + mst_free)
                psT1_free = tcp
                tdmd = {}
                for g in range(NQG):
                    for hh in range(2):
                        QTh = QtP[par][hh]
                        tdmd[(g, hh)] = WL(lambda g_, g=g, hh=hh, QTh=QTh:
                                           g_.dma_start(
                            out=QTh[64:65, g * 512:(g + 1) * 512],
                            in_=mstage[hh * 16 + g * 4:hh * 16 + g * 4 + 4, 0:128]),
                            waits=[tcp, kq_free[par]],
                            key="g_mrow%d" % (g * 2 + hh))
                mst_free = list(tdmd.values())
                prep_tok[p]["tdm"] = tdmd

            def chunk(kind, kc, ci):
                return [proj_mm_unit(kind, kc, u) for u in range(4)] \
                    + [proj_evac(kind, kc, ci)]

            def zipped(projs, norms):
                out, pi, ni = [], 0, 0
                while pi < len(projs) or ni < len(norms):
                    if pi < len(projs):
                        out.append(projs[pi]); pi += 1
                    if ni < len(norms):
                        out.append(norms[ni]); ni += 1
                return out

            pchunks = [chunk("k", kc, kc) for kc in range(8)] \
                + [chunk("q", g, 8 + g) for g in range(NQG)]
            tasks += pchunks[0] + pchunks[1] + [t_bar(False)]
            for half in range(4):
                nj = norm_job("k", 0, half) + norm_job("k", 1, half)
                tasks += zipped(pchunks[2 * half + 2] + pchunks[2 * half + 3], nj)
                tasks.append(t_bar(half == 3 and False))
            nj = norm_job("q", 0, 0) + norm_job("q", 1, 0)
            tasks += zipped(pchunks[10] + pchunks[11], nj)
            tasks.append(t_bar(True))
            tasks += norm_job("q", 0, 1) + norm_job("q", 1, 1)
            tasks.append(t_kmax)
            tasks.append(t_bcast)
            tasks.append(t_mrow)
            prep_tok[p] = {}
            return tasks

        for _f in build_prep(0):
            _f()
        _stop = {"p0": 0, "kv0": 0.2, "pa0": 0.4, "pb0": 0.6, "pb1": 0.7,
                 "pr1": 0.8, "op0": 0.9, "opnb": 0.95,
                 "attn0": 1, "attn": 2, "router": 3, "all": 9}[stage]

        def _finish(local_tensors):
            cw = [(e, em.cnt[e]) for e in CENGS]
            for tname, tshape, tdt in taps:
                if tname in local_tensors:
                    tsrc = local_tensors[tname]
                    LD(lambda s_, tsrc=tsrc, tname=tname: s_.dma_start(
                        out=tap_hbm[tname], in_=tsrc), waits=cw, key="s_tap")
            em.wait_only("sync", em.all_keys() + [(e, em.cnt[e]) for e in CENGS])
            sems = make_sems()
            with nc.Block() as block:
                emit(em, block, sems)
            return nc

        if _stop == 0:
            return _finish({"xnT": xnT[:], "nmean": nmean[:], "nmean1": nmean1[:],
                            "varn": varn[:], "rstd": rstd[:], "xst": xst[:]})


        for grp in range(4):
            if grp == 0:
                v4_all = vdir_last[0]
            for kt in ([] if grp == 0 else range(NKT)):
                psX = psV if kt % 2 == 0 else psO
                psX_f = psV_free if kt % 2 == 0 else psO_free
                last = None
                for mc in range(NMC):
                    last = T(lambda tt, kt=kt, mc=mc, psX=psX: tt.matmul(
                        psX[:, 0:256],
                        xnT[:, mc * S + kt * 128:mc * S + (kt + 1) * 128],
                        wv_sb[:, mc * 256:(mc + 1) * 256],
                        start=(mc == 0), stop=False),
                        waits=[xn_ready[kt], wv_ld[grp], psX_f])
                trk = T(lambda tt, grp=grp, psX=psX: tt.matmul(
                    psX[:, 0:256], cones[0:1, 0:128], bv_bf[0:1, grp * 256:(grp + 1) * 256],
                    start=False, stop=True), waits=[t_gconst, t_ones])
                tev = V(lambda v, kt=kt, psX=psX: v.tensor_copy(
                    V4[:, kt * 260:(kt + 1) * 260].rearrange(
                        "p (h d) -> p h d", d=65)[:, :, 0:64],
                    psX[:, 0:256].rearrange("p (h d) -> p h d", d=64)),
                    waits=[trk, vm0, v4_free])
                if kt % 2 == 0:
                    psV_free = tev
                else:
                    psO_free = tev
                vdir_last[grp] = tev
            v4_all = vdir_last[grp]

            grp_av_last = None
            for pr in (2 * grp, 2 * grp + 1):
                par = pr % 2
                ptk = prep_tok[pr]
                kq_scal = ptk["kq_scal"]
                kq_sync = ptk["kq_sync"]
                tdm_tok = ptk["tdm"]
                tasks = build_prep(pr + 1) if pr + 1 < 8 else []
                n_tasks = len(tasks)
                t_next = 0
                itc = 0

                if _stop == 0.2 and pr == 0:
                    return _finish({"Kt0": KtP[0][0][:], "Kt1": KtP[0][1][:],
                                    "Qt0": QtP[0][0][:], "Qt1": QtP[0][1][:],
                                    "V4": V4[:]})

                if pr % 2 == 1 and grp + 1 < 4:
                    wv_free[0] = vdir_last[grp]
                    load_wv(grp + 1)

                def dance_step(pv, k):
                    nonlocal psT1_free, ps2b_free, zt_free, zraw_free, zg_free
                    nonlocal zdma_last
                    qt, phase = k // 2, k % 2
                    if phase == 0:
                        tt1 = T(lambda tt, qt=qt, pv=pv: tt.transpose(
                            psT1[0:128, 0:65],
                            zraw[0:65, qt * 128:(qt + 1) * 128],
                            id_f32[0:65, 0:65]),
                            waits=[pv["tze"], psT1_free, t_sconst])
                        zraw_free = tt1
                        tg = V(lambda v: v.tensor_single_scalar(
                            zguard[:, 0:1], psT1[:, 64:65], 1e-30, OP.max),
                            waits=[tt1, zg_free])
                        trc = V(lambda v: v.reciprocal(rzc[:], zguard[:, 0:1]),
                                waits=[tg])
                        zg_free = trc
                        tsc = V(lambda v: v.tensor_single_scalar(
                            zt_sb[:], psT1[:, 0:64], rzc[:, 0:1], OP.mult),
                            waits=[trc, zt_free])
                        psT1_free = tsc
                        pv["tsc"] = tsc
                    else:
                        tt2 = T(lambda tt, pv=pv: tt.transpose(
                            psT2[0:64, 0:128], zt_sb[:], id_bf[:]),
                            waits=[pv["tsc"], ps2b_free, t_gconst])
                        zt_free = tt2
                        g, hh, zp, zsl = pv["g"], pv["hh"], pv["zp"], pv["zsl"]
                        if hh == 0:
                            tev = V(lambda v, g=g, qt=qt, zp=zp: v.tensor_copy(
                                zp[0:64, g * 512 + qt * 128:g * 512 + (qt + 1) * 128],
                                psT2[0:64, 0:128]),
                                waits=[tt2, zall_free_all])
                            if qt == 3:
                                zev_tok[(pv["pl"], g)] = tev
                        else:
                            tev = V(lambda v, qt=qt, zsl=zsl: v.tensor_copy(
                                zst[0:64, zsl + qt * 128:zsl + (qt + 1) * 128],
                                psT2[0:64, 0:128]),
                                waits=[tt2, zst_free[g % 2]])
                        ps2b_free = tev
                        if qt == 3 and hh == 1:
                            tzd = LD(lambda s_, g=g, zp=zp, zsl=zsl: s_.dma_start(
                                out=zp[64:128, g * 512:(g + 1) * 512],
                                in_=zst[0:64, zsl:zsl + 512]),
                                waits=[tev, zall_free_all],
                                key="s_zst%d" % (g % 2))
                            zst_free[g % 2] = tzd
                            zdma_tok[(pv["pl"], g)] = tzd

                prev = pend_dance
                for g in range(NQG):
                    nk = 8 * (g + 1)
                    for hh in range(2):
                        KT = KtP[par][hh]
                        QT = QtP[par][hh]
                        h4 = 2 * (pr % 2) + hh
                        zi = zrot % 2
                        zrot += 1
                        psZX = psZ if zi == 0 else psO
                        psZX_f = psZ_free if zi == 0 else psO_free
                        av_last = None
                        pend_av = None
                        for kc in range(nk + 1):
                            if kc < nk:
                                spb = kc % 2
                                psSX = (psB, psB2)[spb]
                                slot = kc % 4
                                tmm = T(lambda tt, kc=kc, psSX=psSX, KT=KT, QT=QT,
                                        g=g: tt.matmul(
                                    psSX[:], KT[0:69, kc * 128:(kc + 1) * 128],
                                    QT[0:69, g * 512:(g + 1) * 512],
                                    start=True, stop=True),
                                    waits=[tdm_tok[(g, hh)], psS_free[spb],
                                           kq_scal] + kq_sync)
                                texp = C(lambda sc, slot=slot, psSX=psSX:
                                         sc.activation(
                                    probs[:, slot * 512:(slot + 1) * 512], psSX[:],
                                    AF.Exp, bias=0.0, scale=0.125),
                                    waits=[tmm, probs_free[slot]])
                                psS_free[spb] = texp
                                prb = texp
                                dg = kc - (8 * g + 4)
                                if 0 <= dg < 4:
                                    prb = V(lambda v, slot=slot, dg=dg:
                                            v.tensor_tensor(
                                        probs[:, slot * 512:(slot + 1) * 512],
                                        probs[:, slot * 512:(slot + 1) * 512],
                                        diag_bf[:, dg * 512:(dg + 1) * 512],
                                        OP.mult), waits=[prb, t_gmask])
                            if pend_av is not None:
                                kcp, slotp, prbp = pend_av
                                tav = T(lambda tt, kc=kcp, slot=slotp, h4=h4,
                                        nk=nk, psZX=psZX: tt.matmul(
                                    psZX[0:65, :],
                                    V4[:, kc * 260 + h4 * 65:kc * 260 + h4 * 65 + 65],
                                    probs[:, slot * 512:(slot + 1) * 512],
                                    start=(kc == 0), stop=(kc == nk - 1)),
                                    waits=[prbp, v4_all, psZX_f])
                                probs_free[slotp] = tav
                                av_last = tav
                            if kc < nk:
                                pend_av = (kc, slot, prb)
                            if prev is not None and kc < 8:
                                dance_step(prev, kc)
                            if kc < nk:
                                itc += 1
                                tgt = n_tasks * itc // 160
                                while t_next < tgt:
                                    tasks[t_next]()
                                    t_next += 1
                        grp_av_last = av_last
                        tze = C(lambda sc, psZX=psZX: sc.activation(
                            zraw[0:65, :], psZX[0:65, :], AF.Identity),
                            waits=[av_last, zraw_free])
                        if zi == 0:
                            psZ_free = tze
                        else:
                            psO_free = tze
                        prev = {"tze": tze, "g": g, "hh": hh, "zp": zallT[pr % 2],
                                "zsl": (g % 2) * 512, "pl": pr % 2}
                while t_next < n_tasks:
                    tasks[t_next]()
                    t_next += 1
                kq_free[par] = grp_av_last
                pend_dance = prev
                if pr % 2 == 1:
                    for k in range(8):
                        dance_step(pend_dance, k)
                    pend_dance = None


            op_last = None
            _nqt = 1 if _stop == 0.9 else 16
            for qt16 in range(_nqt):
                gq = qt16 // 4
                zw = [zev_tok.get((0, gq)), zev_tok.get((1, gq)),
                      zdma_tok.get((0, gq)), zdma_tok.get((1, gq))]
                for mh in range(2):
                    psX = psV if mh == 0 else psO
                    ps_free = psV_free if mh == 0 else psO_free
                    last = None
                    for pl in range(2):
                        last = T(lambda tt, psX=psX, qt16=qt16,
                                 mh=mh, pl=pl, grp=grp: tt.matmul(
                            psX[:],
                            zallT[pl][0:128, qt16 * 128:(qt16 + 1) * 128],
                            wo_sb[0:128,
                                  pl * 1024 + mh * 512:
                                  pl * 1024 + (mh + 1) * 512],
                            start=(pl == 0),
                            stop=(pl == 1 and (grp != 0 or _stop == 0.95))),
                            waits=[wo_ld[grp], ps_free] + zw)
                    if grp == 0 and _stop != 0.95:
                        last = T(lambda tt, mh=mh, psX=psX: tt.matmul(
                            psX[:], cones[0:1, 0:128], bo_bf[0:1, mh * 512:(mh + 1) * 512],
                            start=False, stop=True), waits=[t_gconst, t_ones])
                    aof = qt16 * 1024 + mh * 512
                    if grp == 0:
                        tev = V(lambda v, aof=aof, psX=psX: v.tensor_copy(
                            acc[:, aof:aof + 512], psX[:]), waits=[last])
                    else:
                        tev = V(lambda v, aof=aof, psX=psX: v.tensor_tensor(
                            acc[:, aof:aof + 512], acc[:, aof:aof + 512], psX[:],
                            OP.add), waits=[last, acc_tok[qt16][mh]])
                    acc_tok[qt16][mh] = tev
                    op_last = tev
                    if mh == 0:
                        psV_free = tev
                    else:
                        psO_free = tev
            zall_free_all = op_last
            v4_free = grp_av_last
            if _stop in (0.9, 0.95, 1):
                break
            if grp + 1 < 4:
                wo_free[0] = op_last
                load_wo(grp + 1)

        attn_fin = op_last

    with ExitStack() as fctx:
        sbf = lambda n, sh, dt=BF16: fctx.enter_context(nc.sbuf_tensor(n, sh, dt))
        wg_sb = sbf("wg_sb", [128, 2 * 4096])
        wu_sb = sbf("wu_sb", [128, 2 * 4096])
        wd_sb = sbf("wd_sb", [128, 2 * 4096])
        sil_sb = sbf("sil_sb", [128, 2 * 512])
        hT = sbf("hT", [128, 4 * 512])
        lgs = sbf("lgs", [128, 128], F32)
        v1s = sbf("v1s", [128, 16], F32)
        v2s = sbf("v2s", [128, 16], F32)
        d12 = sbf("d12", [128, 16], F32)
        p1s = sbf("p1s", [128, 16], F32)
        p2s = sbf("p2s", [128, 16], F32)
        eq1 = sbf("eq1", [128, 128], F32)
        eq2 = sbf("eq2", [128, 128], F32)
        rw1 = sbf("rw1", [128, 128], F32)
        rws = sbf("rws", [128, 128], F32)
        cbig = sbf("cbig", [128, 1], F32)
        xm_sb = sbf("xm_sb", [128, 2 * D], F32)
        out_sb = sbf("out_sb", [128, 2 * D], F32)

        wff_ld = [None] * 10
        wff_free = [attn_fin, attn_fin]

        def load_ffn(e):
            sl = e % 2
            t1 = WL(lambda g, e=e, sl=sl: g.dma_start(
                out=wg_sb[:, sl * 4096:(sl + 1) * 4096], in_=wgE[e]),
                waits=[wff_free[sl]], key="g_wg%d" % sl)
            t2 = WL(lambda g, e=e, sl=sl: g.dma_start(
                out=wu_sb[:, sl * 4096:(sl + 1) * 4096], in_=wuE[e]),
                waits=[wff_free[sl]], key="g_wu%d" % sl)
            t3 = WL(lambda g, e=e, sl=sl: g.dma_start(
                out=wd_sb[:, sl * 4096:(sl + 1) * 4096], in_=wdE[e]),
                waits=[wff_free[sl]], key="g_wd%d" % sl)
            wff_ld[e] = (t1, t2, t3)

        load_ffn(0)
        load_ffn(1)

        tmc = V(lambda v: v.memset(cbig[:], -1.0e9))
        psV_f = psV_free
        lg_last = None
        for t16 in range(16):
            g16, tl = t16 // 4, t16 % 4
            qof = (2 * g16 + 1) * 512 + tl * 128
            last = None
            for mc in range(NMC):
                last = T(lambda tt, qof=qof, mc=mc: tt.matmul(
                    psV[:, 0:8],
                    xnT[:, mc * S + qof:mc * S + qof + 128],
                    wr_bf[:, mc * 8:(mc + 1) * 8],
                    start=(mc == 0), stop=False), waits=[t_gconst, psV_f])
            last = T(lambda tt: tt.matmul(
                psV[:, 0:8], cones[0:1, 0:128], rb_bf[0:1, 0:8],
                start=False, stop=True), waits=[t_gconst, t_ones])
            lg_last = V(lambda v, t16=t16: v.tensor_copy(
                lgs[:, t16 * 8:(t16 + 1) * 8], psV[:, 0:8]), waits=[last])
            psV_f = lg_last
        lg3 = lgs[:].rearrange("p (t e) -> p t e", e=8)
        tv1 = V(lambda v: v.tensor_reduce(v1s[:].unsqueeze(2), lg3, AX, OP.max),
                waits=[lg_last])
        teq1 = V(lambda v: v.tensor_tensor(
            eq1[:].rearrange("p (t e) -> p t e", e=8), lg3,
            v1s[:].unsqueeze(2).broadcast_to([128, 16, 8]), OP.is_equal),
            waits=[tv1])
        tl2 = V(lambda v: v.scalar_tensor_tensor(
            rw1[:], eq1[:], cbig[:, 0:1], lgs[:], OP.mult, OP.add),
            waits=[tmc, teq1])
        tv2 = V(lambda v: v.tensor_reduce(
            v2s[:].unsqueeze(2), rw1[:].rearrange("p (t e) -> p t e", e=8), AX,
            OP.max), waits=[tl2])
        teq2 = V(lambda v: v.tensor_tensor(
            eq2[:].rearrange("p (t e) -> p t e", e=8), lg3,
            v2s[:].unsqueeze(2).broadcast_to([128, 16, 8]), OP.is_equal),
            waits=[tv2])
        td = V(lambda v: v.tensor_sub(d12[:], v1s[:], v2s[:]), waits=[tv2])
        tp1 = C(lambda sc: sc.activation(p1s[:], d12[:], AF.Sigmoid), waits=[td])
        tp2 = C(lambda sc: sc.activation(p2s[:], d12[:], AF.Sigmoid, scale=-1.0),
                waits=[td])
        tr1 = V(lambda v: v.tensor_tensor(
            rw1[:].rearrange("p (t e) -> p t e", e=8),
            eq1[:].rearrange("p (t e) -> p t e", e=8),
            p1s[:].unsqueeze(2).broadcast_to([128, 16, 8]), OP.mult), waits=[tp1])
        tr2 = V(lambda v: v.tensor_tensor(
            eq2[:].rearrange("p (t e) -> p t e", e=8),
            eq2[:].rearrange("p (t e) -> p t e", e=8),
            p2s[:].unsqueeze(2).broadcast_to([128, 16, 8]), OP.mult), waits=[tp2])
        trw = V(lambda v: v.tensor_add(rws[:], rw1[:], eq2[:]), waits=[tr1, tr2])

        if _stop == 3:
            return _finish({"rws": rws[:], "lgs": lgs[:], "acc": acc[:]})
        gA_f = [psA_free]
        gU_f = [psS_free[0]]
        dn_banks = [psB2, psZ, psV, psO]
        dn_f = [psS_free[1], psZ_free, psV_free, psO_free]
        sil_free = [None, None]
        hT_free = [None] * 4
        for e in range(10):
            sl = e % 2
            dn_last = None
            for g in range(NQG):
                qof = (2 * g + 1) * 512
                for fs in range(4):
                    fp = 0
                    psG = psA
                    psU = psB
                    gl = None
                    ul = None
                    for mc in range(NMC):
                        gl = T(lambda tt, sl=sl, mc=mc, fs=fs, qof=qof, psG=psG:
                               tt.matmul(
                            psG[:],
                            wg_sb[:, sl * 4096 + (mc * 4 + fs) * 128:
                                  sl * 4096 + (mc * 4 + fs) * 128 + 128],
                            xnT[:, mc * S + qof:mc * S + qof + 512],
                            start=(mc == 0), stop=(mc == NMC - 1)),
                            waits=[wff_ld[e][0], gA_f[fp], xn_all])
                    for mc in range(NMC):
                        ul = T(lambda tt, sl=sl, mc=mc, fs=fs, qof=qof, psU=psU:
                               tt.matmul(
                            psU[:],
                            wu_sb[:, sl * 4096 + (mc * 4 + fs) * 128:
                                  sl * 4096 + (mc * 4 + fs) * 128 + 128],
                            xnT[:, mc * S + qof:mc * S + qof + 512],
                            start=(mc == 0), stop=(mc == NMC - 1)),
                            waits=[wff_ld[e][1], gU_f[fp]])
                    ssl = fs % 2
                    tsil = C(lambda sc, ssl=ssl, psG=psG: sc.activation(
                        sil_sb[:, ssl * 512:(ssl + 1) * 512], psG[:], AF.Silu),
                        waits=[gl, sil_free[ssl]])
                    gA_f[fp] = tsil
                    th = V(lambda v, fs=fs, ssl=ssl, psU=psU: v.tensor_tensor(
                        hT[:, fs * 512:(fs + 1) * 512],
                        sil_sb[:, ssl * 512:(ssl + 1) * 512], psU[:], OP.mult),
                        waits=[tsil, ul, hT_free[fs]])
                    gU_f[fp] = th
                    sil_free[ssl] = th
                    hT_free[fs] = th
                for tl4 in range(4):
                    t16 = g * 4 + tl4
                    for mh in range(2):
                        dbi = (tl4 * 2 + mh) % 4
                        psX = dn_banks[dbi]
                        psX_f = dn_f[dbi]
                        last = None
                        for fc in range(4):
                            last = T(lambda tt, sl=sl, fc=fc, tl4=tl4, mh=mh,
                                     psX=psX: tt.matmul(
                                psX[:],
                                hT[:, fc * 512 + tl4 * 128:fc * 512 + tl4 * 128 + 128],
                                wd_sb[:, sl * 4096 + fc * 1024 + mh * 512:
                                      sl * 4096 + fc * 1024 + (mh + 1) * 512],
                                start=(fc == 0), stop=(fc == 3)),
                                waits=[wff_ld[e][2], psX_f, hT_free[fc]])
                        aof = t16 * 1024 + mh * 512
                        if e < 8:
                            tadd = V(lambda v, aof=aof, psX=psX, t16=t16, e=e:
                                     v.scalar_tensor_tensor(
                                acc[:, aof:aof + 512], psX[:],
                                rws[:, t16 * 8 + e:t16 * 8 + e + 1],
                                acc[:, aof:aof + 512], OP.mult, OP.add),
                                waits=[last, trw, acc_tok[t16][mh]])
                        else:
                            tadd = V(lambda v, aof=aof, psX=psX: v.tensor_tensor(
                                acc[:, aof:aof + 512], acc[:, aof:aof + 512],
                                psX[:], OP.add), waits=[last, acc_tok[t16][mh]])
                        acc_tok[t16][mh] = tadd
                        dn_last = tadd
                        dn_f[dbi] = tadd
                for fc in range(4):
                    hT_free[fc] = dn_last
            if e + 2 < 10:
                wff_free[sl] = dn_last
                load_ffn(e + 2)

        xm_free = [None, None]
        os_free = [None, None]
        for t16 in range(16):
            sl = t16 % 2
            g16, tl4 = t16 // 4, t16 % 4
            row0 = (2 * g16 + 1) * 512 + tl4 * 128
            tx = LD(lambda s, sl=sl, row0=row0: s.dma_start(
                out=xm_sb[:, sl * D:(sl + 1) * D], in_=xkv[row0:row0 + 128, :]),
                waits=[xm_free[sl]], key="s_xm%d" % sl)
            tme = V(lambda v, sl=sl, t16=t16: v.tensor_tensor(
                out_sb[:, sl * D:(sl + 1) * D], xm_sb[:, sl * D:(sl + 1) * D],
                acc[:, t16 * D:(t16 + 1) * D], OP.add),
                waits=[tx, acc_tok[t16][0], acc_tok[t16][1], os_free[sl]])
            xm_free[sl] = tme
            tst = LD(lambda s, sl=sl, t16=t16: s.dma_start(
                out=out_hbm[t16 * 128:(t16 + 1) * 128, :],
                in_=out_sb[:, sl * D:(sl + 1) * D]), waits=[tme],
                key="s_out%d" % sl)
            os_free[sl] = tst

        em.wait_only("sync", em.all_keys() + [(e, em.cnt[e]) for e in CENGS])

    sems = make_sems()
    with nc.Block() as block:
        emit(em, block, sems)
    return nc


_PROG = None


def _get_prog():
    global _PROG
    if _PROG is None:
        _PROG = build_program()
    return _PROG


def _host_layouts(inp):
    f = np.float32
    WQ = np.asarray(inp["L0_attn_W_Q"], f)
    WK = np.asarray(inp["L0_attn_W_K"], f)
    WV = np.asarray(inp["L0_attn_W_V"], f)
    WO = np.asarray(inp["L0_attn_W_O"], f)
    bQ = np.asarray(inp["L0_attn_b_Q"], f)
    bK = np.asarray(inp["L0_attn_b_K"], f)
    bV = np.asarray(inp["L0_attn_b_V"], f)
    bO = np.asarray(inp["L0_attn_b_O"], f)
    Wr = np.asarray(inp["L0_ffn_Wr"], f)
    Wg = np.asarray(inp["L0_ffn_Wg"], f)
    Wu = np.asarray(inp["L0_ffn_Wu"], f)
    Wd = np.asarray(inp["L0_ffn_Wd"], f)
    Sg = np.asarray(inp["L0_ffn_Sg"], f)
    Su = np.asarray(inp["L0_ffn_Su"], f)
    Sd = np.asarray(inp["L0_ffn_Sd"], f)
    eb = np.asarray(inp["L0_ffn_expert_bias"], f)

    d = {}
    d["wqT"] = np.ascontiguousarray(
        WQ.reshape(8, 2, 8, 128, 64).transpose(3, 0, 2, 1, 4).reshape(128, 8192))
    d["wkT"] = np.ascontiguousarray(
        WK.reshape(8, 2, 8, 128, 64).transpose(3, 0, 2, 1, 4).reshape(128, 8192))
    d["wv4"] = np.ascontiguousarray(
        WV.reshape(4, 4, 8, 128, 64).transpose(3, 0, 2, 1, 4).reshape(128, 8192))
    d["woT"] = np.ascontiguousarray(
        WO.reshape(4, 2, 2, 64, 1024).transpose(2, 3, 0, 1, 4).reshape(128, 8192))
    d["bqk"] = np.ascontiguousarray(
        np.stack([bQ.reshape(8, 2 * 64), bK.reshape(8, 2 * 64)], axis=2)
        .transpose(1, 0, 2).reshape(128, 16))
    d["bvr"] = bV.reshape(1, 1024).copy()
    d["bor"] = bO.reshape(1, 1024).copy()
    d["wrR"] = np.ascontiguousarray(
        Wr.reshape(8, 128, 8).transpose(1, 0, 2).reshape(128, 64))
    d["rbr"] = eb.reshape(1, 8).copy()

    def gate_layout(w):
        return np.ascontiguousarray(
            w.reshape(-1, 8, 128, 4, 128).transpose(0, 2, 1, 3, 4).reshape(-1, 128, 4096))

    def down_layout(w):
        return np.ascontiguousarray(
            w.reshape(-1, 4, 128, 1024).transpose(0, 2, 1, 3).reshape(-1, 128, 4096))

    g_sh = np.stack([Sg[:, 0:512], Sg[:, 512:1024]], 0)
    u_sh = np.stack([Su[:, 0:512], Su[:, 512:1024]], 0)
    dd_sh = np.stack([Sd[0:512, :], Sd[512:1024, :]], 0)
    d["wgE"] = np.concatenate([gate_layout(Wg), gate_layout(g_sh)], 0)
    d["wuE"] = np.concatenate([gate_layout(Wu), gate_layout(u_sh)], 0)
    d["wdE"] = np.concatenate([down_layout(Wd), down_layout(dd_sh)], 0)

    dg = np.zeros((128, 2048), np.float32)
    for c in range(4):
        for p in range(128):
            dg[p, c * 512 + c * 128 + p:(c + 1) * 512] = 1.0
    d["diag"] = dg
    d["ident"] = np.eye(128, dtype=np.float32)
    return d


_PERM_B = [1, 0, 3, 2, 5, 4, 7, 6]


def kernel(**inputs):
    x = np.asarray(inputs["L0_attn_x"], np.float32)
    shared = _host_layouts(inputs)
    nc = _get_prog()

    qi = np.zeros((4, OWN), np.float32)
    for g in range(4):
        qi[g, g * 512:(g + 1) * 512] = -1.0

    km_A = np.zeros((4, S), np.float32)
    km_B = np.zeros((4, S), np.float32)
    for g in range(4):
        km_B[g, 2 * g * 512:(2 * g + 1) * 512] = BIGMASK

    in_maps = []
    for core in range(8):
        b = core // 2
        role = core % 2
        if role == 0:
            xc = np.ascontiguousarray(x[b])
            km = km_A
        else:
            xc = np.ascontiguousarray(x[b].reshape(8, 512, D)[_PERM_B].reshape(S, D))
            km = km_B
        m = dict(shared)
        m["xkv"] = xc
        m["kmask"] = km
        m["qind"] = qi
        in_maps.append(m)

    res = run_bass_kernel_spmd(nc, in_maps, core_ids=list(range(8)))
    y = np.empty((B, S, D), np.float32)
    for core in range(8):
        b = core // 2
        role = core % 2
        o = np.asarray(res.results[core]["out"], np.float32)
        for g in range(4):
            blk = 2 * g + 1 if role == 0 else 2 * g
            y[b, blk * 512:(blk + 1) * 512, :] = o[g * 512:(g + 1) * 512, :]
    return y


# version 14
# speedup vs baseline: 1.0289x; 1.0168x over previous
import numpy as np
from contextlib import ExitStack

import concourse.bass as bass
import concourse.mybir as mybir
from concourse.bass_utils import run_bass_kernel_spmd

F32 = mybir.dt.float32
BF16 = mybir.dt.bfloat16
AX = mybir.AxisListType.X
OP = mybir.AluOpType
AF = mybir.ActivationFunctionType

B, S, D, H, DH = 4, 4096, 1024, 16, 64
NMC = D // 128
NKT = S // 128
NQG = 4
OWN = 2048
EPS = 1e-5
BIGMASK = 1.0e5

ENGS = ("sync", "gpsimd", "tensor", "vector", "scalar")
CENGS = ("tensor", "vector", "scalar")


class Em:

    def __init__(self):
        self.items = {e: [] for e in ENGS}
        self.cnt = {e: 0 for e in CENGS}
        self.kcnt = {}

    def op(self, eng, fn, waits=(), key=None):
        for w in waits:
            if w is not None:
                top = self.cnt.get(w[0], None)
                if top is None:
                    top = self.kcnt[w[0]]
                assert w[1] <= top, (eng, w)
        if eng in CENGS:
            assert key is None
            self.cnt[eng] += 1
            tok = (eng, self.cnt[eng])
            self.items[eng].append((fn, [w for w in waits if w is not None], 1, eng))
        else:
            assert key is not None
            self.kcnt[key] = self.kcnt.get(key, 0) + 16
            tok = (key, self.kcnt[key])
            self.items[eng].append((fn, [w for w in waits if w is not None], 16, key))
        return tok

    def wait_only(self, eng, waits):
        self.items[eng].append((None, [w for w in waits if w is not None], 0, None))

    def all_keys(self):
        return [(k, v) for k, v in self.kcnt.items()]


def emit(em, block, sems):
    def runner(ename):
        def run(engine):
            last = {}
            cnt = 0
            for fn, waits, inc, key in em.items[ename]:
                for (we, wv) in waits:
                    if we == ename and fn is not None:
                        if ename in ("vector", "scalar") and cnt - wv <= 1 \
                                and wv > last.get(we, 0):
                            engine.wait_ge(sems[we], wv)
                            last[we] = wv
                        continue
                    if wv > last.get(we, 0) or fn is None:
                        engine.wait_ge(sems[we], wv)
                        last[we] = max(last.get(we, 0), wv)
                if fn is not None:
                    ins = fn(engine)
                    ins.then_inc(sems[key], inc)
                    if key == ename:
                        cnt += inc
        return run

    block.sync(runner("sync"))
    block.gpsimd(runner("gpsimd"))
    block.tensor(runner("tensor"))
    block.vector(runner("vector"))
    block.scalar(runner("scalar"))


def build_program(stage="all", taps=()):
    nc = bass.Bass("TRN2", target_bir_lowering=False, debug=False)

    dt_in = lambda n, sh, dt=F32: nc.dram_tensor(n, sh, dt, kind="ExternalInput").ap()
    xkv = dt_in("xkv", [S, D])
    wqT = dt_in("wqT", [128, 8192])
    wkT = dt_in("wkT", [128, 8192])
    wv4 = dt_in("wv4", [128, 8192])
    woT = dt_in("woT", [128, 8192])
    bqk = dt_in("bqk", [128, 16])
    bvr = dt_in("bvr", [1, 1024])
    bor = dt_in("bor", [1, 1024])
    wrR = dt_in("wrR", [128, 64])
    rbr = dt_in("rbr", [1, 8])
    wgE = dt_in("wgE", [10, 128, 4096])
    wuE = dt_in("wuE", [10, 128, 4096])
    wdE = dt_in("wdE", [10, 128, 4096])
    kmask = dt_in("kmask", [4, S])
    qind = dt_in("qind", [4, OWN])
    diag = dt_in("diag", [128, 2048])
    ident = dt_in("ident", [128, 128])
    out_hbm = nc.dram_tensor("out", [OWN, D], F32, kind="ExternalOutput").ap()
    tap_hbm = {}
    for tname, tshape, tdt in taps:
        tap_hbm[tname] = nc.dram_tensor("tap_" + tname, tshape, tdt,
                                        kind="ExternalOutput").ap()

    em = Em()
    T = lambda fn, waits=(): em.op("tensor", fn, waits)
    V = lambda fn, waits=(): em.op("vector", fn, waits)
    C = lambda fn, waits=(): em.op("scalar", fn, waits)
    LD = lambda fn, waits=(), key=None: em.op("sync", fn, waits, key=key)
    WL = lambda fn, waits=(), key=None: em.op("gpsimd", fn, waits, key=key)

    ctx = ExitStack()
    sb = lambda n, sh, dt=BF16: ctx.enter_context(nc.sbuf_tensor(n, sh, dt))
    psd = lambda n, sh, dt=F32: ctx.enter_context(nc.psum_tensor(n, sh, dt))

    def make_sems():
        sems = {e: ctx.enter_context(nc.semaphore("sem_" + e)) for e in CENGS}
        for k in em.kcnt:
            sems[k] = ctx.enter_context(nc.semaphore("sem_" + k))
        return sems

    xnT = sb("xnT", [128, NMC * S])
    acc = sb("acc", [128, 16 * D])
    id_bf = sb("id_bf", [128, 128])
    id_f32 = sb("id_f32", [128, 128], F32)
    cones = sb("cones", [9, 256])
    bqk_sb = sb("bqk_sb", [128, 16], F32)
    bv_bf = sb("bv_bf", [9, 1088])
    bo_bf = sb("bo_bf", [9, 1088])
    rb_bf = sb("rb_bf", [9, 72])
    wr_bf = sb("wr_bf", [128, 128])
    epsc = sb("epsc", [128, 1], F32)

    psA = psd("psA", [128, 512])
    psB = psd("psB", [128, 512])
    psB2 = psd("psB2", [128, 512])
    psZ = psd("psZ", [128, 512])
    psV = psd("psV", [128, 512])
    psO = psd("psO", [128, 512])
    psT1 = psd("psT1", [128, 128])
    psT2 = psd("psT2", [128, 512], BF16)

    WL(lambda g: g.dma_start(out=id_bf[:], in_=ident), key="g_const")
    LD(lambda s: s.dma_start(out=id_f32[:], in_=ident), key="s_const")
    LD(lambda s: s.dma_start(out=bqk_sb[:], in_=bqk), key="s_const")
    WL(lambda g: g.dma_start(out=bv_bf[0:1, 0:1024], in_=bvr), key="g_const")
    WL(lambda g: g.dma_start(out=bo_bf[0:1, 0:1024], in_=bor), key="g_const")
    WL(lambda g: g.dma_start(out=rb_bf[0:1, 0:8], in_=rbr), key="g_const")
    WL(lambda g: g.dma_start(out=wr_bf[:, 0:64], in_=wrR), key="g_const")
    onesf = sb("onesf", [128, 1], F32)
    t_ones = V(lambda v: v.memset(cones[0:1, 0:256], 1.0))
    t_eps = V(lambda v: v.memset(epsc[:], EPS))
    t_onesf = V(lambda v: v.memset(onesf[:], 1.0))
    t_gconst = ("g_const", em.kcnt["g_const"])
    t_sconst = ("s_const", em.kcnt["s_const"])
    t_id = t_idf = t_bqk = t_bv = t_bo = t_rb = t_wr = None

    acc_tok = [[None, None] for _ in range(16)]

    with ExitStack() as actx:
        sba = lambda n, sh, dt=BF16: actx.enter_context(nc.sbuf_tensor(n, sh, dt))
        nmean = sba("nmean", [128, 2], F32)
        nmean1 = sba("nmean1", [128, 2], F32)
        varn = sba("varn", [128, 2], F32)
        sq_s = sba("sq_s", [128, 2], F32)
        rstd = sba("rstd", [128, 2], F32)
        diag_bf = sba("diag_bf", [128, 2048])
        wq_sb = sba("wq_sb", [128, 2 * 1024])
        wk_sb = sba("wk_sb", [128, 2 * 1024])
        wv_sb = sba("wv_sb", [128, 2048])
        wo_sb = sba("wo_sb", [128, 2112])
        V4 = sba("V4", [128, NKT * 260])
        KtP = [[sba("Kt%d%d" % (par, hh), [69, S]) for hh in range(2)]
               for par in range(2)]
        QtP = [[sba("Qt%d%d" % (par, hh), [69, OWN]) for hh in range(2)]
               for par in range(2)]
        pstage = sba("pstage", [128, 2 * 512])
        onescol = sba("onescol", [128, 8])
        kmx_c = sba("kmx_c", [128, 1], F32)
        kmax_sb = sba("kmax_sb", [9, 8])
        kb_sb = sba("kb_sb", [128, 2], F32)
        negc = sba("negc", [128, 32], F32)
        zguard = sba("zguard", [128, 1], F32)
        zt_sb = sba("zt_sb", [128, 64])
        rzc = sba("rzc", [128, 1], F32)
        pctx = ExitStack()
        sbp = lambda n, sh, dt=BF16: pctx.enter_context(nc.sbuf_tensor(n, sh, dt))
        xst = sbp("xst", [128, 2 * D], F32)
        lntr = sbp("lntr", [128, D], F32)
        xnt_st = sbp("xnt_st", [128, 2 * D])

        WL(lambda g: g.dma_start(out=diag_bf[:], in_=diag), key="g_mask")

        t_onescol = V(lambda v: v.memset(onescol[:, 0:8], 1.0))
        ktc = []
        for par in range(2):
            for hh in range(2):
                ktc.append(V(lambda v, par=par, hh=hh: v.memset(
                    KtP[par][hh][64:65, :], 1.0)))
                WL(lambda g, par=par, hh=hh: g.dma_start(
                    out=KtP[par][hh][65:69, :], in_=kmask), key="g_mask")
                WL(lambda g, par=par, hh=hh: g.dma_start(
                    out=QtP[par][hh][65:69, :], in_=qind), key="g_mask")
        t_gmask = ("g_mask", em.kcnt["g_mask"])
        qtc = [t_gmask]

        wq_free = [None, None]
        wk_free = [None, None]
        wv_free = [None, None]
        wo_free = [None, None]
        wq_ld = [None] * 8
        wk_ld = [None] * 8
        wv_ld = [None] * 4
        wo_ld = [None] * 4
        kq_free = [None, None]
        v4_free = None
        psS_free = [None, None]
        psA_free = None
        psZ_free = None
        psV_free = None
        psO_free = None
        psT1_free = None
        kmx_free = None
        kmaxsb_free = None
        kb_free = None
        negc_free = None
        zg_free = None
        zrot = 0
        pend_dance = None
        zdma_last = None
        mst_free = []
        ps2b_free = None
        pst_free = [None, None]
        probs_free = [None] * 4
        prep_tok = {}
        zev_tok = {}
        zdma_tok = {}

        zraw_free = None
        zt_free = None
        zall_free_all = None
        zst_free = [None, None]
        vdir_last = [None] * 4

        def load_wqk(p):
            slp = p % 2
            wq_ld[p] = WL(lambda g, p=p, slp=slp: g.dma_start(
                out=wq_sb[:, slp * 1024:(slp + 1) * 1024],
                in_=wqT[:, p * 1024:(p + 1) * 1024]), waits=[wq_free[slp]],
                key="g_wq%d" % slp)
            wk_ld[p] = WL(lambda g, p=p, slp=slp: g.dma_start(
                out=wk_sb[:, slp * 1024:(slp + 1) * 1024],
                in_=wkT[:, p * 1024:(p + 1) * 1024]), waits=[wk_free[slp]],
                key="g_wk%d" % slp)

        def load_wv(grp):
            wv_ld[grp] = WL(lambda g, grp=grp: g.dma_start(
                out=wv_sb[:],
                in_=wv4[:, grp * 2048:(grp + 1) * 2048]), waits=[wv_free[0]],
                key="g_wv")

        def load_wo(grp):
            wo_ld[grp] = WL(lambda g, grp=grp: g.dma_start(
                out=wo_sb[0:128, 0:2048],
                in_=woT[:, grp * 2048:(grp + 1) * 2048]), waits=[wo_free[0]],
                key="g_wo")

        load_wqk(0)
        load_wqk(1)
        load_wv(0)
        load_wo(0)

        vm0 = V(lambda v: v.memset(V4[:], 1.0))
        vdir_evac0 = None

        xst_free = [None, None]
        xnst_free = [None, None]
        ps2_free = None
        prev_sq = [None, None]
        xn_ready = [None] * NKT
        stats_tok = [None] * NKT
        vdir_evac0 = None

        def ln_tail(t):
            nonlocal ps2_free, psV_free, psO_free, vdir_evac0
            sl = t % 2
            x_sl = xst[:, sl * D:(sl + 1) * D]
            xn_sl = xnt_st[:, sl * D:(sl + 1) * D]
            t_sq, t_sqrt = stats_tok[t]
            t_rcp = V(lambda v, sl=sl: v.reciprocal(rstd[:, sl:sl + 1],
                                                    sq_s[:, sl:sl + 1]),
                      waits=[t_sqrt])
            t_app = V(lambda v, sl=sl, x_sl=x_sl, xn_sl=xn_sl: v.tensor_scalar(
                xn_sl, x_sl, nmean1[:, sl:sl + 1], rstd[:, sl:sl + 1],
                OP.add, OP.mult),
                waits=[xnst_free[sl], t_sq, t_rcp])
            xst_free[sl] = t_app
            evl = None
            for half in range(2):
                tps = []
                for b4 in range(4):
                    mc = half * 4 + b4
                    tps.append(T(lambda tt, mc=mc, b4=b4, xn_sl=xn_sl: tt.transpose(
                        psT2[:, b4 * 128:(b4 + 1) * 128],
                        xn_sl[:, mc * 128:(mc + 1) * 128], id_bf[:]),
                        waits=[t_app, t_gconst, ps2_free]))

                def ev(v, half=half, t=t):
                    o = xnT[:].rearrange("p (mc s) -> p mc s", s=S)[
                        :, half * 4:half * 4 + 4, t * 128:(t + 1) * 128]
                    i = psT2[:].rearrange("p (b s) -> p b s", s=128)
                    return v.tensor_copy(o, i)
                evl = V(ev, waits=tps)
                ps2_free = evl
            xnst_free[sl] = evl
            xn_ready[t] = evl
            kt = t
            psX = psV if kt % 2 == 0 else psO
            psX_f = psV_free if kt % 2 == 0 else psO_free
            for mc in range(NMC):
                T(lambda tt, kt=kt, mc=mc, psX=psX: tt.matmul(
                    psX[:, 0:256],
                    xnT[:, mc * S + kt * 128:mc * S + (kt + 1) * 128],
                    wv_sb[:, mc * 256:(mc + 1) * 256],
                    start=(mc == 0), stop=False),
                    waits=[xn_ready[kt], wv_ld[0], psX_f])
            trk = T(lambda tt, psX=psX: tt.matmul(
                psX[:, 0:256], cones[0:1, 0:128], bv_bf[0:1, 0:256],
                start=False, stop=True), waits=[t_gconst, t_ones])
            tev = V(lambda v, kt=kt, psX=psX: v.tensor_copy(
                V4[:, kt * 260:(kt + 1) * 260].rearrange(
                    "p (h d) -> p h d", d=65)[:, :, 0:64],
                psX[:, 0:256].rearrange("p (h d) -> p h d", d=64)),
                waits=[trk, vm0])
            if kt % 2 == 0:
                psV_free = tev
            else:
                psO_free = tev
            vdir_evac0 = tev

        for t in range(NKT):
            sl = t % 2
            x_sl = xst[:, sl * D:(sl + 1) * D]
            t_load = LD(lambda s, t=t, x_sl=x_sl: s.dma_start(
                out=x_sl, in_=xkv[t * 128:(t + 1) * 128, :]), waits=[xst_free[sl]],
                key="s_x%d" % sl)
            t_red = V(lambda v, sl=sl, x_sl=x_sl: v.reduce_sum(
                nmean[:, sl:sl + 1], x_sl, AX), waits=[t_load])
            t_nm = V(lambda v, sl=sl: v.tensor_single_scalar(
                nmean1[:, sl:sl + 1], nmean[:, sl:sl + 1], -1.0 / D, OP.mult),
                waits=[prev_sq[sl], t_red])
            t_sq = C(lambda sc, sl=sl, x_sl=x_sl: sc.activation(
                lntr[:], x_sl, AF.Square, bias=nmean1[:, sl:sl + 1], scale=1.0,
                accum_out=varn[:, sl:sl + 1]), waits=[t_nm])
            prev_sq[sl] = t_sq
            t_sqrt = C(lambda sc, sl=sl: sc.activation(
                sq_s[:, sl:sl + 1], varn[:, sl:sl + 1], AF.Sqrt,
                bias=epsc[:, 0:1], scale=1.0 / D), waits=[t_eps, t_sq])
            stats_tok[t] = (t_sq, t_sqrt)
            if t >= 1:
                ln_tail(t - 1)
        ln_tail(NKT - 1)
        xn_all = xn_ready[NKT - 1]
        vdir_last[0] = vdir_evac0
        ps2b_free = ps2_free
        pctx.close()
        probs = sba("probs", [128, 4 * 512])
        mstage = sba("mstage", [32, 128], F32)
        zraw = sba("zraw", [65, 512], F32)
        zallT = [sba("zallT%d" % i, [128, OWN]) for i in range(2)]
        zst = sba("zst", [69, 2 * 512])
        sqstage = sba("sqstage", [128, 1024])
        t_sqz = V(lambda v: v.memset(sqstage[64:128, :], 0.0), waits=[xn_all])

        def build_prep(p):
            par = p % 2
            st = {}
            tasks = []

            def proj_mm_unit(kind, kc, u):
                def f():
                    last = None
                    for mc in (2 * u, 2 * u + 1):
                        if kind == "k":
                            lhs, rhs_col = wk_sb, kc * 512
                            xw, wld = xn_ready[kc * 4 + 3], wk_ld[p]
                        else:
                            lhs, rhs_col = wq_sb, (2 * kc + 1) * 512
                            xw, wld = xn_all, wq_ld[p]
                        last = T(lambda tt, mc=mc, lhs=lhs, rhs_col=rhs_col:
                                 tt.matmul(
                            psV[:],
                            lhs[:, par * 1024 + mc * 128:par * 1024 + (mc + 1) * 128],
                            xnT[:, mc * S + rhs_col:mc * S + rhs_col + 512],
                            start=(mc == 0), stop=(mc == NMC - 1)),
                            waits=[xw, wld, psV_free])
                    st[("mm", kind, kc)] = last
                return f

            def proj_evac(kind, kc, ci):
                def f():
                    nonlocal psV_free
                    sps = ci % 2
                    cc = 2 * p + (1 if kind == "k" else 0)
                    dA = KtP[par][0] if kind == "k" else QtP[par][0]
                    dB = KtP[par][1] if kind == "k" else QtP[par][1]
                    col = kc * 512
                    mm = st[("mm", kind, kc)]
                    te0 = V(lambda v: v.tensor_scalar(
                        dA[0:64, col:col + 512], psV[0:64, :],
                        bqk_sb[0:64, cc:cc + 1], onesf[0:64, 0:1],
                        OP.add, OP.mult),
                        waits=[mm, t_sconst, t_onesf, t_gmask, kq_free[par]]
                        + ktc)
                    te1 = V(lambda v: v.tensor_scalar(
                        pstage[64:128, sps * 512:(sps + 1) * 512], psV[64:128, :],
                        bqk_sb[64:128, cc:cc + 1], onesf[64:128, 0:1],
                        OP.add, OP.mult),
                        waits=[pst_free[sps]])
                    psV_free = te1
                    tsh = LD(lambda s: s.dma_start(
                        out=dB[0:64, col:col + 512],
                        in_=pstage[64:128, sps * 512:(sps + 1) * 512]),
                        waits=[te1, kq_free[par]], key="s_kq%d" % sps)
                    pst_free[sps] = tsh
                    st["evac_last"] = te1
                return f

            def t_bar(final):
                def f():
                    st["sync"] = [("s_kq0", em.kcnt.get("s_kq0", 0)),
                                  ("s_kq1", em.kcnt.get("s_kq1", 0))]
                    st["scal"] = st["evac_last"]
                    if final:
                        prep_tok[p]["kq_scal"] = st["evac_last"]
                        prep_tok[p]["kq_sync"] = list(st["sync"])
                        if p + 2 < 8:
                            wq_free[par] = st[("mm", "q", 3)]
                            wk_free[par] = st[("mm", "q", 3)]
                            load_wqk(p + 2)
                return f

            def t_square(src, soff):
                def f():
                    st["sq"] = V(lambda v: v.tensor_mul(
                        sqstage[0:64, 0:1024], src[0:64, soff:soff + 1024],
                        src[0:64, soff:soff + 1024]),
                        waits=[t_sqz, st.get("mm_last"), st["scal"]]
                        + st["sync"])
                return f

            def t_normmm(col0, u):
                def f():
                    for c8 in range(4 * u, 4 * u + 4):
                        st["mm_last"] = T(lambda tt, c8=c8: tt.matmul(
                            psA[:, col0 + c8:col0 + c8 + 1],
                            sqstage[0:128, c8 * 128:(c8 + 1) * 128],
                            onescol[0:128, 0:1], start=True, stop=True),
                            waits=[st["sq"], psA_free, t_onescol])
                return f

            def norm_job(kind, hh, half):
                if kind == "k":
                    src, col0 = KtP[par][hh], hh * 32 + half * 8
                else:
                    src, col0 = QtP[par][hh], 64 + hh * 16 + half * 8
                return [t_square(src, half * 1024),
                        t_normmm(col0, 0), t_normmm(col0, 1)]

            def t_kmax():
                nonlocal psT1_free, kmx_free
                for hh in range(2):
                    tr1 = V(lambda v, hh=hh: v.reduce_max(
                        kmx_c[:, 0:1], psA[:, hh * 32:(hh + 1) * 32], AX),
                        waits=[st["mm_last"], kmx_free, kmaxsb_free])
                    ttp = T(lambda tt: tt.transpose(
                        psT1[0:1, 0:128], kmx_c[:, 0:1], id_f32[:]),
                        waits=[tr1, psT1_free, t_sconst])
                    kmx_free = ttp
                    tr2 = V(lambda v, hh=hh: v.reduce_max(
                        kmax_sb[0:1, hh:hh + 1], psT1[0:1, 0:128], AX),
                        waits=[ttp])
                    psT1_free = tr2
                    st["km%d" % hh] = tr2

            def t_bcast():
                nonlocal psT1_free, psA_free, kmaxsb_free, kb_free
                tkb = T(lambda tt: tt.matmul(
                    psT1[0:128, 0:2], cones[0:1, 0:128], kmax_sb[0:1, 0:2],
                    start=True, stop=True),
                    waits=[st["km0"], st["km1"], psT1_free, t_ones])
                kmaxsb_free = tkb
                tkbe = V(lambda v: v.tensor_copy(kb_sb[:, 0:2], psT1[0:128, 0:2]),
                         waits=[tkb, kb_free])
                psT1_free = tkbe
                tpr = V(lambda v: v.tensor_tensor(
                    negc[:].rearrange("p (h q) -> p h q", q=16),
                    psA[:, 64:96].rearrange("p (h q) -> p h q", q=16),
                    kb_sb[:, 0:2].unsqueeze(2).broadcast_to([128, 2, 16]),
                    OP.mult),
                    waits=[tkbe, st["mm_last"], negc_free])
                kb_free = tpr
                psA_free = tpr
                tnc = C(lambda sc: sc.activation(negc[:], negc[:], AF.Sqrt),
                        waits=[tpr])
                st["negc"] = V(lambda v: v.tensor_single_scalar(
                    negc[:], negc[:], -1.0, OP.mult), waits=[tnc])

            def t_mrow():
                nonlocal psT1_free, negc_free, mst_free
                ttr = T(lambda tt: tt.transpose(
                    psT1[0:32, 0:128], negc[:, 0:32], id_f32[:]),
                    waits=[st["negc"], psT1_free, t_sconst])
                negc_free = ttr
                tcp = V(lambda v: v.tensor_copy(mstage[0:32, 0:128],
                                                psT1[0:32, 0:128]),
                        waits=[ttr] + mst_free)
                psT1_free = tcp
                tdmd = {}
                for g in range(NQG):
                    for hh in range(2):
                        QTh = QtP[par][hh]
                        tdmd[(g, hh)] = WL(lambda g_, g=g, hh=hh, QTh=QTh:
                                           g_.dma_start(
                            out=QTh[64:65, g * 512:(g + 1) * 512],
                            in_=mstage[hh * 16 + g * 4:hh * 16 + g * 4 + 4, 0:128]),
                            waits=[tcp, kq_free[par]],
                            key="g_mrow%d" % (g * 2 + hh))
                mst_free = list(tdmd.values())
                prep_tok[p]["tdm"] = tdmd

            def chunk(kind, kc, ci):
                return [proj_mm_unit(kind, kc, u) for u in range(4)] \
                    + [proj_evac(kind, kc, ci)]

            def zipped(projs, norms):
                out, pi, ni = [], 0, 0
                while pi < len(projs) or ni < len(norms):
                    if pi < len(projs):
                        out.append(projs[pi]); pi += 1
                    if ni < len(norms):
                        out.append(norms[ni]); ni += 1
                return out

            pchunks = [chunk("k", kc, kc) for kc in range(8)] \
                + [chunk("q", g, 8 + g) for g in range(NQG)]
            tasks += pchunks[0] + pchunks[1] + [t_bar(False)]
            for half in range(4):
                nj = norm_job("k", 0, half) + norm_job("k", 1, half)
                tasks += zipped(pchunks[2 * half + 2] + pchunks[2 * half + 3], nj)
                tasks.append(t_bar(half == 3 and False))
            nj = norm_job("q", 0, 0) + norm_job("q", 1, 0)
            tasks += zipped(pchunks[10] + pchunks[11], nj)
            tasks.append(t_bar(True))
            tasks += norm_job("q", 0, 1) + norm_job("q", 1, 1)
            tasks.append(t_kmax)
            tasks.append(t_bcast)
            tasks.append(t_mrow)
            prep_tok[p] = {}
            return tasks

        for _f in build_prep(0):
            _f()
        _stop = {"p0": 0, "kv0": 0.2, "pa0": 0.4, "pb0": 0.6, "pb1": 0.7,
                 "pr1": 0.8, "op0": 0.9, "opnb": 0.95,
                 "attn0": 1, "attn": 2, "router": 3, "all": 9}[stage]

        def _finish(local_tensors):
            cw = [(e, em.cnt[e]) for e in CENGS]
            for tname, tshape, tdt in taps:
                if tname in local_tensors:
                    tsrc = local_tensors[tname]
                    LD(lambda s_, tsrc=tsrc, tname=tname: s_.dma_start(
                        out=tap_hbm[tname], in_=tsrc), waits=cw, key="s_tap")
            em.wait_only("sync", em.all_keys() + [(e, em.cnt[e]) for e in CENGS])
            sems = make_sems()
            with nc.Block() as block:
                emit(em, block, sems)
            return nc

        if _stop == 0:
            return _finish({"xnT": xnT[:], "nmean": nmean[:], "nmean1": nmean1[:],
                            "varn": varn[:], "rstd": rstd[:], "xst": xst[:]})


        for grp in range(4):
            if grp == 0:
                v4_all = vdir_last[0]
            for kt in ([] if grp == 0 else range(NKT)):
                psX = psV if kt % 2 == 0 else psO
                psX_f = psV_free if kt % 2 == 0 else psO_free
                last = None
                for mc in range(NMC):
                    last = T(lambda tt, kt=kt, mc=mc, psX=psX: tt.matmul(
                        psX[:, 0:256],
                        xnT[:, mc * S + kt * 128:mc * S + (kt + 1) * 128],
                        wv_sb[:, mc * 256:(mc + 1) * 256],
                        start=(mc == 0), stop=False),
                        waits=[xn_ready[kt], wv_ld[grp], psX_f])
                trk = T(lambda tt, grp=grp, psX=psX: tt.matmul(
                    psX[:, 0:256], cones[0:1, 0:128], bv_bf[0:1, grp * 256:(grp + 1) * 256],
                    start=False, stop=True), waits=[t_gconst, t_ones])
                tev = V(lambda v, kt=kt, psX=psX: v.tensor_copy(
                    V4[:, kt * 260:(kt + 1) * 260].rearrange(
                        "p (h d) -> p h d", d=65)[:, :, 0:64],
                    psX[:, 0:256].rearrange("p (h d) -> p h d", d=64)),
                    waits=[trk, vm0, v4_free])
                if kt % 2 == 0:
                    psV_free = tev
                else:
                    psO_free = tev
                vdir_last[grp] = tev
            v4_all = vdir_last[grp]

            grp_av_last = None
            for pr in (2 * grp, 2 * grp + 1):
                par = pr % 2
                ptk = prep_tok[pr]
                kq_scal = ptk["kq_scal"]
                kq_sync = ptk["kq_sync"]
                tdm_tok = ptk["tdm"]
                tasks = build_prep(pr + 1) if pr + 1 < 8 else []
                n_tasks = len(tasks)
                t_next = 0
                itc = 0

                if _stop == 0.2 and pr == 0:
                    return _finish({"Kt0": KtP[0][0][:], "Kt1": KtP[0][1][:],
                                    "Qt0": QtP[0][0][:], "Qt1": QtP[0][1][:],
                                    "V4": V4[:]})

                if pr % 2 == 1 and grp + 1 < 4:
                    wv_free[0] = vdir_last[grp]
                    load_wv(grp + 1)

                def dance_step(pv, k):
                    nonlocal psT1_free, ps2b_free, zt_free, zraw_free, zg_free
                    nonlocal zdma_last
                    qt, phase = k // 2, k % 2
                    if phase == 0:
                        tt1 = T(lambda tt, qt=qt, pv=pv: tt.transpose(
                            psT1[0:128, 0:65],
                            zraw[0:65, qt * 128:(qt + 1) * 128],
                            id_f32[0:65, 0:65]),
                            waits=[pv["tze"], psT1_free, t_sconst])
                        zraw_free = tt1
                        tg = V(lambda v: v.tensor_single_scalar(
                            zguard[:, 0:1], psT1[:, 64:65], 1e-30, OP.max),
                            waits=[tt1, zg_free])
                        trc = V(lambda v: v.reciprocal(rzc[:], zguard[:, 0:1]),
                                waits=[tg])
                        zg_free = trc
                        tsc = V(lambda v: v.tensor_single_scalar(
                            zt_sb[:], psT1[:, 0:64], rzc[:, 0:1], OP.mult),
                            waits=[trc, zt_free])
                        psT1_free = tsc
                        pv["tsc"] = tsc
                    else:
                        tt2 = T(lambda tt, pv=pv: tt.transpose(
                            psT2[0:64, 0:128], zt_sb[:], id_bf[:]),
                            waits=[pv["tsc"], ps2b_free, t_gconst])
                        zt_free = tt2
                        g, hh, zp, zsl = pv["g"], pv["hh"], pv["zp"], pv["zsl"]
                        if hh == 0:
                            tev = V(lambda v, g=g, qt=qt, zp=zp: v.tensor_copy(
                                zp[0:64, g * 512 + qt * 128:g * 512 + (qt + 1) * 128],
                                psT2[0:64, 0:128]),
                                waits=[tt2, zall_free_all])
                            if qt == 3:
                                zev_tok[(pv["pl"], g)] = tev
                        else:
                            tev = V(lambda v, qt=qt, zsl=zsl: v.tensor_copy(
                                zst[0:64, zsl + qt * 128:zsl + (qt + 1) * 128],
                                psT2[0:64, 0:128]),
                                waits=[tt2, zst_free[g % 2]])
                        ps2b_free = tev
                        if qt == 3 and hh == 1:
                            tzd = LD(lambda s_, g=g, zp=zp, zsl=zsl: s_.dma_start(
                                out=zp[64:128, g * 512:(g + 1) * 512],
                                in_=zst[0:64, zsl:zsl + 512]),
                                waits=[tev, zall_free_all],
                                key="s_zst%d" % (g % 2))
                            zst_free[g % 2] = tzd
                            zdma_tok[(pv["pl"], g)] = tzd

                prev = pend_dance
                for g in range(NQG):
                    nk = 8 * (g + 1)
                    for hh in range(2):
                        KT = KtP[par][hh]
                        QT = QtP[par][hh]
                        h4 = 2 * (pr % 2) + hh
                        zi = zrot % 2
                        zrot += 1
                        psZX = psZ if zi == 0 else psO
                        psZX_f = psZ_free if zi == 0 else psO_free
                        av_last = None
                        pend_av = None
                        for kc in range(nk + 1):
                            if kc < nk:
                                spb = kc % 2
                                psSX = (psB, psB2)[spb]
                                slot = kc % 4
                                tmm = T(lambda tt, kc=kc, psSX=psSX, KT=KT, QT=QT,
                                        g=g: tt.matmul(
                                    psSX[:], KT[0:69, kc * 128:(kc + 1) * 128],
                                    QT[0:69, g * 512:(g + 1) * 512],
                                    start=True, stop=True),
                                    waits=[tdm_tok[(g, hh)], psS_free[spb],
                                           kq_scal] + kq_sync)
                                texp = C(lambda sc, slot=slot, psSX=psSX:
                                         sc.activation(
                                    probs[:, slot * 512:(slot + 1) * 512], psSX[:],
                                    AF.Exp, bias=0.0, scale=0.125),
                                    waits=[tmm, probs_free[slot]])
                                psS_free[spb] = texp
                                prb = texp
                                dg = kc - (8 * g + 4)
                                if 0 <= dg < 4:
                                    prb = V(lambda v, slot=slot, dg=dg:
                                            v.tensor_tensor(
                                        probs[:, slot * 512:(slot + 1) * 512],
                                        probs[:, slot * 512:(slot + 1) * 512],
                                        diag_bf[:, dg * 512:(dg + 1) * 512],
                                        OP.mult), waits=[prb, t_gmask])
                            if pend_av is not None:
                                kcp, slotp, prbp = pend_av
                                tav = T(lambda tt, kc=kcp, slot=slotp, h4=h4,
                                        nk=nk, psZX=psZX: tt.matmul(
                                    psZX[0:65, :],
                                    V4[:, kc * 260 + h4 * 65:kc * 260 + h4 * 65 + 65],
                                    probs[:, slot * 512:(slot + 1) * 512],
                                    start=(kc == 0), stop=(kc == nk - 1)),
                                    waits=[prbp, v4_all, psZX_f])
                                probs_free[slotp] = tav
                                av_last = tav
                            if kc < nk:
                                pend_av = (kc, slot, prb)
                            if prev is not None and kc < 8:
                                dance_step(prev, kc)
                            if kc < nk:
                                itc += 1
                                tgt = n_tasks * itc // 160
                                while t_next < tgt:
                                    tasks[t_next]()
                                    t_next += 1
                        grp_av_last = av_last
                        tze = C(lambda sc, psZX=psZX: sc.activation(
                            zraw[0:65, :], psZX[0:65, :], AF.Identity),
                            waits=[av_last, zraw_free])
                        if zi == 0:
                            psZ_free = tze
                        else:
                            psO_free = tze
                        prev = {"tze": tze, "g": g, "hh": hh, "zp": zallT[pr % 2],
                                "zsl": (g % 2) * 512, "pl": pr % 2}
                while t_next < n_tasks:
                    tasks[t_next]()
                    t_next += 1
                kq_free[par] = grp_av_last
                pend_dance = prev
                if pr % 2 == 1:
                    for k in range(8):
                        dance_step(pend_dance, k)
                    pend_dance = None


            op_last = None
            _nqt = 1 if _stop == 0.9 else 16
            for qt16 in range(_nqt):
                gq = qt16 // 4
                zw = [zev_tok.get((0, gq)), zev_tok.get((1, gq)),
                      zdma_tok.get((0, gq)), zdma_tok.get((1, gq))]
                for mh in range(2):
                    psX = psV if mh == 0 else psO
                    ps_free = psV_free if mh == 0 else psO_free
                    last = None
                    for pl in range(2):
                        last = T(lambda tt, psX=psX, qt16=qt16,
                                 mh=mh, pl=pl, grp=grp: tt.matmul(
                            psX[:],
                            zallT[pl][0:128, qt16 * 128:(qt16 + 1) * 128],
                            wo_sb[0:128,
                                  pl * 1024 + mh * 512:
                                  pl * 1024 + (mh + 1) * 512],
                            start=(pl == 0),
                            stop=(pl == 1 and (grp != 0 or _stop == 0.95))),
                            waits=[wo_ld[grp], ps_free] + zw)
                    if grp == 0 and _stop != 0.95:
                        last = T(lambda tt, mh=mh, psX=psX: tt.matmul(
                            psX[:], cones[0:1, 0:128], bo_bf[0:1, mh * 512:(mh + 1) * 512],
                            start=False, stop=True), waits=[t_gconst, t_ones])
                    aof = qt16 * 1024 + mh * 512
                    if grp == 0:
                        tev = V(lambda v, aof=aof, psX=psX: v.tensor_copy(
                            acc[:, aof:aof + 512], psX[:]), waits=[last])
                    else:
                        tev = V(lambda v, aof=aof, psX=psX: v.tensor_tensor(
                            acc[:, aof:aof + 512], acc[:, aof:aof + 512], psX[:],
                            OP.add), waits=[last, acc_tok[qt16][mh]])
                    acc_tok[qt16][mh] = tev
                    op_last = tev
                    if mh == 0:
                        psV_free = tev
                    else:
                        psO_free = tev
            zall_free_all = op_last
            v4_free = grp_av_last
            if _stop in (0.9, 0.95, 1):
                break
            if grp + 1 < 4:
                wo_free[0] = op_last
                load_wo(grp + 1)

        attn_fin = op_last

    with ExitStack() as fctx:
        sbf = lambda n, sh, dt=BF16: fctx.enter_context(nc.sbuf_tensor(n, sh, dt))
        wg_sb = sbf("wg_sb", [128, 2 * 4096])
        wu_sb = sbf("wu_sb", [128, 2 * 4096])
        wd_sb = sbf("wd_sb", [128, 2 * 4096])
        sil_sb = sbf("sil_sb", [128, 2 * 512])
        hT = sbf("hT", [128, 4 * 512])
        lgs = sbf("lgs", [128, 128], F32)
        v1s = sbf("v1s", [128, 16], F32)
        v2s = sbf("v2s", [128, 16], F32)
        d12 = sbf("d12", [128, 16], F32)
        p1s = sbf("p1s", [128, 16], F32)
        p2s = sbf("p2s", [128, 16], F32)
        eq1 = sbf("eq1", [128, 128], F32)
        eq2 = sbf("eq2", [128, 128], F32)
        rw1 = sbf("rw1", [128, 128], F32)
        rws = sbf("rws", [128, 128], F32)
        cbig = sbf("cbig", [128, 1], F32)
        xm_sb = sbf("xm_sb", [128, 2 * D], F32)
        out_sb = sbf("out_sb", [128, 2 * D], F32)

        wff_ld = [None] * 10
        wff_free = [attn_fin, attn_fin]

        def load_ffn(e):
            sl = e % 2
            t1 = WL(lambda g, e=e, sl=sl: g.dma_start(
                out=wg_sb[:, sl * 4096:(sl + 1) * 4096], in_=wgE[e]),
                waits=[wff_free[sl]], key="g_wg%d" % sl)
            t2 = WL(lambda g, e=e, sl=sl: g.dma_start(
                out=wu_sb[:, sl * 4096:(sl + 1) * 4096], in_=wuE[e]),
                waits=[wff_free[sl]], key="g_wu%d" % sl)
            t3 = WL(lambda g, e=e, sl=sl: g.dma_start(
                out=wd_sb[:, sl * 4096:(sl + 1) * 4096], in_=wdE[e]),
                waits=[wff_free[sl]], key="g_wd%d" % sl)
            wff_ld[e] = (t1, t2, t3)

        load_ffn(0)
        load_ffn(1)

        tmc = V(lambda v: v.memset(cbig[:], -1.0e9))
        psV_f = psV_free
        lg_last = None
        for t16 in range(16):
            g16, tl = t16 // 4, t16 % 4
            qof = (2 * g16 + 1) * 512 + tl * 128
            last = None
            for mc in range(NMC):
                last = T(lambda tt, qof=qof, mc=mc: tt.matmul(
                    psV[:, 0:8],
                    xnT[:, mc * S + qof:mc * S + qof + 128],
                    wr_bf[:, mc * 8:(mc + 1) * 8],
                    start=(mc == 0), stop=False), waits=[t_gconst, psV_f])
            last = T(lambda tt: tt.matmul(
                psV[:, 0:8], cones[0:1, 0:128], rb_bf[0:1, 0:8],
                start=False, stop=True), waits=[t_gconst, t_ones])
            lg_last = V(lambda v, t16=t16: v.tensor_copy(
                lgs[:, t16 * 8:(t16 + 1) * 8], psV[:, 0:8]), waits=[last])
            psV_f = lg_last
        lg3 = lgs[:].rearrange("p (t e) -> p t e", e=8)
        tv1 = V(lambda v: v.tensor_reduce(v1s[:].unsqueeze(2), lg3, AX, OP.max),
                waits=[lg_last])
        teq1 = V(lambda v: v.tensor_tensor(
            eq1[:].rearrange("p (t e) -> p t e", e=8), lg3,
            v1s[:].unsqueeze(2).broadcast_to([128, 16, 8]), OP.is_equal),
            waits=[tv1])
        tl2 = V(lambda v: v.scalar_tensor_tensor(
            rw1[:], eq1[:], cbig[:, 0:1], lgs[:], OP.mult, OP.add),
            waits=[tmc, teq1])
        tv2 = V(lambda v: v.tensor_reduce(
            v2s[:].unsqueeze(2), rw1[:].rearrange("p (t e) -> p t e", e=8), AX,
            OP.max), waits=[tl2])
        teq2 = V(lambda v: v.tensor_tensor(
            eq2[:].rearrange("p (t e) -> p t e", e=8), lg3,
            v2s[:].unsqueeze(2).broadcast_to([128, 16, 8]), OP.is_equal),
            waits=[tv2])
        td = V(lambda v: v.tensor_sub(d12[:], v1s[:], v2s[:]), waits=[tv2])
        tp1 = C(lambda sc: sc.activation(p1s[:], d12[:], AF.Sigmoid), waits=[td])
        tp2 = C(lambda sc: sc.activation(p2s[:], d12[:], AF.Sigmoid, scale=-1.0),
                waits=[td])
        tr1 = V(lambda v: v.tensor_tensor(
            rw1[:].rearrange("p (t e) -> p t e", e=8),
            eq1[:].rearrange("p (t e) -> p t e", e=8),
            p1s[:].unsqueeze(2).broadcast_to([128, 16, 8]), OP.mult), waits=[tp1])
        tr2 = V(lambda v: v.tensor_tensor(
            eq2[:].rearrange("p (t e) -> p t e", e=8),
            eq2[:].rearrange("p (t e) -> p t e", e=8),
            p2s[:].unsqueeze(2).broadcast_to([128, 16, 8]), OP.mult), waits=[tp2])
        trw = V(lambda v: v.tensor_add(rws[:], rw1[:], eq2[:]), waits=[tr1, tr2])

        if _stop == 3:
            return _finish({"rws": rws[:], "lgs": lgs[:], "acc": acc[:]})
        xm_free = [None, None]
        os_free = [None, None]

        def record_merge(t16):
            sl = t16 % 2
            g16, tl4 = t16 // 4, t16 % 4
            row0 = (2 * g16 + 1) * 512 + tl4 * 128
            tx = WL(lambda g_, sl=sl, row0=row0: g_.dma_start(
                out=xm_sb[:, sl * D:(sl + 1) * D], in_=xkv[row0:row0 + 128, :]),
                waits=[xm_free[sl]], key="g_xm%d" % sl)
            tme = V(lambda v, sl=sl, t16=t16: v.tensor_tensor(
                out_sb[:, sl * D:(sl + 1) * D], xm_sb[:, sl * D:(sl + 1) * D],
                acc[:, t16 * D:(t16 + 1) * D], OP.add),
                waits=[tx, acc_tok[t16][0], acc_tok[t16][1], os_free[sl]])
            xm_free[sl] = tme
            tst = LD(lambda s, sl=sl, t16=t16: s.dma_start(
                out=out_hbm[t16 * 128:(t16 + 1) * 128, :],
                in_=out_sb[:, sl * D:(sl + 1) * D]), waits=[tme],
                key="s_out%d" % sl)
            os_free[sl] = tst

        gA_f = [psA_free]
        gU_f = [psS_free[0]]
        dn_banks = [psB2, psZ, psV, psO]
        dn_f = [psS_free[1], psZ_free, psV_free, psO_free]
        sil_free = [None, None]
        hT_free = [None] * 4
        for e in range(10):
            sl = e % 2
            dn_last = None
            for g in range(NQG):
                qof = (2 * g + 1) * 512
                for fs in range(4):
                    fp = 0
                    psG = psA
                    psU = psB
                    gl = None
                    ul = None
                    for mc in range(NMC):
                        gl = T(lambda tt, sl=sl, mc=mc, fs=fs, qof=qof, psG=psG:
                               tt.matmul(
                            psG[:],
                            wg_sb[:, sl * 4096 + (mc * 4 + fs) * 128:
                                  sl * 4096 + (mc * 4 + fs) * 128 + 128],
                            xnT[:, mc * S + qof:mc * S + qof + 512],
                            start=(mc == 0), stop=(mc == NMC - 1)),
                            waits=[wff_ld[e][0], gA_f[fp], xn_all])
                    for mc in range(NMC):
                        ul = T(lambda tt, sl=sl, mc=mc, fs=fs, qof=qof, psU=psU:
                               tt.matmul(
                            psU[:],
                            wu_sb[:, sl * 4096 + (mc * 4 + fs) * 128:
                                  sl * 4096 + (mc * 4 + fs) * 128 + 128],
                            xnT[:, mc * S + qof:mc * S + qof + 512],
                            start=(mc == 0), stop=(mc == NMC - 1)),
                            waits=[wff_ld[e][1], gU_f[fp]])
                    ssl = fs % 2
                    tsil = C(lambda sc, ssl=ssl, psG=psG: sc.activation(
                        sil_sb[:, ssl * 512:(ssl + 1) * 512], psG[:], AF.Silu),
                        waits=[gl, sil_free[ssl]])
                    gA_f[fp] = tsil
                    th = V(lambda v, fs=fs, ssl=ssl, psU=psU: v.tensor_tensor(
                        hT[:, fs * 512:(fs + 1) * 512],
                        sil_sb[:, ssl * 512:(ssl + 1) * 512], psU[:], OP.mult),
                        waits=[tsil, ul, hT_free[fs]])
                    gU_f[fp] = th
                    sil_free[ssl] = th
                    hT_free[fs] = th
                for tl4 in range(4):
                    t16 = g * 4 + tl4
                    for mh in range(2):
                        dbi = (tl4 * 2 + mh) % 4
                        psX = dn_banks[dbi]
                        psX_f = dn_f[dbi]
                        last = None
                        for fc in range(4):
                            last = T(lambda tt, sl=sl, fc=fc, tl4=tl4, mh=mh,
                                     psX=psX: tt.matmul(
                                psX[:],
                                hT[:, fc * 512 + tl4 * 128:fc * 512 + tl4 * 128 + 128],
                                wd_sb[:, sl * 4096 + fc * 1024 + mh * 512:
                                      sl * 4096 + fc * 1024 + (mh + 1) * 512],
                                start=(fc == 0), stop=(fc == 3)),
                                waits=[wff_ld[e][2], psX_f, hT_free[fc]])
                        aof = t16 * 1024 + mh * 512
                        if e < 8:
                            tadd = V(lambda v, aof=aof, psX=psX, t16=t16, e=e:
                                     v.scalar_tensor_tensor(
                                acc[:, aof:aof + 512], psX[:],
                                rws[:, t16 * 8 + e:t16 * 8 + e + 1],
                                acc[:, aof:aof + 512], OP.mult, OP.add),
                                waits=[last, trw, acc_tok[t16][mh]])
                        else:
                            tadd = V(lambda v, aof=aof, psX=psX: v.tensor_tensor(
                                acc[:, aof:aof + 512], acc[:, aof:aof + 512],
                                psX[:], OP.add), waits=[last, acc_tok[t16][mh]])
                        acc_tok[t16][mh] = tadd
                        dn_last = tadd
                        dn_f[dbi] = tadd
                for fc in range(4):
                    hT_free[fc] = dn_last
                if e == 9:
                    for tl4m in range(4):
                        record_merge(g * 4 + tl4m)
            if e + 2 < 10:
                wff_free[sl] = dn_last
                load_ffn(e + 2)

        em.wait_only("sync", em.all_keys() + [(e, em.cnt[e]) for e in CENGS])

    sems = make_sems()
    with nc.Block() as block:
        emit(em, block, sems)
    return nc


_PROG = None


def _get_prog():
    global _PROG
    if _PROG is None:
        _PROG = build_program()
    return _PROG


def _host_layouts(inp):
    f = np.float32
    WQ = np.asarray(inp["L0_attn_W_Q"], f)
    WK = np.asarray(inp["L0_attn_W_K"], f)
    WV = np.asarray(inp["L0_attn_W_V"], f)
    WO = np.asarray(inp["L0_attn_W_O"], f)
    bQ = np.asarray(inp["L0_attn_b_Q"], f)
    bK = np.asarray(inp["L0_attn_b_K"], f)
    bV = np.asarray(inp["L0_attn_b_V"], f)
    bO = np.asarray(inp["L0_attn_b_O"], f)
    Wr = np.asarray(inp["L0_ffn_Wr"], f)
    Wg = np.asarray(inp["L0_ffn_Wg"], f)
    Wu = np.asarray(inp["L0_ffn_Wu"], f)
    Wd = np.asarray(inp["L0_ffn_Wd"], f)
    Sg = np.asarray(inp["L0_ffn_Sg"], f)
    Su = np.asarray(inp["L0_ffn_Su"], f)
    Sd = np.asarray(inp["L0_ffn_Sd"], f)
    eb = np.asarray(inp["L0_ffn_expert_bias"], f)

    d = {}
    d["wqT"] = np.ascontiguousarray(
        WQ.reshape(8, 2, 8, 128, 64).transpose(3, 0, 2, 1, 4).reshape(128, 8192))
    d["wkT"] = np.ascontiguousarray(
        WK.reshape(8, 2, 8, 128, 64).transpose(3, 0, 2, 1, 4).reshape(128, 8192))
    d["wv4"] = np.ascontiguousarray(
        WV.reshape(4, 4, 8, 128, 64).transpose(3, 0, 2, 1, 4).reshape(128, 8192))
    d["woT"] = np.ascontiguousarray(
        WO.reshape(4, 2, 2, 64, 1024).transpose(2, 3, 0, 1, 4).reshape(128, 8192))
    d["bqk"] = np.ascontiguousarray(
        np.stack([bQ.reshape(8, 2 * 64), bK.reshape(8, 2 * 64)], axis=2)
        .transpose(1, 0, 2).reshape(128, 16))
    d["bvr"] = bV.reshape(1, 1024).copy()
    d["bor"] = bO.reshape(1, 1024).copy()
    d["wrR"] = np.ascontiguousarray(
        Wr.reshape(8, 128, 8).transpose(1, 0, 2).reshape(128, 64))
    d["rbr"] = eb.reshape(1, 8).copy()

    def gate_layout(w):
        return np.ascontiguousarray(
            w.reshape(-1, 8, 128, 4, 128).transpose(0, 2, 1, 3, 4).reshape(-1, 128, 4096))

    def down_layout(w):
        return np.ascontiguousarray(
            w.reshape(-1, 4, 128, 1024).transpose(0, 2, 1, 3).reshape(-1, 128, 4096))

    g_sh = np.stack([Sg[:, 0:512], Sg[:, 512:1024]], 0)
    u_sh = np.stack([Su[:, 0:512], Su[:, 512:1024]], 0)
    dd_sh = np.stack([Sd[0:512, :], Sd[512:1024, :]], 0)
    d["wgE"] = np.concatenate([gate_layout(Wg), gate_layout(g_sh)], 0)
    d["wuE"] = np.concatenate([gate_layout(Wu), gate_layout(u_sh)], 0)
    d["wdE"] = np.concatenate([down_layout(Wd), down_layout(dd_sh)], 0)

    dg = np.zeros((128, 2048), np.float32)
    for c in range(4):
        for p in range(128):
            dg[p, c * 512 + c * 128 + p:(c + 1) * 512] = 1.0
    d["diag"] = dg
    d["ident"] = np.eye(128, dtype=np.float32)
    return d


_PERM_B = [1, 0, 3, 2, 5, 4, 7, 6]


def kernel(**inputs):
    x = np.asarray(inputs["L0_attn_x"], np.float32)
    shared = _host_layouts(inputs)
    nc = _get_prog()

    qi = np.zeros((4, OWN), np.float32)
    for g in range(4):
        qi[g, g * 512:(g + 1) * 512] = -1.0

    km_A = np.zeros((4, S), np.float32)
    km_B = np.zeros((4, S), np.float32)
    for g in range(4):
        km_B[g, 2 * g * 512:(2 * g + 1) * 512] = BIGMASK

    in_maps = []
    for core in range(8):
        b = core // 2
        role = core % 2
        if role == 0:
            xc = np.ascontiguousarray(x[b])
            km = km_A
        else:
            xc = np.ascontiguousarray(x[b].reshape(8, 512, D)[_PERM_B].reshape(S, D))
            km = km_B
        m = dict(shared)
        m["xkv"] = xc
        m["kmask"] = km
        m["qind"] = qi
        in_maps.append(m)

    res = run_bass_kernel_spmd(nc, in_maps, core_ids=list(range(8)))
    y = np.empty((B, S, D), np.float32)
    for core in range(8):
        b = core // 2
        role = core % 2
        o = np.asarray(res.results[core]["out"], np.float32)
        for g in range(4):
            blk = 2 * g + 1 if role == 0 else 2 * g
            y[b, blk * 512:(blk + 1) * 512, :] = o[g * 512:(g + 1) * 512, :]
    return y


# version 15
# speedup vs baseline: 1.0304x; 1.0014x over previous
import numpy as np
from contextlib import ExitStack

import concourse.bass as bass
import concourse.mybir as mybir
from concourse.bass_utils import run_bass_kernel_spmd

F32 = mybir.dt.float32
BF16 = mybir.dt.bfloat16
AX = mybir.AxisListType.X
OP = mybir.AluOpType
AF = mybir.ActivationFunctionType

B, S, D, H, DH = 4, 4096, 1024, 16, 64
NMC = D // 128
NKT = S // 128
NQG = 4
OWN = 2048
EPS = 1e-5
BIGMASK = 1.0e5

ENGS = ("sync", "gpsimd", "tensor", "vector", "scalar")
CENGS = ("tensor", "vector", "scalar")


class Em:

    def __init__(self):
        self.items = {e: [] for e in ENGS}
        self.cnt = {e: 0 for e in CENGS}
        self.kcnt = {}

    def op(self, eng, fn, waits=(), key=None):
        for w in waits:
            if w is not None:
                top = self.cnt.get(w[0], None)
                if top is None:
                    top = self.kcnt[w[0]]
                assert w[1] <= top, (eng, w)
        if eng in CENGS:
            assert key is None
            self.cnt[eng] += 1
            tok = (eng, self.cnt[eng])
            self.items[eng].append((fn, [w for w in waits if w is not None], 1, eng))
        else:
            assert key is not None
            self.kcnt[key] = self.kcnt.get(key, 0) + 16
            tok = (key, self.kcnt[key])
            self.items[eng].append((fn, [w for w in waits if w is not None], 16, key))
        return tok

    def wait_only(self, eng, waits):
        self.items[eng].append((None, [w for w in waits if w is not None], 0, None))

    def all_keys(self):
        return [(k, v) for k, v in self.kcnt.items()]


def emit(em, block, sems):
    def runner(ename):
        def run(engine):
            last = {}
            cnt = 0
            for fn, waits, inc, key in em.items[ename]:
                for (we, wv) in waits:
                    if we == ename and fn is not None:
                        if ename in ("vector", "scalar") and cnt - wv <= 1 \
                                and wv > last.get(we, 0):
                            engine.wait_ge(sems[we], wv)
                            last[we] = wv
                        continue
                    if wv > last.get(we, 0) or fn is None:
                        engine.wait_ge(sems[we], wv)
                        last[we] = max(last.get(we, 0), wv)
                if fn is not None:
                    ins = fn(engine)
                    ins.then_inc(sems[key], inc)
                    if key == ename:
                        cnt += inc
        return run

    block.sync(runner("sync"))
    block.gpsimd(runner("gpsimd"))
    block.tensor(runner("tensor"))
    block.vector(runner("vector"))
    block.scalar(runner("scalar"))


def build_program(stage="all", taps=()):
    nc = bass.Bass("TRN2", target_bir_lowering=False, debug=False)

    dt_in = lambda n, sh, dt=F32: nc.dram_tensor(n, sh, dt, kind="ExternalInput").ap()
    xkv = dt_in("xkv", [S, D])
    wqT = dt_in("wqT", [128, 8192])
    wkT = dt_in("wkT", [128, 8192])
    wv4 = dt_in("wv4", [128, 8192])
    woT = dt_in("woT", [128, 8192])
    bqk = dt_in("bqk", [128, 16])
    bvr = dt_in("bvr", [1, 1024])
    bor = dt_in("bor", [1, 1024])
    wrR = dt_in("wrR", [128, 64])
    rbr = dt_in("rbr", [1, 8])
    wgE = dt_in("wgE", [10, 128, 4096])
    wuE = dt_in("wuE", [10, 128, 4096])
    wdE = dt_in("wdE", [10, 128, 4096])
    kmask = dt_in("kmask", [4, S])
    qind = dt_in("qind", [4, OWN])
    diag = dt_in("diag", [128, 2048])
    ident = dt_in("ident", [128, 128])
    out_hbm = nc.dram_tensor("out", [OWN, D], F32, kind="ExternalOutput").ap()
    tap_hbm = {}
    for tname, tshape, tdt in taps:
        tap_hbm[tname] = nc.dram_tensor("tap_" + tname, tshape, tdt,
                                        kind="ExternalOutput").ap()

    em = Em()
    T = lambda fn, waits=(): em.op("tensor", fn, waits)
    V = lambda fn, waits=(): em.op("vector", fn, waits)
    C = lambda fn, waits=(): em.op("scalar", fn, waits)
    LD = lambda fn, waits=(), key=None: em.op("sync", fn, waits, key=key)
    WL = lambda fn, waits=(), key=None: em.op("gpsimd", fn, waits, key=key)

    ctx = ExitStack()
    sb = lambda n, sh, dt=BF16: ctx.enter_context(nc.sbuf_tensor(n, sh, dt))
    psd = lambda n, sh, dt=F32: ctx.enter_context(nc.psum_tensor(n, sh, dt))

    def make_sems():
        sems = {e: ctx.enter_context(nc.semaphore("sem_" + e)) for e in CENGS}
        for k in em.kcnt:
            sems[k] = ctx.enter_context(nc.semaphore("sem_" + k))
        return sems

    xnT = sb("xnT", [128, NMC * S])
    acc = sb("acc", [128, 16 * D])
    id_bf = sb("id_bf", [128, 128])
    id_f32 = sb("id_f32", [128, 128], F32)
    cones = sb("cones", [9, 256])
    bqk_sb = sb("bqk_sb", [128, 16], F32)
    bv_bf = sb("bv_bf", [9, 1088])
    bo_bf = sb("bo_bf", [9, 1088])
    rb_bf = sb("rb_bf", [9, 72])
    wr_bf = sb("wr_bf", [128, 128])
    epsc = sb("epsc", [128, 1], F32)

    psA = psd("psA", [128, 512])
    psB = psd("psB", [128, 512])
    psB2 = psd("psB2", [128, 512])
    psZ = psd("psZ", [128, 512])
    psV = psd("psV", [128, 512])
    psO = psd("psO", [128, 512])
    psT1 = psd("psT1", [128, 128])
    psT2 = psd("psT2", [128, 512], BF16)

    WL(lambda g: g.dma_start(out=id_bf[:], in_=ident), key="g_const")
    LD(lambda s: s.dma_start(out=id_f32[:], in_=ident), key="s_const")
    LD(lambda s: s.dma_start(out=bqk_sb[:], in_=bqk), key="s_const")
    WL(lambda g: g.dma_start(out=bv_bf[0:1, 0:1024], in_=bvr), key="g_const")
    WL(lambda g: g.dma_start(out=bo_bf[0:1, 0:1024], in_=bor), key="g_const")
    WL(lambda g: g.dma_start(out=rb_bf[0:1, 0:8], in_=rbr), key="g_const")
    WL(lambda g: g.dma_start(out=wr_bf[:, 0:64], in_=wrR), key="g_const")
    onesf = sb("onesf", [128, 1], F32)
    t_ones = V(lambda v: v.memset(cones[0:1, 0:256], 1.0))
    t_eps = V(lambda v: v.memset(epsc[:], EPS))
    t_onesf = V(lambda v: v.memset(onesf[:], 1.0))
    t_gconst = ("g_const", em.kcnt["g_const"])
    t_sconst = ("s_const", em.kcnt["s_const"])
    t_id = t_idf = t_bqk = t_bv = t_bo = t_rb = t_wr = None

    acc_tok = [[None, None] for _ in range(16)]

    with ExitStack() as actx:
        sba = lambda n, sh, dt=BF16: actx.enter_context(nc.sbuf_tensor(n, sh, dt))
        nmean = sba("nmean", [128, 2], F32)
        nmean1 = sba("nmean1", [128, 2], F32)
        varn = sba("varn", [128, 2], F32)
        sq_s = sba("sq_s", [128, 2], F32)
        rstd = sba("rstd", [128, 2], F32)
        diag_bf = sba("diag_bf", [128, 2048])
        wq_sb = sba("wq_sb", [128, 2 * 1024])
        wk_sb = sba("wk_sb", [128, 2 * 1024])
        wv_sb = sba("wv_sb", [128, 2048])
        wo_sb = sba("wo_sb", [128, 2112])
        V4 = sba("V4", [128, NKT * 260])
        KtP = [[sba("Kt%d%d" % (par, hh), [69, S]) for hh in range(2)]
               for par in range(2)]
        QtP = [[sba("Qt%d%d" % (par, hh), [69, OWN]) for hh in range(2)]
               for par in range(2)]
        pstage = sba("pstage", [128, 2 * 512])
        onescol = sba("onescol", [128, 8])
        kmx_c = sba("kmx_c", [128, 1], F32)
        kmax_sb = sba("kmax_sb", [9, 8])
        kb_sb = sba("kb_sb", [128, 2], F32)
        negc = sba("negc", [128, 32], F32)
        zguard = sba("zguard", [128, 1], F32)
        zt_sb = sba("zt_sb", [128, 64])
        rzc = sba("rzc", [128, 1], F32)
        pctx = ExitStack()
        sbp = lambda n, sh, dt=BF16: pctx.enter_context(nc.sbuf_tensor(n, sh, dt))
        xst = sbp("xst", [128, 2 * D], F32)
        lntr = sbp("lntr", [128, D], F32)
        xnt_st = sbp("xnt_st", [128, 2 * D])

        WL(lambda g: g.dma_start(out=diag_bf[:], in_=diag), key="g_mask")

        t_onescol = V(lambda v: v.memset(onescol[:, 0:8], 1.0))
        ktc = []
        for par in range(2):
            for hh in range(2):
                ktc.append(V(lambda v, par=par, hh=hh: v.memset(
                    KtP[par][hh][64:65, :], 1.0)))
                WL(lambda g, par=par, hh=hh: g.dma_start(
                    out=KtP[par][hh][65:69, :], in_=kmask), key="g_mask")
                WL(lambda g, par=par, hh=hh: g.dma_start(
                    out=QtP[par][hh][65:69, :], in_=qind), key="g_mask")
        t_gmask = ("g_mask", em.kcnt["g_mask"])
        qtc = [t_gmask]

        wq_free = [None, None]
        wk_free = [None, None]
        wv_free = [None, None]
        wo_free = [None, None]
        wq_ld = [None] * 8
        wk_ld = [None] * 8
        wv_ld = [None] * 4
        wo_ld = [None] * 4
        kq_free = [None, None]
        v4_free = None
        psS_free = [None, None]
        psA_free = None
        psZ_free = None
        psV_free = None
        psO_free = None
        psT1_free = None
        kmx_free = None
        kmaxsb_free = None
        kb_free = None
        negc_free = None
        zg_free = None
        zrot = 0
        pend_dance = None
        zdma_last = None
        mst_free = []
        ps2b_free = None
        pst_free = [None, None]
        probs_free = [None] * 4
        prep_tok = {}
        zev_tok = {}
        zdma_tok = {}

        zraw_free = None
        zt_free = None
        zall_free_all = None
        zst_free = [None, None]
        vdir_last = [None] * 4

        def load_wqk(p):
            slp = p % 2
            wq_ld[p] = WL(lambda g, p=p, slp=slp: g.dma_start(
                out=wq_sb[:, slp * 1024:(slp + 1) * 1024],
                in_=wqT[:, p * 1024:(p + 1) * 1024]), waits=[wq_free[slp]],
                key="g_wq%d" % slp)
            wk_ld[p] = WL(lambda g, p=p, slp=slp: g.dma_start(
                out=wk_sb[:, slp * 1024:(slp + 1) * 1024],
                in_=wkT[:, p * 1024:(p + 1) * 1024]), waits=[wk_free[slp]],
                key="g_wk%d" % slp)

        def load_wv(grp):
            wv_ld[grp] = WL(lambda g, grp=grp: g.dma_start(
                out=wv_sb[:],
                in_=wv4[:, grp * 2048:(grp + 1) * 2048]), waits=[wv_free[0]],
                key="g_wv")

        def load_wo(grp):
            wo_ld[grp] = WL(lambda g, grp=grp: g.dma_start(
                out=wo_sb[0:128, 0:2048],
                in_=woT[:, grp * 2048:(grp + 1) * 2048]), waits=[wo_free[0]],
                key="g_wo")

        load_wqk(0)
        load_wqk(1)
        load_wv(0)
        load_wo(0)

        vm0 = V(lambda v: v.memset(V4[:], 1.0))
        vdir_evac0 = None

        xst_free = [None, None]
        xnst_free = [None, None]
        ps2_free = None
        prev_sq = [None, None]
        xn_ready = [None] * NKT
        stats_tok = [None] * NKT
        vdir_evac0 = None

        def ln_tail(t):
            nonlocal ps2_free, psV_free, psO_free, vdir_evac0
            sl = t % 2
            x_sl = xst[:, sl * D:(sl + 1) * D]
            xn_sl = xnt_st[:, sl * D:(sl + 1) * D]
            t_sq, t_sqrt = stats_tok[t]
            t_rcp = V(lambda v, sl=sl: v.reciprocal(rstd[:, sl:sl + 1],
                                                    sq_s[:, sl:sl + 1]),
                      waits=[t_sqrt])
            t_app = V(lambda v, sl=sl, x_sl=x_sl, xn_sl=xn_sl: v.tensor_scalar(
                xn_sl, x_sl, nmean1[:, sl:sl + 1], rstd[:, sl:sl + 1],
                OP.add, OP.mult),
                waits=[xnst_free[sl], t_sq, t_rcp])
            xst_free[sl] = t_app
            evl = None
            for half in range(2):
                tps = []
                for b4 in range(4):
                    mc = half * 4 + b4
                    tps.append(T(lambda tt, mc=mc, b4=b4, xn_sl=xn_sl: tt.transpose(
                        psT2[:, b4 * 128:(b4 + 1) * 128],
                        xn_sl[:, mc * 128:(mc + 1) * 128], id_bf[:]),
                        waits=[t_app, t_gconst, ps2_free]))

                def ev(v, half=half, t=t):
                    o = xnT[:].rearrange("p (mc s) -> p mc s", s=S)[
                        :, half * 4:half * 4 + 4, t * 128:(t + 1) * 128]
                    i = psT2[:].rearrange("p (b s) -> p b s", s=128)
                    return v.tensor_copy(o, i)
                evl = V(ev, waits=tps)
                ps2_free = evl
            xnst_free[sl] = evl
            xn_ready[t] = evl
            kt = t
            psX = psV if kt % 2 == 0 else psO
            psX_f = psV_free if kt % 2 == 0 else psO_free
            for mc in range(NMC):
                T(lambda tt, kt=kt, mc=mc, psX=psX: tt.matmul(
                    psX[:, 0:256],
                    xnT[:, mc * S + kt * 128:mc * S + (kt + 1) * 128],
                    wv_sb[:, mc * 256:(mc + 1) * 256],
                    start=(mc == 0), stop=False),
                    waits=[xn_ready[kt], wv_ld[0], psX_f])
            trk = T(lambda tt, psX=psX: tt.matmul(
                psX[:, 0:256], cones[0:1, 0:128], bv_bf[0:1, 0:256],
                start=False, stop=True), waits=[t_gconst, t_ones])
            tev = V(lambda v, kt=kt, psX=psX: v.tensor_copy(
                V4[:, kt * 260:(kt + 1) * 260].rearrange(
                    "p (h d) -> p h d", d=65)[:, :, 0:64],
                psX[:, 0:256].rearrange("p (h d) -> p h d", d=64)),
                waits=[trk, vm0])
            if kt % 2 == 0:
                psV_free = tev
            else:
                psO_free = tev
            vdir_evac0 = tev

        for t in range(NKT):
            sl = t % 2
            x_sl = xst[:, sl * D:(sl + 1) * D]
            if sl == 0:
                t_load = LD(lambda s, t=t, x_sl=x_sl: s.dma_start(
                    out=x_sl, in_=xkv[t * 128:(t + 1) * 128, :]),
                    waits=[xst_free[sl]], key="s_x0")
            else:
                t_load = WL(lambda g, t=t, x_sl=x_sl: g.dma_start(
                    out=x_sl, in_=xkv[t * 128:(t + 1) * 128, :]),
                    waits=[xst_free[sl]], key="g_x1")
            t_red = V(lambda v, sl=sl, x_sl=x_sl: v.reduce_sum(
                nmean[:, sl:sl + 1], x_sl, AX), waits=[t_load])
            t_nm = V(lambda v, sl=sl: v.tensor_single_scalar(
                nmean1[:, sl:sl + 1], nmean[:, sl:sl + 1], -1.0 / D, OP.mult),
                waits=[prev_sq[sl], t_red])
            t_sq = C(lambda sc, sl=sl, x_sl=x_sl: sc.activation(
                lntr[:], x_sl, AF.Square, bias=nmean1[:, sl:sl + 1], scale=1.0,
                accum_out=varn[:, sl:sl + 1]), waits=[t_nm])
            prev_sq[sl] = t_sq
            t_sqrt = C(lambda sc, sl=sl: sc.activation(
                sq_s[:, sl:sl + 1], varn[:, sl:sl + 1], AF.Sqrt,
                bias=epsc[:, 0:1], scale=1.0 / D), waits=[t_eps, t_sq])
            stats_tok[t] = (t_sq, t_sqrt)
            if t >= 1:
                ln_tail(t - 1)
        ln_tail(NKT - 1)
        xn_all = xn_ready[NKT - 1]
        vdir_last[0] = vdir_evac0
        ps2b_free = ps2_free
        pctx.close()
        probs = sba("probs", [128, 4 * 512])
        mstage = sba("mstage", [32, 128], F32)
        zraw = sba("zraw", [65, 512], F32)
        zallT = [sba("zallT%d" % i, [128, OWN]) for i in range(2)]
        zst = sba("zst", [69, 2 * 512])
        sqstage = sba("sqstage", [128, 1024])
        t_sqz = V(lambda v: v.memset(sqstage[64:128, :], 0.0), waits=[xn_all])

        def build_prep(p):
            par = p % 2
            st = {}
            tasks = []

            def proj_mm_unit(kind, kc, u):
                def f():
                    last = None
                    for mc in (2 * u, 2 * u + 1):
                        if kind == "k":
                            lhs, rhs_col = wk_sb, kc * 512
                            xw, wld = xn_ready[kc * 4 + 3], wk_ld[p]
                        else:
                            lhs, rhs_col = wq_sb, (2 * kc + 1) * 512
                            xw, wld = xn_all, wq_ld[p]
                        last = T(lambda tt, mc=mc, lhs=lhs, rhs_col=rhs_col:
                                 tt.matmul(
                            psV[:],
                            lhs[:, par * 1024 + mc * 128:par * 1024 + (mc + 1) * 128],
                            xnT[:, mc * S + rhs_col:mc * S + rhs_col + 512],
                            start=(mc == 0), stop=(mc == NMC - 1)),
                            waits=[xw, wld, psV_free])
                    st[("mm", kind, kc)] = last
                return f

            def proj_evac(kind, kc, ci):
                def f():
                    nonlocal psV_free
                    sps = ci % 2
                    cc = 2 * p + (1 if kind == "k" else 0)
                    dA = KtP[par][0] if kind == "k" else QtP[par][0]
                    dB = KtP[par][1] if kind == "k" else QtP[par][1]
                    col = kc * 512
                    mm = st[("mm", kind, kc)]
                    te0 = V(lambda v: v.tensor_scalar(
                        dA[0:64, col:col + 512], psV[0:64, :],
                        bqk_sb[0:64, cc:cc + 1], onesf[0:64, 0:1],
                        OP.add, OP.mult),
                        waits=[mm, t_sconst, t_onesf, t_gmask, kq_free[par]]
                        + ktc)
                    te1 = V(lambda v: v.tensor_scalar(
                        pstage[64:128, sps * 512:(sps + 1) * 512], psV[64:128, :],
                        bqk_sb[64:128, cc:cc + 1], onesf[64:128, 0:1],
                        OP.add, OP.mult),
                        waits=[pst_free[sps]])
                    psV_free = te1
                    tsh = LD(lambda s: s.dma_start(
                        out=dB[0:64, col:col + 512],
                        in_=pstage[64:128, sps * 512:(sps + 1) * 512]),
                        waits=[te1, kq_free[par]], key="s_kq%d" % sps)
                    pst_free[sps] = tsh
                    st["evac_last"] = te1
                return f

            def t_bar(final):
                def f():
                    st["sync"] = [("s_kq0", em.kcnt.get("s_kq0", 0)),
                                  ("s_kq1", em.kcnt.get("s_kq1", 0))]
                    st["scal"] = st["evac_last"]
                    if final:
                        prep_tok[p]["kq_scal"] = st["evac_last"]
                        prep_tok[p]["kq_sync"] = list(st["sync"])
                        if p + 2 < 8:
                            wq_free[par] = st[("mm", "q", 3)]
                            wk_free[par] = st[("mm", "q", 3)]
                            load_wqk(p + 2)
                return f

            def t_square(src, soff):
                def f():
                    st["sq"] = V(lambda v: v.tensor_mul(
                        sqstage[0:64, 0:1024], src[0:64, soff:soff + 1024],
                        src[0:64, soff:soff + 1024]),
                        waits=[t_sqz, st.get("mm_last"), st["scal"]]
                        + st["sync"])
                return f

            def t_normmm(col0, u):
                def f():
                    for c8 in range(4 * u, 4 * u + 4):
                        st["mm_last"] = T(lambda tt, c8=c8: tt.matmul(
                            psA[:, col0 + c8:col0 + c8 + 1],
                            sqstage[0:128, c8 * 128:(c8 + 1) * 128],
                            onescol[0:128, 0:1], start=True, stop=True),
                            waits=[st["sq"], psA_free, t_onescol])
                return f

            def norm_job(kind, hh, half):
                if kind == "k":
                    src, col0 = KtP[par][hh], hh * 32 + half * 8
                else:
                    src, col0 = QtP[par][hh], 64 + hh * 16 + half * 8
                return [t_square(src, half * 1024),
                        t_normmm(col0, 0), t_normmm(col0, 1)]

            def t_kmax():
                nonlocal psT1_free, kmx_free
                for hh in range(2):
                    tr1 = V(lambda v, hh=hh: v.reduce_max(
                        kmx_c[:, 0:1], psA[:, hh * 32:(hh + 1) * 32], AX),
                        waits=[st["mm_last"], kmx_free, kmaxsb_free])
                    ttp = T(lambda tt: tt.transpose(
                        psT1[0:1, 0:128], kmx_c[:, 0:1], id_f32[:]),
                        waits=[tr1, psT1_free, t_sconst])
                    kmx_free = ttp
                    tr2 = V(lambda v, hh=hh: v.reduce_max(
                        kmax_sb[0:1, hh:hh + 1], psT1[0:1, 0:128], AX),
                        waits=[ttp])
                    psT1_free = tr2
                    st["km%d" % hh] = tr2

            def t_bcast():
                nonlocal psT1_free, psA_free, kmaxsb_free, kb_free
                tkb = T(lambda tt: tt.matmul(
                    psT1[0:128, 0:2], cones[0:1, 0:128], kmax_sb[0:1, 0:2],
                    start=True, stop=True),
                    waits=[st["km0"], st["km1"], psT1_free, t_ones])
                kmaxsb_free = tkb
                tkbe = V(lambda v: v.tensor_copy(kb_sb[:, 0:2], psT1[0:128, 0:2]),
                         waits=[tkb, kb_free])
                psT1_free = tkbe
                tpr = V(lambda v: v.tensor_tensor(
                    negc[:].rearrange("p (h q) -> p h q", q=16),
                    psA[:, 64:96].rearrange("p (h q) -> p h q", q=16),
                    kb_sb[:, 0:2].unsqueeze(2).broadcast_to([128, 2, 16]),
                    OP.mult),
                    waits=[tkbe, st["mm_last"], negc_free])
                kb_free = tpr
                psA_free = tpr
                tnc = C(lambda sc: sc.activation(negc[:], negc[:], AF.Sqrt),
                        waits=[tpr])
                st["negc"] = V(lambda v: v.tensor_single_scalar(
                    negc[:], negc[:], -1.0, OP.mult), waits=[tnc])

            def t_mrow():
                nonlocal psT1_free, negc_free, mst_free
                ttr = T(lambda tt: tt.transpose(
                    psT1[0:32, 0:128], negc[:, 0:32], id_f32[:]),
                    waits=[st["negc"], psT1_free, t_sconst])
                negc_free = ttr
                tcp = V(lambda v: v.tensor_copy(mstage[0:32, 0:128],
                                                psT1[0:32, 0:128]),
                        waits=[ttr] + mst_free)
                psT1_free = tcp
                tdmd = {}
                for g in range(NQG):
                    for hh in range(2):
                        QTh = QtP[par][hh]
                        tdmd[(g, hh)] = WL(lambda g_, g=g, hh=hh, QTh=QTh:
                                           g_.dma_start(
                            out=QTh[64:65, g * 512:(g + 1) * 512],
                            in_=mstage[hh * 16 + g * 4:hh * 16 + g * 4 + 4, 0:128]),
                            waits=[tcp, kq_free[par]],
                            key="g_mrow%d" % (g * 2 + hh))
                mst_free = list(tdmd.values())
                prep_tok[p]["tdm"] = tdmd

            def chunk(kind, kc, ci):
                return [proj_mm_unit(kind, kc, u) for u in range(4)] \
                    + [proj_evac(kind, kc, ci)]

            def zipped(projs, norms):
                out, pi, ni = [], 0, 0
                while pi < len(projs) or ni < len(norms):
                    if pi < len(projs):
                        out.append(projs[pi]); pi += 1
                    if ni < len(norms):
                        out.append(norms[ni]); ni += 1
                return out

            pchunks = [chunk("k", kc, kc) for kc in range(8)] \
                + [chunk("q", g, 8 + g) for g in range(NQG)]
            tasks += pchunks[0] + pchunks[1] + [t_bar(False)]
            for half in range(4):
                nj = norm_job("k", 0, half) + norm_job("k", 1, half)
                tasks += zipped(pchunks[2 * half + 2] + pchunks[2 * half + 3], nj)
                tasks.append(t_bar(half == 3 and False))
            nj = norm_job("q", 0, 0) + norm_job("q", 1, 0)
            tasks += zipped(pchunks[10] + pchunks[11], nj)
            tasks.append(t_bar(True))
            tasks += norm_job("q", 0, 1) + norm_job("q", 1, 1)
            tasks.append(t_kmax)
            tasks.append(t_bcast)
            tasks.append(t_mrow)
            prep_tok[p] = {}
            return tasks

        for _f in build_prep(0):
            _f()
        _stop = {"p0": 0, "kv0": 0.2, "pa0": 0.4, "pb0": 0.6, "pb1": 0.7,
                 "pr1": 0.8, "op0": 0.9, "opnb": 0.95,
                 "attn0": 1, "attn": 2, "router": 3, "all": 9}[stage]

        def _finish(local_tensors):
            cw = [(e, em.cnt[e]) for e in CENGS]
            for tname, tshape, tdt in taps:
                if tname in local_tensors:
                    tsrc = local_tensors[tname]
                    LD(lambda s_, tsrc=tsrc, tname=tname: s_.dma_start(
                        out=tap_hbm[tname], in_=tsrc), waits=cw, key="s_tap")
            em.wait_only("sync", em.all_keys() + [(e, em.cnt[e]) for e in CENGS])
            sems = make_sems()
            with nc.Block() as block:
                emit(em, block, sems)
            return nc

        if _stop == 0:
            return _finish({"xnT": xnT[:], "nmean": nmean[:], "nmean1": nmean1[:],
                            "varn": varn[:], "rstd": rstd[:], "xst": xst[:]})


        for grp in range(4):
            if grp == 0:
                v4_all = vdir_last[0]
            for kt in ([] if grp == 0 else range(NKT)):
                psX = psV if kt % 2 == 0 else psO
                psX_f = psV_free if kt % 2 == 0 else psO_free
                last = None
                for mc in range(NMC):
                    last = T(lambda tt, kt=kt, mc=mc, psX=psX: tt.matmul(
                        psX[:, 0:256],
                        xnT[:, mc * S + kt * 128:mc * S + (kt + 1) * 128],
                        wv_sb[:, mc * 256:(mc + 1) * 256],
                        start=(mc == 0), stop=False),
                        waits=[xn_ready[kt], wv_ld[grp], psX_f])
                trk = T(lambda tt, grp=grp, psX=psX: tt.matmul(
                    psX[:, 0:256], cones[0:1, 0:128], bv_bf[0:1, grp * 256:(grp + 1) * 256],
                    start=False, stop=True), waits=[t_gconst, t_ones])
                tev = V(lambda v, kt=kt, psX=psX: v.tensor_copy(
                    V4[:, kt * 260:(kt + 1) * 260].rearrange(
                        "p (h d) -> p h d", d=65)[:, :, 0:64],
                    psX[:, 0:256].rearrange("p (h d) -> p h d", d=64)),
                    waits=[trk, vm0, v4_free])
                if kt % 2 == 0:
                    psV_free = tev
                else:
                    psO_free = tev
                vdir_last[grp] = tev
            v4_all = vdir_last[grp]

            grp_av_last = None
            for pr in (2 * grp, 2 * grp + 1):
                par = pr % 2
                ptk = prep_tok[pr]
                kq_scal = ptk["kq_scal"]
                kq_sync = ptk["kq_sync"]
                tdm_tok = ptk["tdm"]
                tasks = build_prep(pr + 1) if pr + 1 < 8 else []
                n_tasks = len(tasks)
                t_next = 0
                itc = 0

                if _stop == 0.2 and pr == 0:
                    return _finish({"Kt0": KtP[0][0][:], "Kt1": KtP[0][1][:],
                                    "Qt0": QtP[0][0][:], "Qt1": QtP[0][1][:],
                                    "V4": V4[:]})

                if pr % 2 == 1 and grp + 1 < 4:
                    wv_free[0] = vdir_last[grp]
                    load_wv(grp + 1)

                def dance_step(pv, k):
                    nonlocal psT1_free, ps2b_free, zt_free, zraw_free, zg_free
                    nonlocal zdma_last
                    qt, phase = k // 2, k % 2
                    if phase == 0:
                        tt1 = T(lambda tt, qt=qt, pv=pv: tt.transpose(
                            psT1[0:128, 0:65],
                            zraw[0:65, qt * 128:(qt + 1) * 128],
                            id_f32[0:65, 0:65]),
                            waits=[pv["tze"], psT1_free, t_sconst])
                        zraw_free = tt1
                        tg = V(lambda v: v.tensor_single_scalar(
                            zguard[:, 0:1], psT1[:, 64:65], 1e-30, OP.max),
                            waits=[tt1, zg_free])
                        trc = V(lambda v: v.reciprocal(rzc[:], zguard[:, 0:1]),
                                waits=[tg])
                        zg_free = trc
                        tsc = V(lambda v: v.tensor_single_scalar(
                            zt_sb[:], psT1[:, 0:64], rzc[:, 0:1], OP.mult),
                            waits=[trc, zt_free])
                        psT1_free = tsc
                        pv["tsc"] = tsc
                    else:
                        tt2 = T(lambda tt, pv=pv: tt.transpose(
                            psT2[0:64, 0:128], zt_sb[:], id_bf[:]),
                            waits=[pv["tsc"], ps2b_free, t_gconst])
                        zt_free = tt2
                        g, hh, zp, zsl = pv["g"], pv["hh"], pv["zp"], pv["zsl"]
                        if hh == 0:
                            tev = V(lambda v, g=g, qt=qt, zp=zp: v.tensor_copy(
                                zp[0:64, g * 512 + qt * 128:g * 512 + (qt + 1) * 128],
                                psT2[0:64, 0:128]),
                                waits=[tt2, zall_free_all])
                            if qt == 3:
                                zev_tok[(pv["pl"], g)] = tev
                        else:
                            tev = V(lambda v, qt=qt, zsl=zsl: v.tensor_copy(
                                zst[0:64, zsl + qt * 128:zsl + (qt + 1) * 128],
                                psT2[0:64, 0:128]),
                                waits=[tt2, zst_free[g % 2]])
                        ps2b_free = tev
                        if qt == 3 and hh == 1:
                            tzd = LD(lambda s_, g=g, zp=zp, zsl=zsl: s_.dma_start(
                                out=zp[64:128, g * 512:(g + 1) * 512],
                                in_=zst[0:64, zsl:zsl + 512]),
                                waits=[tev, zall_free_all],
                                key="s_zst%d" % (g % 2))
                            zst_free[g % 2] = tzd
                            zdma_tok[(pv["pl"], g)] = tzd

                prev = pend_dance
                for g in range(NQG):
                    nk = 8 * (g + 1)
                    for hh in range(2):
                        KT = KtP[par][hh]
                        QT = QtP[par][hh]
                        h4 = 2 * (pr % 2) + hh
                        zi = zrot % 2
                        zrot += 1
                        psZX = psZ if zi == 0 else psO
                        psZX_f = psZ_free if zi == 0 else psO_free
                        av_last = None
                        pend_av = None
                        for kc in range(nk + 1):
                            if kc < nk:
                                spb = kc % 2
                                psSX = (psB, psB2)[spb]
                                slot = kc % 4
                                tmm = T(lambda tt, kc=kc, psSX=psSX, KT=KT, QT=QT,
                                        g=g: tt.matmul(
                                    psSX[:], KT[0:69, kc * 128:(kc + 1) * 128],
                                    QT[0:69, g * 512:(g + 1) * 512],
                                    start=True, stop=True),
                                    waits=[tdm_tok[(g, hh)], psS_free[spb],
                                           kq_scal] + kq_sync)
                                texp = C(lambda sc, slot=slot, psSX=psSX:
                                         sc.activation(
                                    probs[:, slot * 512:(slot + 1) * 512], psSX[:],
                                    AF.Exp, bias=0.0, scale=0.125),
                                    waits=[tmm, probs_free[slot]])
                                psS_free[spb] = texp
                                prb = texp
                                dg = kc - (8 * g + 4)
                                if 0 <= dg < 4:
                                    prb = V(lambda v, slot=slot, dg=dg:
                                            v.tensor_tensor(
                                        probs[:, slot * 512:(slot + 1) * 512],
                                        probs[:, slot * 512:(slot + 1) * 512],
                                        diag_bf[:, dg * 512:(dg + 1) * 512],
                                        OP.mult), waits=[prb, t_gmask])
                            if pend_av is not None:
                                kcp, slotp, prbp = pend_av
                                tav = T(lambda tt, kc=kcp, slot=slotp, h4=h4,
                                        nk=nk, psZX=psZX: tt.matmul(
                                    psZX[0:65, :],
                                    V4[:, kc * 260 + h4 * 65:kc * 260 + h4 * 65 + 65],
                                    probs[:, slot * 512:(slot + 1) * 512],
                                    start=(kc == 0), stop=(kc == nk - 1)),
                                    waits=[prbp, v4_all, psZX_f])
                                probs_free[slotp] = tav
                                av_last = tav
                            if kc < nk:
                                pend_av = (kc, slot, prb)
                            if prev is not None and kc < 8:
                                dance_step(prev, kc)
                            if kc < nk:
                                itc += 1
                                tgt = n_tasks * itc // 160
                                while t_next < tgt:
                                    tasks[t_next]()
                                    t_next += 1
                        grp_av_last = av_last
                        tze = C(lambda sc, psZX=psZX: sc.activation(
                            zraw[0:65, :], psZX[0:65, :], AF.Identity),
                            waits=[av_last, zraw_free])
                        if zi == 0:
                            psZ_free = tze
                        else:
                            psO_free = tze
                        prev = {"tze": tze, "g": g, "hh": hh, "zp": zallT[pr % 2],
                                "zsl": (g % 2) * 512, "pl": pr % 2}
                while t_next < n_tasks:
                    tasks[t_next]()
                    t_next += 1
                kq_free[par] = grp_av_last
                pend_dance = prev
                if pr % 2 == 1:
                    for k in range(8):
                        dance_step(pend_dance, k)
                    pend_dance = None


            op_last = None
            _nqt = 1 if _stop == 0.9 else 16
            for qt16 in range(_nqt):
                gq = qt16 // 4
                zw = [zev_tok.get((0, gq)), zev_tok.get((1, gq)),
                      zdma_tok.get((0, gq)), zdma_tok.get((1, gq))]
                for mh in range(2):
                    psX = psV if mh == 0 else psO
                    ps_free = psV_free if mh == 0 else psO_free
                    last = None
                    for pl in range(2):
                        last = T(lambda tt, psX=psX, qt16=qt16,
                                 mh=mh, pl=pl, grp=grp: tt.matmul(
                            psX[:],
                            zallT[pl][0:128, qt16 * 128:(qt16 + 1) * 128],
                            wo_sb[0:128,
                                  pl * 1024 + mh * 512:
                                  pl * 1024 + (mh + 1) * 512],
                            start=(pl == 0),
                            stop=(pl == 1 and (grp != 0 or _stop == 0.95))),
                            waits=[wo_ld[grp], ps_free] + zw)
                    if grp == 0 and _stop != 0.95:
                        last = T(lambda tt, mh=mh, psX=psX: tt.matmul(
                            psX[:], cones[0:1, 0:128], bo_bf[0:1, mh * 512:(mh + 1) * 512],
                            start=False, stop=True), waits=[t_gconst, t_ones])
                    aof = qt16 * 1024 + mh * 512
                    if grp == 0:
                        tev = V(lambda v, aof=aof, psX=psX: v.tensor_copy(
                            acc[:, aof:aof + 512], psX[:]), waits=[last])
                    else:
                        tev = V(lambda v, aof=aof, psX=psX: v.tensor_tensor(
                            acc[:, aof:aof + 512], acc[:, aof:aof + 512], psX[:],
                            OP.add), waits=[last, acc_tok[qt16][mh]])
                    acc_tok[qt16][mh] = tev
                    op_last = tev
                    if mh == 0:
                        psV_free = tev
                    else:
                        psO_free = tev
            zall_free_all = op_last
            v4_free = grp_av_last
            if _stop in (0.9, 0.95, 1):
                break
            if grp + 1 < 4:
                wo_free[0] = op_last
                load_wo(grp + 1)

        attn_fin = op_last

    with ExitStack() as fctx:
        sbf = lambda n, sh, dt=BF16: fctx.enter_context(nc.sbuf_tensor(n, sh, dt))
        wg_sb = sbf("wg_sb", [128, 2 * 4096])
        wu_sb = sbf("wu_sb", [128, 2 * 4096])
        wd_sb = sbf("wd_sb", [128, 2 * 4096])
        sil_sb = sbf("sil_sb", [128, 2 * 512])
        hT = sbf("hT", [128, 4 * 512])
        lgs = sbf("lgs", [128, 128], F32)
        v1s = sbf("v1s", [128, 16], F32)
        v2s = sbf("v2s", [128, 16], F32)
        d12 = sbf("d12", [128, 16], F32)
        p1s = sbf("p1s", [128, 16], F32)
        p2s = sbf("p2s", [128, 16], F32)
        eq1 = sbf("eq1", [128, 128], F32)
        eq2 = sbf("eq2", [128, 128], F32)
        rw1 = sbf("rw1", [128, 128], F32)
        rws = sbf("rws", [128, 128], F32)
        cbig = sbf("cbig", [128, 1], F32)
        xm_sb = sbf("xm_sb", [128, 2 * D], F32)
        out_sb = sbf("out_sb", [128, 2 * D], F32)

        wff_ld = [None] * 10
        wff_free = [attn_fin, attn_fin]

        def load_ffn(e):
            sl = e % 2
            t1 = WL(lambda g, e=e, sl=sl: g.dma_start(
                out=wg_sb[:, sl * 4096:(sl + 1) * 4096], in_=wgE[e]),
                waits=[wff_free[sl]], key="g_wg%d" % sl)
            t2 = WL(lambda g, e=e, sl=sl: g.dma_start(
                out=wu_sb[:, sl * 4096:(sl + 1) * 4096], in_=wuE[e]),
                waits=[wff_free[sl]], key="g_wu%d" % sl)
            t3 = WL(lambda g, e=e, sl=sl: g.dma_start(
                out=wd_sb[:, sl * 4096:(sl + 1) * 4096], in_=wdE[e]),
                waits=[wff_free[sl]], key="g_wd%d" % sl)
            wff_ld[e] = (t1, t2, t3)

        load_ffn(0)
        load_ffn(1)

        tmc = V(lambda v: v.memset(cbig[:], -1.0e9))
        psV_f = psV_free
        lg_last = None
        for t16 in range(16):
            g16, tl = t16 // 4, t16 % 4
            qof = (2 * g16 + 1) * 512 + tl * 128
            last = None
            for mc in range(NMC):
                last = T(lambda tt, qof=qof, mc=mc: tt.matmul(
                    psV[:, 0:8],
                    xnT[:, mc * S + qof:mc * S + qof + 128],
                    wr_bf[:, mc * 8:(mc + 1) * 8],
                    start=(mc == 0), stop=False), waits=[t_gconst, psV_f])
            last = T(lambda tt: tt.matmul(
                psV[:, 0:8], cones[0:1, 0:128], rb_bf[0:1, 0:8],
                start=False, stop=True), waits=[t_gconst, t_ones])
            lg_last = V(lambda v, t16=t16: v.tensor_copy(
                lgs[:, t16 * 8:(t16 + 1) * 8], psV[:, 0:8]), waits=[last])
            psV_f = lg_last
        lg3 = lgs[:].rearrange("p (t e) -> p t e", e=8)
        tv1 = V(lambda v: v.tensor_reduce(v1s[:].unsqueeze(2), lg3, AX, OP.max),
                waits=[lg_last])
        teq1 = V(lambda v: v.tensor_tensor(
            eq1[:].rearrange("p (t e) -> p t e", e=8), lg3,
            v1s[:].unsqueeze(2).broadcast_to([128, 16, 8]), OP.is_equal),
            waits=[tv1])
        tl2 = V(lambda v: v.scalar_tensor_tensor(
            rw1[:], eq1[:], cbig[:, 0:1], lgs[:], OP.mult, OP.add),
            waits=[tmc, teq1])
        tv2 = V(lambda v: v.tensor_reduce(
            v2s[:].unsqueeze(2), rw1[:].rearrange("p (t e) -> p t e", e=8), AX,
            OP.max), waits=[tl2])
        teq2 = V(lambda v: v.tensor_tensor(
            eq2[:].rearrange("p (t e) -> p t e", e=8), lg3,
            v2s[:].unsqueeze(2).broadcast_to([128, 16, 8]), OP.is_equal),
            waits=[tv2])
        td = V(lambda v: v.tensor_sub(d12[:], v1s[:], v2s[:]), waits=[tv2])
        tp1 = C(lambda sc: sc.activation(p1s[:], d12[:], AF.Sigmoid), waits=[td])
        tp2 = C(lambda sc: sc.activation(p2s[:], d12[:], AF.Sigmoid, scale=-1.0),
                waits=[td])
        tr1 = V(lambda v: v.tensor_tensor(
            rw1[:].rearrange("p (t e) -> p t e", e=8),
            eq1[:].rearrange("p (t e) -> p t e", e=8),
            p1s[:].unsqueeze(2).broadcast_to([128, 16, 8]), OP.mult), waits=[tp1])
        tr2 = V(lambda v: v.tensor_tensor(
            eq2[:].rearrange("p (t e) -> p t e", e=8),
            eq2[:].rearrange("p (t e) -> p t e", e=8),
            p2s[:].unsqueeze(2).broadcast_to([128, 16, 8]), OP.mult), waits=[tp2])
        trw = V(lambda v: v.tensor_add(rws[:], rw1[:], eq2[:]), waits=[tr1, tr2])

        if _stop == 3:
            return _finish({"rws": rws[:], "lgs": lgs[:], "acc": acc[:]})
        xm_free = [None, None]
        os_free = [None, None]

        def record_merge(t16):
            sl = t16 % 2
            g16, tl4 = t16 // 4, t16 % 4
            row0 = (2 * g16 + 1) * 512 + tl4 * 128
            tx = WL(lambda g_, sl=sl, row0=row0: g_.dma_start(
                out=xm_sb[:, sl * D:(sl + 1) * D], in_=xkv[row0:row0 + 128, :]),
                waits=[xm_free[sl]], key="g_xm%d" % sl)
            tme = V(lambda v, sl=sl, t16=t16: v.tensor_tensor(
                out_sb[:, sl * D:(sl + 1) * D], xm_sb[:, sl * D:(sl + 1) * D],
                acc[:, t16 * D:(t16 + 1) * D], OP.add),
                waits=[tx, acc_tok[t16][0], acc_tok[t16][1], os_free[sl]])
            xm_free[sl] = tme
            tst = LD(lambda s, sl=sl, t16=t16: s.dma_start(
                out=out_hbm[t16 * 128:(t16 + 1) * 128, :],
                in_=out_sb[:, sl * D:(sl + 1) * D]), waits=[tme],
                key="s_out%d" % sl)
            os_free[sl] = tst

        gA_f = [psA_free]
        gU_f = [psS_free[0]]
        dn_banks = [psB2, psZ, psV, psO]
        dn_f = [psS_free[1], psZ_free, psV_free, psO_free]
        sil_free = [None, None]
        hT_free = [None] * 4
        for e in range(10):
            sl = e % 2
            dn_last = None
            for g in range(NQG):
                qof = (2 * g + 1) * 512
                for fs in range(4):
                    fp = 0
                    psG = psA
                    psU = psB
                    gl = None
                    ul = None
                    for mc in range(NMC):
                        gl = T(lambda tt, sl=sl, mc=mc, fs=fs, qof=qof, psG=psG:
                               tt.matmul(
                            psG[:],
                            wg_sb[:, sl * 4096 + (mc * 4 + fs) * 128:
                                  sl * 4096 + (mc * 4 + fs) * 128 + 128],
                            xnT[:, mc * S + qof:mc * S + qof + 512],
                            start=(mc == 0), stop=(mc == NMC - 1)),
                            waits=[wff_ld[e][0], gA_f[fp], xn_all])
                    for mc in range(NMC):
                        ul = T(lambda tt, sl=sl, mc=mc, fs=fs, qof=qof, psU=psU:
                               tt.matmul(
                            psU[:],
                            wu_sb[:, sl * 4096 + (mc * 4 + fs) * 128:
                                  sl * 4096 + (mc * 4 + fs) * 128 + 128],
                            xnT[:, mc * S + qof:mc * S + qof + 512],
                            start=(mc == 0), stop=(mc == NMC - 1)),
                            waits=[wff_ld[e][1], gU_f[fp]])
                    ssl = fs % 2
                    tsil = C(lambda sc, ssl=ssl, psG=psG: sc.activation(
                        sil_sb[:, ssl * 512:(ssl + 1) * 512], psG[:], AF.Silu),
                        waits=[gl, sil_free[ssl]])
                    gA_f[fp] = tsil
                    th = V(lambda v, fs=fs, ssl=ssl, psU=psU: v.tensor_tensor(
                        hT[:, fs * 512:(fs + 1) * 512],
                        sil_sb[:, ssl * 512:(ssl + 1) * 512], psU[:], OP.mult),
                        waits=[tsil, ul, hT_free[fs]])
                    gU_f[fp] = th
                    sil_free[ssl] = th
                    hT_free[fs] = th
                for tl4 in range(4):
                    t16 = g * 4 + tl4
                    for mh in range(2):
                        dbi = (tl4 * 2 + mh) % 4
                        psX = dn_banks[dbi]
                        psX_f = dn_f[dbi]
                        last = None
                        for fc in range(4):
                            last = T(lambda tt, sl=sl, fc=fc, tl4=tl4, mh=mh,
                                     psX=psX: tt.matmul(
                                psX[:],
                                hT[:, fc * 512 + tl4 * 128:fc * 512 + tl4 * 128 + 128],
                                wd_sb[:, sl * 4096 + fc * 1024 + mh * 512:
                                      sl * 4096 + fc * 1024 + (mh + 1) * 512],
                                start=(fc == 0), stop=(fc == 3)),
                                waits=[wff_ld[e][2], psX_f, hT_free[fc]])
                        aof = t16 * 1024 + mh * 512
                        if e < 8:
                            tadd = V(lambda v, aof=aof, psX=psX, t16=t16, e=e:
                                     v.scalar_tensor_tensor(
                                acc[:, aof:aof + 512], psX[:],
                                rws[:, t16 * 8 + e:t16 * 8 + e + 1],
                                acc[:, aof:aof + 512], OP.mult, OP.add),
                                waits=[last, trw, acc_tok[t16][mh]])
                        else:
                            tadd = V(lambda v, aof=aof, psX=psX: v.tensor_tensor(
                                acc[:, aof:aof + 512], acc[:, aof:aof + 512],
                                psX[:], OP.add), waits=[last, acc_tok[t16][mh]])
                        acc_tok[t16][mh] = tadd
                        dn_last = tadd
                        dn_f[dbi] = tadd
                for fc in range(4):
                    hT_free[fc] = dn_last
                if e == 9:
                    for tl4m in range(4):
                        record_merge(g * 4 + tl4m)
            if e + 2 < 10:
                wff_free[sl] = dn_last
                load_ffn(e + 2)

        em.wait_only("sync", em.all_keys() + [(e, em.cnt[e]) for e in CENGS])

    sems = make_sems()
    with nc.Block() as block:
        emit(em, block, sems)
    return nc


_PROG = None


def _get_prog():
    global _PROG
    if _PROG is None:
        _PROG = build_program()
    return _PROG


def _host_layouts(inp):
    f = np.float32
    WQ = np.asarray(inp["L0_attn_W_Q"], f)
    WK = np.asarray(inp["L0_attn_W_K"], f)
    WV = np.asarray(inp["L0_attn_W_V"], f)
    WO = np.asarray(inp["L0_attn_W_O"], f)
    bQ = np.asarray(inp["L0_attn_b_Q"], f)
    bK = np.asarray(inp["L0_attn_b_K"], f)
    bV = np.asarray(inp["L0_attn_b_V"], f)
    bO = np.asarray(inp["L0_attn_b_O"], f)
    Wr = np.asarray(inp["L0_ffn_Wr"], f)
    Wg = np.asarray(inp["L0_ffn_Wg"], f)
    Wu = np.asarray(inp["L0_ffn_Wu"], f)
    Wd = np.asarray(inp["L0_ffn_Wd"], f)
    Sg = np.asarray(inp["L0_ffn_Sg"], f)
    Su = np.asarray(inp["L0_ffn_Su"], f)
    Sd = np.asarray(inp["L0_ffn_Sd"], f)
    eb = np.asarray(inp["L0_ffn_expert_bias"], f)

    d = {}
    d["wqT"] = np.ascontiguousarray(
        WQ.reshape(8, 2, 8, 128, 64).transpose(3, 0, 2, 1, 4).reshape(128, 8192))
    d["wkT"] = np.ascontiguousarray(
        WK.reshape(8, 2, 8, 128, 64).transpose(3, 0, 2, 1, 4).reshape(128, 8192))
    d["wv4"] = np.ascontiguousarray(
        WV.reshape(4, 4, 8, 128, 64).transpose(3, 0, 2, 1, 4).reshape(128, 8192))
    d["woT"] = np.ascontiguousarray(
        WO.reshape(4, 2, 2, 64, 1024).transpose(2, 3, 0, 1, 4).reshape(128, 8192))
    d["bqk"] = np.ascontiguousarray(
        np.stack([bQ.reshape(8, 2 * 64), bK.reshape(8, 2 * 64)], axis=2)
        .transpose(1, 0, 2).reshape(128, 16))
    d["bvr"] = bV.reshape(1, 1024).copy()
    d["bor"] = bO.reshape(1, 1024).copy()
    d["wrR"] = np.ascontiguousarray(
        Wr.reshape(8, 128, 8).transpose(1, 0, 2).reshape(128, 64))
    d["rbr"] = eb.reshape(1, 8).copy()

    def gate_layout(w):
        return np.ascontiguousarray(
            w.reshape(-1, 8, 128, 4, 128).transpose(0, 2, 1, 3, 4).reshape(-1, 128, 4096))

    def down_layout(w):
        return np.ascontiguousarray(
            w.reshape(-1, 4, 128, 1024).transpose(0, 2, 1, 3).reshape(-1, 128, 4096))

    g_sh = np.stack([Sg[:, 0:512], Sg[:, 512:1024]], 0)
    u_sh = np.stack([Su[:, 0:512], Su[:, 512:1024]], 0)
    dd_sh = np.stack([Sd[0:512, :], Sd[512:1024, :]], 0)
    d["wgE"] = np.concatenate([gate_layout(Wg), gate_layout(g_sh)], 0)
    d["wuE"] = np.concatenate([gate_layout(Wu), gate_layout(u_sh)], 0)
    d["wdE"] = np.concatenate([down_layout(Wd), down_layout(dd_sh)], 0)

    dg = np.zeros((128, 2048), np.float32)
    for c in range(4):
        for p in range(128):
            dg[p, c * 512 + c * 128 + p:(c + 1) * 512] = 1.0
    d["diag"] = dg
    d["ident"] = np.eye(128, dtype=np.float32)
    return d


_PERM_B = [1, 0, 3, 2, 5, 4, 7, 6]


def kernel(**inputs):
    x = np.asarray(inputs["L0_attn_x"], np.float32)
    shared = _host_layouts(inputs)
    nc = _get_prog()

    qi = np.zeros((4, OWN), np.float32)
    for g in range(4):
        qi[g, g * 512:(g + 1) * 512] = -1.0

    km_A = np.zeros((4, S), np.float32)
    km_B = np.zeros((4, S), np.float32)
    for g in range(4):
        km_B[g, 2 * g * 512:(2 * g + 1) * 512] = BIGMASK

    in_maps = []
    for core in range(8):
        b = core // 2
        role = core % 2
        if role == 0:
            xc = np.ascontiguousarray(x[b])
            km = km_A
        else:
            xc = np.ascontiguousarray(x[b].reshape(8, 512, D)[_PERM_B].reshape(S, D))
            km = km_B
        m = dict(shared)
        m["xkv"] = xc
        m["kmask"] = km
        m["qind"] = qi
        in_maps.append(m)

    res = run_bass_kernel_spmd(nc, in_maps, core_ids=list(range(8)))
    y = np.empty((B, S, D), np.float32)
    for core in range(8):
        b = core // 2
        role = core % 2
        o = np.asarray(res.results[core]["out"], np.float32)
        for g in range(4):
            blk = 2 * g + 1 if role == 0 else 2 * g
            y[b, blk * 512:(blk + 1) * 512, :] = o[g * 512:(g + 1) * 512, :]
    return y


# version 16
# speedup vs baseline: 1.0398x; 1.0091x over previous
import numpy as np
from contextlib import ExitStack

import concourse.bass as bass
import concourse.mybir as mybir
from concourse.bass_utils import run_bass_kernel_spmd

F32 = mybir.dt.float32
BF16 = mybir.dt.bfloat16
AX = mybir.AxisListType.X
OP = mybir.AluOpType
AF = mybir.ActivationFunctionType

B, S, D, H, DH = 4, 4096, 1024, 16, 64
NMC = D // 128
NKT = S // 128
NQG = 4
OWN = 2048
EPS = 1e-5
BIGMASK = 1.0e5

ENGS = ("sync", "gpsimd", "tensor", "vector", "scalar")
CENGS = ("tensor", "vector", "scalar")


class Em:

    def __init__(self):
        self.items = {e: [] for e in ENGS}
        self.cnt = {e: 0 for e in CENGS}
        self.kcnt = {}

    def op(self, eng, fn, waits=(), key=None):
        for w in waits:
            if w is not None:
                top = self.cnt.get(w[0], None)
                if top is None:
                    top = self.kcnt[w[0]]
                assert w[1] <= top, (eng, w)
        if eng in CENGS:
            assert key is None
            self.cnt[eng] += 1
            tok = (eng, self.cnt[eng])
            self.items[eng].append((fn, [w for w in waits if w is not None], 1, eng))
        else:
            assert key is not None
            self.kcnt[key] = self.kcnt.get(key, 0) + 16
            tok = (key, self.kcnt[key])
            self.items[eng].append((fn, [w for w in waits if w is not None], 16, key))
        return tok

    def wait_only(self, eng, waits):
        self.items[eng].append((None, [w for w in waits if w is not None], 0, None))

    def all_keys(self):
        return [(k, v) for k, v in self.kcnt.items()]


def emit(em, block, sems):
    def runner(ename):
        def run(engine):
            last = {}
            cnt = 0
            for fn, waits, inc, key in em.items[ename]:
                for (we, wv) in waits:
                    if we == ename and fn is not None:
                        if ename in ("vector", "scalar") and cnt - wv <= 1 \
                                and wv > last.get(we, 0):
                            engine.wait_ge(sems[we], wv)
                            last[we] = wv
                        continue
                    if wv > last.get(we, 0) or fn is None:
                        engine.wait_ge(sems[we], wv)
                        last[we] = max(last.get(we, 0), wv)
                if fn is not None:
                    ins = fn(engine)
                    ins.then_inc(sems[key], inc)
                    if key == ename:
                        cnt += inc
        return run

    block.sync(runner("sync"))
    block.gpsimd(runner("gpsimd"))
    block.tensor(runner("tensor"))
    block.vector(runner("vector"))
    block.scalar(runner("scalar"))


def build_program(stage="all", taps=()):
    nc = bass.Bass("TRN2", target_bir_lowering=False, debug=False)

    dt_in = lambda n, sh, dt=F32: nc.dram_tensor(n, sh, dt, kind="ExternalInput").ap()
    xkv = dt_in("xkv", [S, D])
    wqT = dt_in("wqT", [128, 8192])
    wkT = dt_in("wkT", [128, 8192])
    wv4 = dt_in("wv4", [128, 8192])
    woT = dt_in("woT", [128, 8192])
    bqk = dt_in("bqk", [128, 16])
    bvr = dt_in("bvr", [1, 1024])
    bor = dt_in("bor", [1, 1024])
    wrR = dt_in("wrR", [128, 64])
    rbr = dt_in("rbr", [1, 8])
    wgE = dt_in("wgE", [10, 128, 4096])
    wuE = dt_in("wuE", [10, 128, 4096])
    wdE = dt_in("wdE", [10, 128, 4096])
    kmask = dt_in("kmask", [4, S])
    qind = dt_in("qind", [4, OWN])
    diag = dt_in("diag", [128, 2048])
    ident = dt_in("ident", [128, 128])
    out_hbm = nc.dram_tensor("out", [OWN, D], F32, kind="ExternalOutput").ap()
    tap_hbm = {}
    for tname, tshape, tdt in taps:
        tap_hbm[tname] = nc.dram_tensor("tap_" + tname, tshape, tdt,
                                        kind="ExternalOutput").ap()

    em = Em()
    T = lambda fn, waits=(): em.op("tensor", fn, waits)
    V = lambda fn, waits=(): em.op("vector", fn, waits)
    C = lambda fn, waits=(): em.op("scalar", fn, waits)
    LD = lambda fn, waits=(), key=None: em.op("sync", fn, waits, key=key)
    WL = lambda fn, waits=(), key=None: em.op("gpsimd", fn, waits, key=key)

    ctx = ExitStack()
    sb = lambda n, sh, dt=BF16: ctx.enter_context(nc.sbuf_tensor(n, sh, dt))
    psd = lambda n, sh, dt=F32: ctx.enter_context(nc.psum_tensor(n, sh, dt))

    def make_sems():
        sems = {e: ctx.enter_context(nc.semaphore("sem_" + e)) for e in CENGS}
        for k in em.kcnt:
            sems[k] = ctx.enter_context(nc.semaphore("sem_" + k))
        return sems

    xnT = sb("xnT", [128, NMC * S])
    acc = sb("acc", [128, 16 * D])
    id_bf = sb("id_bf", [128, 128])
    id_f32 = sb("id_f32", [128, 128], F32)
    cones = sb("cones", [9, 256])
    bqk_sb = sb("bqk_sb", [128, 16], F32)
    bv_bf = sb("bv_bf", [9, 1088])
    bo_bf = sb("bo_bf", [9, 1088])
    rb_bf = sb("rb_bf", [9, 72])
    wr_bf = sb("wr_bf", [128, 128])
    epsc = sb("epsc", [128, 1], F32)

    psA = psd("psA", [128, 512])
    psB = psd("psB", [128, 512])
    psB2 = psd("psB2", [128, 512])
    psZ = psd("psZ", [128, 512])
    psV = psd("psV", [128, 512])
    psO = psd("psO", [128, 512])
    psT1 = psd("psT1", [128, 128])
    psT2 = psd("psT2", [128, 512], BF16)

    WL(lambda g: g.dma_start(out=id_bf[:], in_=ident), key="g_const")
    LD(lambda s: s.dma_start(out=id_f32[:], in_=ident), key="s_const")
    LD(lambda s: s.dma_start(out=bqk_sb[:], in_=bqk), key="s_const")
    WL(lambda g: g.dma_start(out=bv_bf[0:1, 0:1024], in_=bvr), key="g_const")
    WL(lambda g: g.dma_start(out=bo_bf[0:1, 0:1024], in_=bor), key="g_const")
    WL(lambda g: g.dma_start(out=rb_bf[0:1, 0:8], in_=rbr), key="g_const")
    WL(lambda g: g.dma_start(out=wr_bf[:, 0:64], in_=wrR), key="g_const")
    onesf = sb("onesf", [128, 1], F32)
    t_ones = V(lambda v: v.memset(cones[0:1, 0:256], 1.0))
    t_eps = V(lambda v: v.memset(epsc[:], EPS))
    t_onesf = V(lambda v: v.memset(onesf[:], 1.0))
    t_gconst = ("g_const", em.kcnt["g_const"])
    t_sconst = ("s_const", em.kcnt["s_const"])
    t_id = t_idf = t_bqk = t_bv = t_bo = t_rb = t_wr = None

    acc_tok = [[None, None] for _ in range(16)]

    with ExitStack() as actx:
        sba = lambda n, sh, dt=BF16: actx.enter_context(nc.sbuf_tensor(n, sh, dt))
        nmean = sba("nmean", [128, 2], F32)
        nmean1 = sba("nmean1", [128, 2], F32)
        varn = sba("varn", [128, 2], F32)
        sq_s = sba("sq_s", [128, 2], F32)
        rstd = sba("rstd", [128, 2], F32)
        diag_bf = sba("diag_bf", [128, 2048])
        wq_sb = sba("wq_sb", [128, 2 * 1024])
        wk_sb = sba("wk_sb", [128, 2 * 1024])
        wv_sb = sba("wv_sb", [128, 2048])
        wo_sb = sba("wo_sb", [128, 2112])
        V4 = sba("V4", [128, NKT * 260])
        KtP = [[sba("Kt%d%d" % (par, hh), [69, S]) for hh in range(2)]
               for par in range(2)]
        QtP = [[sba("Qt%d%d" % (par, hh), [69, OWN]) for hh in range(2)]
               for par in range(2)]
        pstage = sba("pstage", [128, 2 * 512])
        onescol = sba("onescol", [128, 8])
        kmx_c = sba("kmx_c", [128, 1], F32)
        kmax_sb = sba("kmax_sb", [9, 8])
        kb_sb = sba("kb_sb", [128, 2], F32)
        negc = sba("negc", [128, 32], F32)
        zguard = sba("zguard", [128, 1], F32)
        zt_sb = sba("zt_sb", [128, 64])
        rzc = sba("rzc", [128, 1], F32)
        pctx = ExitStack()
        sbp = lambda n, sh, dt=BF16: pctx.enter_context(nc.sbuf_tensor(n, sh, dt))
        xst = sbp("xst", [128, 2 * D], F32)
        lntr = sbp("lntr", [128, D], F32)
        xnt_st = sbp("xnt_st", [128, 2 * D])

        WL(lambda g: g.dma_start(out=diag_bf[:], in_=diag), key="g_mask")

        t_onescol = V(lambda v: v.memset(onescol[:, 0:8], 1.0))
        ktc = []
        for par in range(2):
            for hh in range(2):
                ktc.append(V(lambda v, par=par, hh=hh: v.memset(
                    KtP[par][hh][64:65, :], 1.0)))
                WL(lambda g, par=par, hh=hh: g.dma_start(
                    out=KtP[par][hh][65:69, :], in_=kmask), key="g_mask")
                WL(lambda g, par=par, hh=hh: g.dma_start(
                    out=QtP[par][hh][65:69, :], in_=qind), key="g_mask")
        t_gmask = ("g_mask", em.kcnt["g_mask"])
        qtc = [t_gmask]

        wq_free = [None, None]
        wk_free = [None, None]
        wv_free = [None, None]
        wo_free = [None, None]
        wq_ld = [None] * 8
        wk_ld = [None] * 8
        wv_ld = [None] * 4
        wo_ld = [None] * 4
        kq_free = [None, None]
        v4_free = None
        psS_free = [None, None]
        psA_free = None
        psZ_free = None
        psV_free = None
        psO_free = None
        psT1_free = None
        kmx_free = None
        kmaxsb_free = None
        kb_free = None
        negc_free = None
        zg_free = None
        zrot = 0
        pend_dance = None
        zdma_last = None
        mst_free = []
        ps2b_free = None
        pst_free = [None, None]
        probs_free = [None] * 4
        prep_tok = {}
        zev_tok = {}
        zdma_tok = {}

        zraw_free = None
        zt_free = None
        zall_free_all = None
        zst_free = [None, None]
        vdir_last = [None] * 4

        def load_wqk(p):
            slp = p % 2
            wq_ld[p] = WL(lambda g, p=p, slp=slp: g.dma_start(
                out=wq_sb[:, slp * 1024:(slp + 1) * 1024],
                in_=wqT[:, p * 1024:(p + 1) * 1024]), waits=[wq_free[slp]],
                key="g_wq%d" % slp)
            wk_ld[p] = WL(lambda g, p=p, slp=slp: g.dma_start(
                out=wk_sb[:, slp * 1024:(slp + 1) * 1024],
                in_=wkT[:, p * 1024:(p + 1) * 1024]), waits=[wk_free[slp]],
                key="g_wk%d" % slp)

        def load_wv(grp):
            wv_ld[grp] = WL(lambda g, grp=grp: g.dma_start(
                out=wv_sb[:],
                in_=wv4[:, grp * 2048:(grp + 1) * 2048]), waits=[wv_free[0]],
                key="g_wv")

        def load_wo(grp):
            wo_ld[grp] = WL(lambda g, grp=grp: g.dma_start(
                out=wo_sb[0:128, 0:2048],
                in_=woT[:, grp * 2048:(grp + 1) * 2048]), waits=[wo_free[0]],
                key="g_wo")

        load_wqk(0)
        load_wqk(1)
        load_wv(0)
        load_wo(0)

        vm0 = V(lambda v: v.memset(V4[:], 1.0))
        vdir_evac0 = None

        xst_free = [None, None]
        xnst_free = [None, None]
        ps2_free = None
        prev_sq = [None, None]
        xn_ready = [None] * NKT
        stats_tok = [None] * NKT
        vdir_evac0 = None

        def ln_tail(t):
            nonlocal ps2_free, psV_free, psO_free, vdir_evac0
            sl = t % 2
            x_sl = xst[:, sl * D:(sl + 1) * D]
            xn_sl = xnt_st[:, sl * D:(sl + 1) * D]
            t_sq, t_sqrt = stats_tok[t]
            t_rcp = V(lambda v, sl=sl: v.reciprocal(rstd[:, sl:sl + 1],
                                                    sq_s[:, sl:sl + 1]),
                      waits=[t_sqrt])
            t_app = V(lambda v, sl=sl, x_sl=x_sl, xn_sl=xn_sl: v.tensor_scalar(
                xn_sl, x_sl, nmean1[:, sl:sl + 1], rstd[:, sl:sl + 1],
                OP.add, OP.mult),
                waits=[xnst_free[sl], t_sq, t_rcp])
            xst_free[sl] = t_app
            evl = None
            for half in range(2):
                tps = []
                for b4 in range(4):
                    mc = half * 4 + b4
                    tps.append(T(lambda tt, mc=mc, b4=b4, xn_sl=xn_sl: tt.transpose(
                        psT2[:, b4 * 128:(b4 + 1) * 128],
                        xn_sl[:, mc * 128:(mc + 1) * 128], id_bf[:]),
                        waits=[t_app, t_gconst, ps2_free]))

                def ev(v, half=half, t=t):
                    o = xnT[:].rearrange("p (mc s) -> p mc s", s=S)[
                        :, half * 4:half * 4 + 4, t * 128:(t + 1) * 128]
                    i = psT2[:].rearrange("p (b s) -> p b s", s=128)
                    return v.tensor_copy(o, i)
                evl = V(ev, waits=tps)
                ps2_free = evl
            xnst_free[sl] = evl
            xn_ready[t] = evl
            kt = t
            psX = psV if kt % 2 == 0 else psO
            psX_f = psV_free if kt % 2 == 0 else psO_free
            for mc in range(NMC):
                T(lambda tt, kt=kt, mc=mc, psX=psX: tt.matmul(
                    psX[:, 0:256],
                    xnT[:, mc * S + kt * 128:mc * S + (kt + 1) * 128],
                    wv_sb[:, mc * 256:(mc + 1) * 256],
                    start=(mc == 0), stop=False),
                    waits=[xn_ready[kt], wv_ld[0], psX_f])
            trk = T(lambda tt, psX=psX: tt.matmul(
                psX[:, 0:256], cones[0:1, 0:128], bv_bf[0:1, 0:256],
                start=False, stop=True), waits=[t_gconst, t_ones])
            tev = V(lambda v, kt=kt, psX=psX: v.tensor_copy(
                V4[:, kt * 260:(kt + 1) * 260].rearrange(
                    "p (h d) -> p h d", d=65)[:, :, 0:64],
                psX[:, 0:256].rearrange("p (h d) -> p h d", d=64)),
                waits=[trk, vm0])
            if kt % 2 == 0:
                psV_free = tev
            else:
                psO_free = tev
            vdir_evac0 = tev

        prev_nm = [None, None]
        for t in range(NKT):
            sl = t % 2
            x_sl = xst[:, sl * D:(sl + 1) * D]
            t_load = LD(lambda s, t=t, x_sl=x_sl: s.dma_start(
                out=x_sl, in_=xkv[t * 128:(t + 1) * 128, :]), waits=[xst_free[sl]],
                key="s_x%d" % sl)
            t_red = C(lambda sc, sl=sl, x_sl=x_sl: sc.activation(
                lntr[:], x_sl, AF.Identity, accum_out=nmean[:, sl:sl + 1]),
                waits=[t_load, prev_nm[sl]])
            t_nm = V(lambda v, sl=sl: v.tensor_single_scalar(
                nmean1[:, sl:sl + 1], nmean[:, sl:sl + 1], -1.0 / D, OP.mult),
                waits=[prev_sq[sl], t_red])
            prev_nm[sl] = t_nm
            t_sq = C(lambda sc, sl=sl, x_sl=x_sl: sc.activation(
                lntr[:], x_sl, AF.Square, bias=nmean1[:, sl:sl + 1], scale=1.0,
                accum_out=varn[:, sl:sl + 1]), waits=[t_nm])
            prev_sq[sl] = t_sq
            t_sqrt = C(lambda sc, sl=sl: sc.activation(
                sq_s[:, sl:sl + 1], varn[:, sl:sl + 1], AF.Sqrt,
                bias=epsc[:, 0:1], scale=1.0 / D), waits=[t_eps, t_sq])
            stats_tok[t] = (t_sq, t_sqrt)
            if t >= 1:
                ln_tail(t - 1)
        ln_tail(NKT - 1)
        xn_all = xn_ready[NKT - 1]
        vdir_last[0] = vdir_evac0
        ps2b_free = ps2_free
        pctx.close()
        probs = sba("probs", [128, 4 * 512])
        mstage = sba("mstage", [32, 128], F32)
        zraw = sba("zraw", [65, 512], F32)
        zallT = [sba("zallT%d" % i, [128, OWN]) for i in range(2)]
        zst = sba("zst", [69, 2 * 512])
        sqstage = sba("sqstage", [128, 1024])
        t_sqz = V(lambda v: v.memset(sqstage[64:128, :], 0.0), waits=[xn_all])

        def build_prep(p):
            par = p % 2
            st = {}
            tasks = []

            def proj_mm_unit(kind, kc, u):
                def f():
                    last = None
                    for mc in (2 * u, 2 * u + 1):
                        if kind == "k":
                            lhs, rhs_col = wk_sb, kc * 512
                            xw, wld = xn_ready[kc * 4 + 3], wk_ld[p]
                        else:
                            lhs, rhs_col = wq_sb, (2 * kc + 1) * 512
                            xw, wld = xn_all, wq_ld[p]
                        last = T(lambda tt, mc=mc, lhs=lhs, rhs_col=rhs_col:
                                 tt.matmul(
                            psV[:],
                            lhs[:, par * 1024 + mc * 128:par * 1024 + (mc + 1) * 128],
                            xnT[:, mc * S + rhs_col:mc * S + rhs_col + 512],
                            start=(mc == 0), stop=(mc == NMC - 1)),
                            waits=[xw, wld, psV_free])
                    st[("mm", kind, kc)] = last
                return f

            def proj_evac(kind, kc, ci):
                def f():
                    nonlocal psV_free
                    sps = ci % 2
                    cc = 2 * p + (1 if kind == "k" else 0)
                    dA = KtP[par][0] if kind == "k" else QtP[par][0]
                    dB = KtP[par][1] if kind == "k" else QtP[par][1]
                    col = kc * 512
                    mm = st[("mm", kind, kc)]
                    te0 = V(lambda v: v.tensor_scalar(
                        dA[0:64, col:col + 512], psV[0:64, :],
                        bqk_sb[0:64, cc:cc + 1], onesf[0:64, 0:1],
                        OP.add, OP.mult),
                        waits=[mm, t_sconst, t_onesf, t_gmask, kq_free[par]]
                        + ktc)
                    te1 = V(lambda v: v.tensor_scalar(
                        pstage[64:128, sps * 512:(sps + 1) * 512], psV[64:128, :],
                        bqk_sb[64:128, cc:cc + 1], onesf[64:128, 0:1],
                        OP.add, OP.mult),
                        waits=[pst_free[sps]])
                    psV_free = te1
                    tsh = LD(lambda s: s.dma_start(
                        out=dB[0:64, col:col + 512],
                        in_=pstage[64:128, sps * 512:(sps + 1) * 512]),
                        waits=[te1, kq_free[par]], key="s_kq%d" % sps)
                    pst_free[sps] = tsh
                    st["evac_last"] = te1
                return f

            def t_bar(final):
                def f():
                    st["sync"] = [("s_kq0", em.kcnt.get("s_kq0", 0)),
                                  ("s_kq1", em.kcnt.get("s_kq1", 0))]
                    st["scal"] = st["evac_last"]
                    if final:
                        prep_tok[p]["kq_scal"] = st["evac_last"]
                        prep_tok[p]["kq_sync"] = list(st["sync"])
                        if p + 2 < 8:
                            wq_free[par] = st[("mm", "q", 3)]
                            wk_free[par] = st[("mm", "q", 3)]
                            load_wqk(p + 2)
                return f

            def t_square(src, soff):
                def f():
                    st["sq"] = V(lambda v: v.tensor_mul(
                        sqstage[0:64, 0:1024], src[0:64, soff:soff + 1024],
                        src[0:64, soff:soff + 1024]),
                        waits=[t_sqz, st.get("mm_last"), st["scal"]]
                        + st["sync"])
                return f

            def t_normmm(col0, u):
                def f():
                    for c8 in range(4 * u, 4 * u + 4):
                        st["mm_last"] = T(lambda tt, c8=c8: tt.matmul(
                            psA[:, col0 + c8:col0 + c8 + 1],
                            sqstage[0:128, c8 * 128:(c8 + 1) * 128],
                            onescol[0:128, 0:1], start=True, stop=True),
                            waits=[st["sq"], psA_free, t_onescol])
                return f

            def norm_job(kind, hh, half):
                if kind == "k":
                    src, col0 = KtP[par][hh], hh * 32 + half * 8
                else:
                    src, col0 = QtP[par][hh], 64 + hh * 16 + half * 8
                return [t_square(src, half * 1024),
                        t_normmm(col0, 0), t_normmm(col0, 1)]

            def t_kmax():
                nonlocal psT1_free, kmx_free
                for hh in range(2):
                    tr1 = V(lambda v, hh=hh: v.reduce_max(
                        kmx_c[:, 0:1], psA[:, hh * 32:(hh + 1) * 32], AX),
                        waits=[st["mm_last"], kmx_free, kmaxsb_free])
                    ttp = T(lambda tt: tt.transpose(
                        psT1[0:1, 0:128], kmx_c[:, 0:1], id_f32[:]),
                        waits=[tr1, psT1_free, t_sconst])
                    kmx_free = ttp
                    tr2 = V(lambda v, hh=hh: v.reduce_max(
                        kmax_sb[0:1, hh:hh + 1], psT1[0:1, 0:128], AX),
                        waits=[ttp])
                    psT1_free = tr2
                    st["km%d" % hh] = tr2

            def t_bcast():
                nonlocal psT1_free, psA_free, kmaxsb_free, kb_free
                tkb = T(lambda tt: tt.matmul(
                    psT1[0:128, 0:2], cones[0:1, 0:128], kmax_sb[0:1, 0:2],
                    start=True, stop=True),
                    waits=[st["km0"], st["km1"], psT1_free, t_ones])
                kmaxsb_free = tkb
                tkbe = V(lambda v: v.tensor_copy(kb_sb[:, 0:2], psT1[0:128, 0:2]),
                         waits=[tkb, kb_free])
                psT1_free = tkbe
                tpr = V(lambda v: v.tensor_tensor(
                    negc[:].rearrange("p (h q) -> p h q", q=16),
                    psA[:, 64:96].rearrange("p (h q) -> p h q", q=16),
                    kb_sb[:, 0:2].unsqueeze(2).broadcast_to([128, 2, 16]),
                    OP.mult),
                    waits=[tkbe, st["mm_last"], negc_free])
                kb_free = tpr
                psA_free = tpr
                tnc = C(lambda sc: sc.activation(negc[:], negc[:], AF.Sqrt),
                        waits=[tpr])
                st["negc"] = V(lambda v: v.tensor_single_scalar(
                    negc[:], negc[:], -1.0, OP.mult), waits=[tnc])

            def t_mrow():
                nonlocal psT1_free, negc_free, mst_free
                ttr = T(lambda tt: tt.transpose(
                    psT1[0:32, 0:128], negc[:, 0:32], id_f32[:]),
                    waits=[st["negc"], psT1_free, t_sconst])
                negc_free = ttr
                tcp = V(lambda v: v.tensor_copy(mstage[0:32, 0:128],
                                                psT1[0:32, 0:128]),
                        waits=[ttr] + mst_free)
                psT1_free = tcp
                tdmd = {}
                for g in range(NQG):
                    for hh in range(2):
                        QTh = QtP[par][hh]
                        tdmd[(g, hh)] = WL(lambda g_, g=g, hh=hh, QTh=QTh:
                                           g_.dma_start(
                            out=QTh[64:65, g * 512:(g + 1) * 512],
                            in_=mstage[hh * 16 + g * 4:hh * 16 + g * 4 + 4, 0:128]),
                            waits=[tcp, kq_free[par]],
                            key="g_mrow%d" % (g * 2 + hh))
                mst_free = list(tdmd.values())
                prep_tok[p]["tdm"] = tdmd

            def chunk(kind, kc, ci):
                return [proj_mm_unit(kind, kc, u) for u in range(4)] \
                    + [proj_evac(kind, kc, ci)]

            def zipped(projs, norms):
                out, pi, ni = [], 0, 0
                while pi < len(projs) or ni < len(norms):
                    if pi < len(projs):
                        out.append(projs[pi]); pi += 1
                    if ni < len(norms):
                        out.append(norms[ni]); ni += 1
                return out

            pchunks = [chunk("k", kc, kc) for kc in range(8)] \
                + [chunk("q", g, 8 + g) for g in range(NQG)]
            tasks += pchunks[0] + pchunks[1] + [t_bar(False)]
            for half in range(4):
                nj = norm_job("k", 0, half) + norm_job("k", 1, half)
                tasks += zipped(pchunks[2 * half + 2] + pchunks[2 * half + 3], nj)
                tasks.append(t_bar(half == 3 and False))
            nj = norm_job("q", 0, 0) + norm_job("q", 1, 0)
            tasks += zipped(pchunks[10] + pchunks[11], nj)
            tasks.append(t_bar(True))
            tasks += norm_job("q", 0, 1) + norm_job("q", 1, 1)
            tasks.append(t_kmax)
            tasks.append(t_bcast)
            tasks.append(t_mrow)
            prep_tok[p] = {}
            return tasks

        for _f in build_prep(0):
            _f()
        _stop = {"p0": 0, "kv0": 0.2, "pa0": 0.4, "pb0": 0.6, "pb1": 0.7,
                 "pr1": 0.8, "op0": 0.9, "opnb": 0.95,
                 "attn0": 1, "attn": 2, "router": 3, "all": 9}[stage]

        def _finish(local_tensors):
            cw = [(e, em.cnt[e]) for e in CENGS]
            for tname, tshape, tdt in taps:
                if tname in local_tensors:
                    tsrc = local_tensors[tname]
                    LD(lambda s_, tsrc=tsrc, tname=tname: s_.dma_start(
                        out=tap_hbm[tname], in_=tsrc), waits=cw, key="s_tap")
            em.wait_only("sync", em.all_keys() + [(e, em.cnt[e]) for e in CENGS])
            sems = make_sems()
            with nc.Block() as block:
                emit(em, block, sems)
            return nc

        if _stop == 0:
            return _finish({"xnT": xnT[:], "nmean": nmean[:], "nmean1": nmean1[:],
                            "varn": varn[:], "rstd": rstd[:], "xst": xst[:]})


        for grp in range(4):
            if grp == 0:
                v4_all = vdir_last[0]
            for kt in ([] if grp == 0 else range(NKT)):
                psX = psV if kt % 2 == 0 else psO
                psX_f = psV_free if kt % 2 == 0 else psO_free
                last = None
                for mc in range(NMC):
                    last = T(lambda tt, kt=kt, mc=mc, psX=psX: tt.matmul(
                        psX[:, 0:256],
                        xnT[:, mc * S + kt * 128:mc * S + (kt + 1) * 128],
                        wv_sb[:, mc * 256:(mc + 1) * 256],
                        start=(mc == 0), stop=False),
                        waits=[xn_ready[kt], wv_ld[grp], psX_f])
                trk = T(lambda tt, grp=grp, psX=psX: tt.matmul(
                    psX[:, 0:256], cones[0:1, 0:128], bv_bf[0:1, grp * 256:(grp + 1) * 256],
                    start=False, stop=True), waits=[t_gconst, t_ones])
                tev = V(lambda v, kt=kt, psX=psX: v.tensor_copy(
                    V4[:, kt * 260:(kt + 1) * 260].rearrange(
                        "p (h d) -> p h d", d=65)[:, :, 0:64],
                    psX[:, 0:256].rearrange("p (h d) -> p h d", d=64)),
                    waits=[trk, vm0, v4_free])
                if kt % 2 == 0:
                    psV_free = tev
                else:
                    psO_free = tev
                vdir_last[grp] = tev
            v4_all = vdir_last[grp]

            grp_av_last = None
            for pr in (2 * grp, 2 * grp + 1):
                par = pr % 2
                ptk = prep_tok[pr]
                kq_scal = ptk["kq_scal"]
                kq_sync = ptk["kq_sync"]
                tdm_tok = ptk["tdm"]
                tasks = build_prep(pr + 1) if pr + 1 < 8 else []
                n_tasks = len(tasks)
                t_next = 0
                itc = 0

                if _stop == 0.2 and pr == 0:
                    return _finish({"Kt0": KtP[0][0][:], "Kt1": KtP[0][1][:],
                                    "Qt0": QtP[0][0][:], "Qt1": QtP[0][1][:],
                                    "V4": V4[:]})

                if pr % 2 == 1 and grp + 1 < 4:
                    wv_free[0] = vdir_last[grp]
                    load_wv(grp + 1)

                def dance_step(pv, k):
                    nonlocal psT1_free, ps2b_free, zt_free, zraw_free, zg_free
                    nonlocal zdma_last
                    qt, phase = k // 2, k % 2
                    if phase == 0:
                        tt1 = T(lambda tt, qt=qt, pv=pv: tt.transpose(
                            psT1[0:128, 0:65],
                            zraw[0:65, qt * 128:(qt + 1) * 128],
                            id_f32[0:65, 0:65]),
                            waits=[pv["tze"], psT1_free, t_sconst])
                        zraw_free = tt1
                        tg = V(lambda v: v.tensor_single_scalar(
                            zguard[:, 0:1], psT1[:, 64:65], 1e-30, OP.max),
                            waits=[tt1, zg_free])
                        trc = V(lambda v: v.reciprocal(rzc[:], zguard[:, 0:1]),
                                waits=[tg])
                        zg_free = trc
                        tsc = V(lambda v: v.tensor_single_scalar(
                            zt_sb[:], psT1[:, 0:64], rzc[:, 0:1], OP.mult),
                            waits=[trc, zt_free])
                        psT1_free = tsc
                        pv["tsc"] = tsc
                    else:
                        tt2 = T(lambda tt, pv=pv: tt.transpose(
                            psT2[0:64, 0:128], zt_sb[:], id_bf[:]),
                            waits=[pv["tsc"], ps2b_free, t_gconst])
                        zt_free = tt2
                        g, hh, zp, zsl = pv["g"], pv["hh"], pv["zp"], pv["zsl"]
                        if hh == 0:
                            tev = V(lambda v, g=g, qt=qt, zp=zp: v.tensor_copy(
                                zp[0:64, g * 512 + qt * 128:g * 512 + (qt + 1) * 128],
                                psT2[0:64, 0:128]),
                                waits=[tt2, zall_free_all])
                            if qt == 3:
                                zev_tok[(pv["pl"], g)] = tev
                        else:
                            tev = V(lambda v, qt=qt, zsl=zsl: v.tensor_copy(
                                zst[0:64, zsl + qt * 128:zsl + (qt + 1) * 128],
                                psT2[0:64, 0:128]),
                                waits=[tt2, zst_free[g % 2]])
                        ps2b_free = tev
                        if qt == 3 and hh == 1:
                            tzd = LD(lambda s_, g=g, zp=zp, zsl=zsl: s_.dma_start(
                                out=zp[64:128, g * 512:(g + 1) * 512],
                                in_=zst[0:64, zsl:zsl + 512]),
                                waits=[tev, zall_free_all],
                                key="s_zst%d" % (g % 2))
                            zst_free[g % 2] = tzd
                            zdma_tok[(pv["pl"], g)] = tzd

                prev = pend_dance
                for g in range(NQG):
                    nk = 8 * (g + 1)
                    for hh in range(2):
                        KT = KtP[par][hh]
                        QT = QtP[par][hh]
                        h4 = 2 * (pr % 2) + hh
                        zi = zrot % 2
                        zrot += 1
                        psZX = psZ if zi == 0 else psO
                        psZX_f = psZ_free if zi == 0 else psO_free
                        av_last = None
                        pend_av = None
                        for kc in range(nk + 1):
                            if kc < nk:
                                spb = kc % 2
                                psSX = (psB, psB2)[spb]
                                slot = kc % 4
                                tmm = T(lambda tt, kc=kc, psSX=psSX, KT=KT, QT=QT,
                                        g=g: tt.matmul(
                                    psSX[:], KT[0:69, kc * 128:(kc + 1) * 128],
                                    QT[0:69, g * 512:(g + 1) * 512],
                                    start=True, stop=True),
                                    waits=[tdm_tok[(g, hh)], psS_free[spb],
                                           kq_scal] + kq_sync)
                                texp = C(lambda sc, slot=slot, psSX=psSX:
                                         sc.activation(
                                    probs[:, slot * 512:(slot + 1) * 512], psSX[:],
                                    AF.Exp, bias=0.0, scale=0.125),
                                    waits=[tmm, probs_free[slot]])
                                psS_free[spb] = texp
                                prb = texp
                                dg = kc - (8 * g + 4)
                                if 0 <= dg < 4:
                                    prb = V(lambda v, slot=slot, dg=dg:
                                            v.tensor_tensor(
                                        probs[:, slot * 512:(slot + 1) * 512],
                                        probs[:, slot * 512:(slot + 1) * 512],
                                        diag_bf[:, dg * 512:(dg + 1) * 512],
                                        OP.mult), waits=[prb, t_gmask])
                            if pend_av is not None:
                                kcp, slotp, prbp = pend_av
                                tav = T(lambda tt, kc=kcp, slot=slotp, h4=h4,
                                        nk=nk, psZX=psZX: tt.matmul(
                                    psZX[0:65, :],
                                    V4[:, kc * 260 + h4 * 65:kc * 260 + h4 * 65 + 65],
                                    probs[:, slot * 512:(slot + 1) * 512],
                                    start=(kc == 0), stop=(kc == nk - 1)),
                                    waits=[prbp, v4_all, psZX_f])
                                probs_free[slotp] = tav
                                av_last = tav
                            if kc < nk:
                                pend_av = (kc, slot, prb)
                            if prev is not None and kc < 8:
                                dance_step(prev, kc)
                            if kc < nk:
                                itc += 1
                                tgt = n_tasks * itc // 160
                                while t_next < tgt:
                                    tasks[t_next]()
                                    t_next += 1
                        grp_av_last = av_last
                        tze = C(lambda sc, psZX=psZX: sc.activation(
                            zraw[0:65, :], psZX[0:65, :], AF.Identity),
                            waits=[av_last, zraw_free])
                        if zi == 0:
                            psZ_free = tze
                        else:
                            psO_free = tze
                        prev = {"tze": tze, "g": g, "hh": hh, "zp": zallT[pr % 2],
                                "zsl": (g % 2) * 512, "pl": pr % 2}
                while t_next < n_tasks:
                    tasks[t_next]()
                    t_next += 1
                kq_free[par] = grp_av_last
                pend_dance = prev
                if pr % 2 == 1:
                    for k in range(8):
                        dance_step(pend_dance, k)
                    pend_dance = None


            op_last = None
            _nqt = 1 if _stop == 0.9 else 16
            for qt16 in range(_nqt):
                gq = qt16 // 4
                zw = [zev_tok.get((0, gq)), zev_tok.get((1, gq)),
                      zdma_tok.get((0, gq)), zdma_tok.get((1, gq))]
                for mh in range(2):
                    psX = psV if mh == 0 else psO
                    ps_free = psV_free if mh == 0 else psO_free
                    last = None
                    for pl in range(2):
                        last = T(lambda tt, psX=psX, qt16=qt16,
                                 mh=mh, pl=pl, grp=grp: tt.matmul(
                            psX[:],
                            zallT[pl][0:128, qt16 * 128:(qt16 + 1) * 128],
                            wo_sb[0:128,
                                  pl * 1024 + mh * 512:
                                  pl * 1024 + (mh + 1) * 512],
                            start=(pl == 0),
                            stop=(pl == 1 and (grp != 0 or _stop == 0.95))),
                            waits=[wo_ld[grp], ps_free] + zw)
                    if grp == 0 and _stop != 0.95:
                        last = T(lambda tt, mh=mh, psX=psX: tt.matmul(
                            psX[:], cones[0:1, 0:128], bo_bf[0:1, mh * 512:(mh + 1) * 512],
                            start=False, stop=True), waits=[t_gconst, t_ones])
                    aof = qt16 * 1024 + mh * 512
                    if grp == 0:
                        tev = V(lambda v, aof=aof, psX=psX: v.tensor_copy(
                            acc[:, aof:aof + 512], psX[:]), waits=[last])
                    else:
                        tev = V(lambda v, aof=aof, psX=psX: v.tensor_tensor(
                            acc[:, aof:aof + 512], acc[:, aof:aof + 512], psX[:],
                            OP.add), waits=[last, acc_tok[qt16][mh]])
                    acc_tok[qt16][mh] = tev
                    op_last = tev
                    if mh == 0:
                        psV_free = tev
                    else:
                        psO_free = tev
            zall_free_all = op_last
            v4_free = grp_av_last
            if _stop in (0.9, 0.95, 1):
                break
            if grp + 1 < 4:
                wo_free[0] = op_last
                load_wo(grp + 1)

        attn_fin = op_last

    with ExitStack() as fctx:
        sbf = lambda n, sh, dt=BF16: fctx.enter_context(nc.sbuf_tensor(n, sh, dt))
        wg_sb = sbf("wg_sb", [128, 2 * 4096])
        wu_sb = sbf("wu_sb", [128, 2 * 4096])
        wd_sb = sbf("wd_sb", [128, 2 * 4096])
        sil_sb = sbf("sil_sb", [128, 2 * 512])
        hT = sbf("hT", [128, 4 * 512])
        lgs = sbf("lgs", [128, 128], F32)
        v1s = sbf("v1s", [128, 16], F32)
        v2s = sbf("v2s", [128, 16], F32)
        d12 = sbf("d12", [128, 16], F32)
        p1s = sbf("p1s", [128, 16], F32)
        p2s = sbf("p2s", [128, 16], F32)
        eq1 = sbf("eq1", [128, 128], F32)
        eq2 = sbf("eq2", [128, 128], F32)
        rw1 = sbf("rw1", [128, 128], F32)
        rws = sbf("rws", [128, 128], F32)
        cbig = sbf("cbig", [128, 1], F32)
        xm_sb = sbf("xm_sb", [128, 2 * D], F32)
        out_sb = sbf("out_sb", [128, 2 * D], F32)

        wff_ld = [None] * 10
        wff_free = [attn_fin, attn_fin]

        def load_ffn(e):
            sl = e % 2
            t1 = WL(lambda g, e=e, sl=sl: g.dma_start(
                out=wg_sb[:, sl * 4096:(sl + 1) * 4096], in_=wgE[e]),
                waits=[wff_free[sl]], key="g_wg%d" % sl)
            t2 = WL(lambda g, e=e, sl=sl: g.dma_start(
                out=wu_sb[:, sl * 4096:(sl + 1) * 4096], in_=wuE[e]),
                waits=[wff_free[sl]], key="g_wu%d" % sl)
            t3 = WL(lambda g, e=e, sl=sl: g.dma_start(
                out=wd_sb[:, sl * 4096:(sl + 1) * 4096], in_=wdE[e]),
                waits=[wff_free[sl]], key="g_wd%d" % sl)
            wff_ld[e] = (t1, t2, t3)

        load_ffn(0)
        load_ffn(1)

        tmc = V(lambda v: v.memset(cbig[:], -1.0e9))
        psV_f = psV_free
        lg_last = None
        for t16 in range(16):
            g16, tl = t16 // 4, t16 % 4
            qof = (2 * g16 + 1) * 512 + tl * 128
            last = None
            for mc in range(NMC):
                last = T(lambda tt, qof=qof, mc=mc: tt.matmul(
                    psV[:, 0:8],
                    xnT[:, mc * S + qof:mc * S + qof + 128],
                    wr_bf[:, mc * 8:(mc + 1) * 8],
                    start=(mc == 0), stop=False), waits=[t_gconst, psV_f])
            last = T(lambda tt: tt.matmul(
                psV[:, 0:8], cones[0:1, 0:128], rb_bf[0:1, 0:8],
                start=False, stop=True), waits=[t_gconst, t_ones])
            lg_last = V(lambda v, t16=t16: v.tensor_copy(
                lgs[:, t16 * 8:(t16 + 1) * 8], psV[:, 0:8]), waits=[last])
            psV_f = lg_last
        lg3 = lgs[:].rearrange("p (t e) -> p t e", e=8)
        tv1 = V(lambda v: v.tensor_reduce(v1s[:].unsqueeze(2), lg3, AX, OP.max),
                waits=[lg_last])
        teq1 = V(lambda v: v.tensor_tensor(
            eq1[:].rearrange("p (t e) -> p t e", e=8), lg3,
            v1s[:].unsqueeze(2).broadcast_to([128, 16, 8]), OP.is_equal),
            waits=[tv1])
        tl2 = V(lambda v: v.scalar_tensor_tensor(
            rw1[:], eq1[:], cbig[:, 0:1], lgs[:], OP.mult, OP.add),
            waits=[tmc, teq1])
        tv2 = V(lambda v: v.tensor_reduce(
            v2s[:].unsqueeze(2), rw1[:].rearrange("p (t e) -> p t e", e=8), AX,
            OP.max), waits=[tl2])
        teq2 = V(lambda v: v.tensor_tensor(
            eq2[:].rearrange("p (t e) -> p t e", e=8), lg3,
            v2s[:].unsqueeze(2).broadcast_to([128, 16, 8]), OP.is_equal),
            waits=[tv2])
        td = V(lambda v: v.tensor_sub(d12[:], v1s[:], v2s[:]), waits=[tv2])
        tp1 = C(lambda sc: sc.activation(p1s[:], d12[:], AF.Sigmoid), waits=[td])
        tp2 = C(lambda sc: sc.activation(p2s[:], d12[:], AF.Sigmoid, scale=-1.0),
                waits=[td])
        tr1 = V(lambda v: v.tensor_tensor(
            rw1[:].rearrange("p (t e) -> p t e", e=8),
            eq1[:].rearrange("p (t e) -> p t e", e=8),
            p1s[:].unsqueeze(2).broadcast_to([128, 16, 8]), OP.mult), waits=[tp1])
        tr2 = V(lambda v: v.tensor_tensor(
            eq2[:].rearrange("p (t e) -> p t e", e=8),
            eq2[:].rearrange("p (t e) -> p t e", e=8),
            p2s[:].unsqueeze(2).broadcast_to([128, 16, 8]), OP.mult), waits=[tp2])
        trw = V(lambda v: v.tensor_add(rws[:], rw1[:], eq2[:]), waits=[tr1, tr2])

        if _stop == 3:
            return _finish({"rws": rws[:], "lgs": lgs[:], "acc": acc[:]})
        xm_free = [None, None]
        os_free = [None, None]

        def record_merge(t16):
            sl = t16 % 2
            g16, tl4 = t16 // 4, t16 % 4
            row0 = (2 * g16 + 1) * 512 + tl4 * 128
            tx = WL(lambda g_, sl=sl, row0=row0: g_.dma_start(
                out=xm_sb[:, sl * D:(sl + 1) * D], in_=xkv[row0:row0 + 128, :]),
                waits=[xm_free[sl]], key="g_xm%d" % sl)
            tme = V(lambda v, sl=sl, t16=t16: v.tensor_tensor(
                out_sb[:, sl * D:(sl + 1) * D], xm_sb[:, sl * D:(sl + 1) * D],
                acc[:, t16 * D:(t16 + 1) * D], OP.add),
                waits=[tx, acc_tok[t16][0], acc_tok[t16][1], os_free[sl]])
            xm_free[sl] = tme
            tst = LD(lambda s, sl=sl, t16=t16: s.dma_start(
                out=out_hbm[t16 * 128:(t16 + 1) * 128, :],
                in_=out_sb[:, sl * D:(sl + 1) * D]), waits=[tme],
                key="s_out%d" % sl)
            os_free[sl] = tst

        gA_f = [psA_free]
        gU_f = [psS_free[0]]
        dn_banks = [psB2, psZ, psV, psO]
        dn_f = [psS_free[1], psZ_free, psV_free, psO_free]
        sil_free = [None, None]
        hT_free = [None] * 4
        for e in range(10):
            sl = e % 2
            dn_last = None
            for g in range(NQG):
                qof = (2 * g + 1) * 512
                for fs in range(4):
                    fp = 0
                    psG = psA
                    psU = psB
                    gl = None
                    ul = None
                    for mc in range(NMC):
                        gl = T(lambda tt, sl=sl, mc=mc, fs=fs, qof=qof, psG=psG:
                               tt.matmul(
                            psG[:],
                            wg_sb[:, sl * 4096 + (mc * 4 + fs) * 128:
                                  sl * 4096 + (mc * 4 + fs) * 128 + 128],
                            xnT[:, mc * S + qof:mc * S + qof + 512],
                            start=(mc == 0), stop=(mc == NMC - 1)),
                            waits=[wff_ld[e][0], gA_f[fp], xn_all])
                    for mc in range(NMC):
                        ul = T(lambda tt, sl=sl, mc=mc, fs=fs, qof=qof, psU=psU:
                               tt.matmul(
                            psU[:],
                            wu_sb[:, sl * 4096 + (mc * 4 + fs) * 128:
                                  sl * 4096 + (mc * 4 + fs) * 128 + 128],
                            xnT[:, mc * S + qof:mc * S + qof + 512],
                            start=(mc == 0), stop=(mc == NMC - 1)),
                            waits=[wff_ld[e][1], gU_f[fp]])
                    ssl = fs % 2
                    tsil = C(lambda sc, ssl=ssl, psG=psG: sc.activation(
                        sil_sb[:, ssl * 512:(ssl + 1) * 512], psG[:], AF.Silu),
                        waits=[gl, sil_free[ssl]])
                    gA_f[fp] = tsil
                    th = V(lambda v, fs=fs, ssl=ssl, psU=psU: v.tensor_tensor(
                        hT[:, fs * 512:(fs + 1) * 512],
                        sil_sb[:, ssl * 512:(ssl + 1) * 512], psU[:], OP.mult),
                        waits=[tsil, ul, hT_free[fs]])
                    gU_f[fp] = th
                    sil_free[ssl] = th
                    hT_free[fs] = th
                for tl4 in range(4):
                    t16 = g * 4 + tl4
                    for mh in range(2):
                        dbi = (tl4 * 2 + mh) % 4
                        psX = dn_banks[dbi]
                        psX_f = dn_f[dbi]
                        last = None
                        for fc in range(4):
                            last = T(lambda tt, sl=sl, fc=fc, tl4=tl4, mh=mh,
                                     psX=psX: tt.matmul(
                                psX[:],
                                hT[:, fc * 512 + tl4 * 128:fc * 512 + tl4 * 128 + 128],
                                wd_sb[:, sl * 4096 + fc * 1024 + mh * 512:
                                      sl * 4096 + fc * 1024 + (mh + 1) * 512],
                                start=(fc == 0), stop=(fc == 3)),
                                waits=[wff_ld[e][2], psX_f, hT_free[fc]])
                        aof = t16 * 1024 + mh * 512
                        if e < 8:
                            tadd = V(lambda v, aof=aof, psX=psX, t16=t16, e=e:
                                     v.scalar_tensor_tensor(
                                acc[:, aof:aof + 512], psX[:],
                                rws[:, t16 * 8 + e:t16 * 8 + e + 1],
                                acc[:, aof:aof + 512], OP.mult, OP.add),
                                waits=[last, trw, acc_tok[t16][mh]])
                        else:
                            tadd = V(lambda v, aof=aof, psX=psX: v.tensor_tensor(
                                acc[:, aof:aof + 512], acc[:, aof:aof + 512],
                                psX[:], OP.add), waits=[last, acc_tok[t16][mh]])
                        acc_tok[t16][mh] = tadd
                        dn_last = tadd
                        dn_f[dbi] = tadd
                for fc in range(4):
                    hT_free[fc] = dn_last
                if e == 9:
                    for tl4m in range(4):
                        record_merge(g * 4 + tl4m)
            if e + 2 < 10:
                wff_free[sl] = dn_last
                load_ffn(e + 2)

        em.wait_only("sync", em.all_keys() + [(e, em.cnt[e]) for e in CENGS])

    sems = make_sems()
    with nc.Block() as block:
        emit(em, block, sems)
    return nc


_PROG = None


def _get_prog():
    global _PROG
    if _PROG is None:
        _PROG = build_program()
    return _PROG


def _host_layouts(inp):
    f = np.float32
    WQ = np.asarray(inp["L0_attn_W_Q"], f)
    WK = np.asarray(inp["L0_attn_W_K"], f)
    WV = np.asarray(inp["L0_attn_W_V"], f)
    WO = np.asarray(inp["L0_attn_W_O"], f)
    bQ = np.asarray(inp["L0_attn_b_Q"], f)
    bK = np.asarray(inp["L0_attn_b_K"], f)
    bV = np.asarray(inp["L0_attn_b_V"], f)
    bO = np.asarray(inp["L0_attn_b_O"], f)
    Wr = np.asarray(inp["L0_ffn_Wr"], f)
    Wg = np.asarray(inp["L0_ffn_Wg"], f)
    Wu = np.asarray(inp["L0_ffn_Wu"], f)
    Wd = np.asarray(inp["L0_ffn_Wd"], f)
    Sg = np.asarray(inp["L0_ffn_Sg"], f)
    Su = np.asarray(inp["L0_ffn_Su"], f)
    Sd = np.asarray(inp["L0_ffn_Sd"], f)
    eb = np.asarray(inp["L0_ffn_expert_bias"], f)

    d = {}
    d["wqT"] = np.ascontiguousarray(
        WQ.reshape(8, 2, 8, 128, 64).transpose(3, 0, 2, 1, 4).reshape(128, 8192))
    d["wkT"] = np.ascontiguousarray(
        WK.reshape(8, 2, 8, 128, 64).transpose(3, 0, 2, 1, 4).reshape(128, 8192))
    d["wv4"] = np.ascontiguousarray(
        WV.reshape(4, 4, 8, 128, 64).transpose(3, 0, 2, 1, 4).reshape(128, 8192))
    d["woT"] = np.ascontiguousarray(
        WO.reshape(4, 2, 2, 64, 1024).transpose(2, 3, 0, 1, 4).reshape(128, 8192))
    d["bqk"] = np.ascontiguousarray(
        np.stack([bQ.reshape(8, 2 * 64), bK.reshape(8, 2 * 64)], axis=2)
        .transpose(1, 0, 2).reshape(128, 16))
    d["bvr"] = bV.reshape(1, 1024).copy()
    d["bor"] = bO.reshape(1, 1024).copy()
    d["wrR"] = np.ascontiguousarray(
        Wr.reshape(8, 128, 8).transpose(1, 0, 2).reshape(128, 64))
    d["rbr"] = eb.reshape(1, 8).copy()

    def gate_layout(w):
        return np.ascontiguousarray(
            w.reshape(-1, 8, 128, 4, 128).transpose(0, 2, 1, 3, 4).reshape(-1, 128, 4096))

    def down_layout(w):
        return np.ascontiguousarray(
            w.reshape(-1, 4, 128, 1024).transpose(0, 2, 1, 3).reshape(-1, 128, 4096))

    g_sh = np.stack([Sg[:, 0:512], Sg[:, 512:1024]], 0)
    u_sh = np.stack([Su[:, 0:512], Su[:, 512:1024]], 0)
    dd_sh = np.stack([Sd[0:512, :], Sd[512:1024, :]], 0)
    d["wgE"] = np.concatenate([gate_layout(Wg), gate_layout(g_sh)], 0)
    d["wuE"] = np.concatenate([gate_layout(Wu), gate_layout(u_sh)], 0)
    d["wdE"] = np.concatenate([down_layout(Wd), down_layout(dd_sh)], 0)

    dg = np.zeros((128, 2048), np.float32)
    for c in range(4):
        for p in range(128):
            dg[p, c * 512 + c * 128 + p:(c + 1) * 512] = 1.0
    d["diag"] = dg
    d["ident"] = np.eye(128, dtype=np.float32)
    return d


_PERM_B = [1, 0, 3, 2, 5, 4, 7, 6]


def kernel(**inputs):
    x = np.asarray(inputs["L0_attn_x"], np.float32)
    shared = _host_layouts(inputs)
    nc = _get_prog()

    qi = np.zeros((4, OWN), np.float32)
    for g in range(4):
        qi[g, g * 512:(g + 1) * 512] = -1.0

    km_A = np.zeros((4, S), np.float32)
    km_B = np.zeros((4, S), np.float32)
    for g in range(4):
        km_B[g, 2 * g * 512:(2 * g + 1) * 512] = BIGMASK

    in_maps = []
    for core in range(8):
        b = core // 2
        role = core % 2
        if role == 0:
            xc = np.ascontiguousarray(x[b])
            km = km_A
        else:
            xc = np.ascontiguousarray(x[b].reshape(8, 512, D)[_PERM_B].reshape(S, D))
            km = km_B
        m = dict(shared)
        m["xkv"] = xc
        m["kmask"] = km
        m["qind"] = qi
        in_maps.append(m)

    res = run_bass_kernel_spmd(nc, in_maps, core_ids=list(range(8)))
    y = np.empty((B, S, D), np.float32)
    for core in range(8):
        b = core // 2
        role = core % 2
        o = np.asarray(res.results[core]["out"], np.float32)
        for g in range(4):
            blk = 2 * g + 1 if role == 0 else 2 * g
            y[b, blk * 512:(blk + 1) * 512, :] = o[g * 512:(g + 1) * 512, :]
    return y
